# version 01
# baseline (speedup 1.0000x reference)
import numpy as np
import ml_dtypes
import concourse.bass as bass
import concourse.mybir as mybir
from concourse.bass_utils import run_bass_kernel_spmd

F32 = mybir.dt.float32
BF16 = mybir.dt.bfloat16
AF = mybir.ActivationFunctionType
ALU = mybir.AluOpType
AX = mybir.AxisListType

NCORES = 8
T = 2048
D = 1024
NDT = D // 128
DEBUG_CKPT = False
ARENA_F32 = 18560

ENGINES = ("tensor", "vector", "scalar", "gpsimd", "sync")
TRACKS = ("tensor", "vector", "scalar", "gpsimd", "sync", "gq", "sq")


NSLOT = 8
DMA_TRACKS = ("sq", "gq")


def _slot_key(track, pos):
    return (track, pos % NSLOT)


class Op:
    __slots__ = ("issue", "track", "fn", "dma", "pos", "know", "inc", "waits",
                 "tag", "barrier")

    def __init__(self, issue, track, fn, dma, pos, tag, barrier=False):
        self.issue = issue
        self.track = track
        self.fn = fn
        self.dma = dma
        self.pos = pos
        self.know = None
        self.inc = dma
        self.waits = []
        self.tag = tag
        self.barrier = barrier


class Sched:

    def __init__(self):
        self.ops = []
        self.track_ops = {t: [] for t in TRACKS}
        self.last_writer = {}
        self.readers = {}
        self.know = {e: {} for e in ENGINES}

    def _dep_key_val(self, dt, dp):
        if dt in DMA_TRACKS:
            return _slot_key(dt, dp), dp // NSLOT + 1
        return dt, dp + 1

    def emit(self, issue, fn, reads=(), writes=(), dma=False, tag=""):
        if dma:
            track = {"sync": "sq", "gpsimd": "gq", "scalar": "sq"}[issue]
        else:
            track = issue
        pos = len(self.track_ops[track])
        op = Op(issue, track, fn, dma, pos, tag)
        if track in ("vector", "scalar", "gpsimd"):
            op.inc = True
        deps = {}

        def add(t, p):
            key = (t, p % NSLOT) if t in DMA_TRACKS else t
            cur = deps.get(key)
            if cur is None or cur[1] < p:
                deps[key] = (t, p)

        for r in reads:
            lw = self.last_writer.get(r)
            if lw is not None:
                add(*lw)
        for w in writes:
            lw = self.last_writer.get(w)
            if lw is not None:
                add(*lw)
            for (rt, rp) in self.readers.get(w, ()):
                add(rt, rp)
        know = self.know[issue]
        if dma and pos >= NSLOT:
            add(track, pos - NSLOT)
        for dkey in sorted(deps, key=str):
            dt, dp = deps[dkey]
            if dt == "tensor" and (dt == track or dt == issue):
                continue
            key, val = self._dep_key_val(dt, dp)
            if know.get(key, 0) >= val:
                continue
            dep_op = self.track_ops[dt][dp]
            dep_op.inc = True
            op.waits.append((key, val, dt, dp))
            for u, c in dep_op.know.items():
                if c > know.get(u, 0):
                    know[u] = c
            know[key] = max(know.get(key, 0), val)
        self.ops.append(op)
        self.track_ops[track].append(op)
        op.know = dict(know)
        okey, oval = self._dep_key_val(track, pos)
        op.know[okey] = max(op.know.get(okey, 0), oval)
        if not dma and track == "tensor":
            self.know[issue][track] = pos + 1
        for r in reads:
            self.readers.setdefault(r, []).append((track, pos))
        for w in writes:
            self.last_writer[w] = (track, pos)
            self.readers[w] = []
        return op

    def _all_dep_targets(self):
        out = []
        for t in TRACKS:
            n = len(self.track_ops[t])
            if t in DMA_TRACKS:
                for s in range(min(n, NSLOT)):
                    p = (n - 1 - s)
                    out.append((t, p))
            else:
                p = n - 1
                while p >= 0 and self.track_ops[t][p].fn is None:
                    p -= 1
                if p >= 0:
                    out.append((t, p))
        return out

    def barrier(self, tag="bar"):
        targets = self._all_dep_targets()
        for e in ENGINES:
            know = self.know[e]
            waits = []
            for (t, dp) in targets:
                key, val = self._dep_key_val(t, dp)
                if know.get(key, 0) >= val:
                    continue
                dep_op = self.track_ops[t][dp]
                dep_op.inc = True
                waits.append((key, val, t, dp))
                for u, c in dep_op.know.items():
                    if c > know.get(u, 0):
                        know[u] = c
                know[key] = max(know.get(key, 0), val)
            if waits:
                pos = len(self.track_ops[e])
                op = Op(e, e, None, False, pos, tag, barrier=True)
                op.waits = waits
                op.know = dict(know)
                self.ops.append(op)
                self.track_ops[e].append(op)
                self.know[e][e] = pos + 1

    def finalize(self, nc, block, total_wait_engine="sync"):
        semkeys = list(ENGINES)
        for q in DMA_TRACKS:
            for s in range(NSLOT):
                semkeys.append((q, s))
        sems = {kk: nc.ctx.enter_context(
            nc.semaphore(f"sem_{kk if isinstance(kk, str) else kk[0] + str(kk[1])}"))
            for kk in semkeys}
        inc_no = {}
        for t in TRACKS:
            if t in DMA_TRACKS:
                continue
            c = 0
            for op in self.track_ops[t]:
                if op.inc:
                    c += 1
                inc_no[(t, op.pos)] = c

        def wait_val(key, val, dt, dp):
            if dt in DMA_TRACKS:
                return 16 * val
            return inc_no[(dt, dp)]

        def run_engine(eng, name):
            for op in self.ops:
                if op.issue != name:
                    continue
                for (key, val, dt, dp) in op.waits:
                    eng.wait_ge(sems[key], wait_val(key, val, dt, dp))
                if op.fn is None:
                    continue
                try:
                    inst = op.fn(eng)
                except Exception:
                    print(f"EMIT FAIL at op tag={op.tag} track={op.track} "
                          f"pos={op.pos}")
                    raise
                if op.inc:
                    if op.dma:
                        inst.then_inc(sems[_slot_key(op.track, op.pos)], 16)
                    else:
                        inst.then_inc(sems[op.track], 1)
            if name == total_wait_engine:
                for (t, dp) in self._all_dep_targets():
                    key, val = self._dep_key_val(t, dp)
                    eng.wait_ge(sems[key], wait_val(key, val, t, dp))

        @block.sync
        def _(eng):
            run_engine(eng, "sync")

        @block.tensor
        def _(eng):
            run_engine(eng, "tensor")

        @block.vector
        def _(eng):
            run_engine(eng, "vector")

        @block.scalar
        def _(eng):
            run_engine(eng, "scalar")

        @block.gpsimd
        def _(eng):
            run_engine(eng, "gpsimd")


class K:
    def __init__(self, nc, sched):
        self.nc = nc
        self.s = sched
        self.t = {}
        self.dram = {}
        self.arena = None
        self.aoff = 0
        self.scopes = []
        self.carve_log = {}

    def sb(self, name, shape, dtype):
        h = self.nc.ctx.enter_context(self.nc.sbuf_tensor(name, shape, dtype))
        self.t[name] = h[:]
        return self.t[name]

    def ps(self, name, shape, dtype=F32):
        h = self.nc.ctx.enter_context(self.nc.psum_tensor(name, shape, dtype))
        self.t[name] = h[:]
        return self.t[name]

    def carve(self, name, shape, dtype):
        p, cols = shape
        f32cols = (cols * mybir.dt.size(dtype) + 3) // 4
        f32cols = (f32cols + 7) & ~7
        assert self.aoff + f32cols <= ARENA_F32, \
            f"arena overflow at {name}: {self.aoff}+{f32cols}"
        view = self.arena[:, self.aoff:self.aoff + f32cols]
        if dtype != F32:
            view = view.bitcast(dtype)
        view = view[0:p, 0:cols]
        self.aoff += f32cols
        self.t[name] = view
        self.carve_log[name] = (self.aoff - f32cols, p, cols, dtype)
        return view

    def push(self):
        self.scopes.append(self.aoff)

    def pop(self, tag="pop"):
        self.aoff = self.scopes.pop()
        self.s.barrier(tag)

    def din(self, name, shape, dtype):
        ap = self.nc.dram_tensor(name, shape, dtype, kind="ExternalInput").ap()
        self.dram[name] = ap
        return ap

    def dout(self, name, shape, dtype):
        ap = self.nc.dram_tensor(name, shape, dtype, kind="ExternalOutput").ap()
        self.dram[name] = ap
        return ap

    def dtmp(self, name, shape, dtype):
        ap = self.nc.dram_tensor(name, shape, dtype, kind="Internal").ap()
        self.dram[name] = ap
        return ap

    def op(self, issue, fn, r=(), w=(), dma=False, tag=""):
        self.s.emit(issue, fn, r, w, dma, tag)

    def dbg_ckpt(self, name):
        if not DEBUG_CKPT:
            return
        ap = self.dtmp(f"ckpt_{name}", [128, NDT * T], F32)
        self.dram[f"ckpt_{name}"] = ap
        xnv = self.t["xn"]
        scr = self.t["scr0"]
        for d2 in range(NDT):
            for nch2 in range(4):
                tsl2 = slice(nch2 * 512, (nch2 + 1) * 512)
                self.op("vector", lambda e, d2=d2, tsl2=tsl2: e.tensor_copy(
                    scr[:, tsl2], xs(self.t["xn"], d2, tsl2)),
                    r=("xn",), w=(f"scr0_{nch2}",), tag=f"ck{name}")
            self.dma("sync", ap[:, d2 * T:(d2 + 1) * T], scr[:, 0:T],
                     r=("scr0_0", "scr0_1", "scr0_2", "scr0_3"),
                     w=(f"ckpt_{name}",), tag=f"ckd{name}")

    def dma(self, queue, out, in_, r=(), w=(), tag=""):
        self.op(queue, lambda e: e.dma_start(out=out, in_=in_), r, w, dma=True,
                tag=tag)

    def orow(self, base=0):
        return self.t["ones_mat"][base:base + 1, 0:128]

    def kcol(self, value, p=128):
        key = f"kc_{value!r}"
        if key not in self.t:
            assert not getattr(self, "init_done", False), \
                f"kcol({value}) created after init barrier"
            col = self.sb(key, [128, 1], F32)
            self.op("gpsimd", lambda e: e.memset(col, float(value)),
                    r=(), w=(key,), tag=key)
        return self.t[key][0:p, :]

    def mm(self, out, lhsT, rhs, start, stop, r=(), w=(), tag=""):
        self.op("tensor",
                lambda e: e.matmul(out, lhsT, rhs, start=start, stop=stop),
                r, w, tag=tag)


def xs(x, d, tsl=None):
    if tsl is None:
        return x[:, d * T:(d + 1) * T]
    return x[:, d * T + tsl.start: d * T + tsl.stop]


def fm3(dram2d, p=128):
    return dram2d.rearrange("(n p) f -> p n f", p=p)


def sb3(view, n):
    return view.rearrange("p (n f) -> p n f", n=n)



def emit_ln(k, tag):
    x, xn = k.t["x"], k.t["xn"]
    bc_mu, bc_rs = k.t["bc_mu"], k.t["bc_rs"]
    scr0 = k.t["scr0"]
    oc = k.t["ones_col"]
    bank = k.t["banks"]
    row = k.t["row"]
    mu_r = bc_mu[0:1, :]
    tmp_r = bc_rs[0:1, :]
    for nch in range(4):
        tsl = slice(nch * 512, (nch + 1) * 512)
        for d in range(NDT):
            sq = scr0[:, (d % 2) * 512:(d % 2) * 512 + 512]
            k.op("scalar", lambda e, d=d, tsl=tsl, sq=sq: e.activation(
                sq, xs(x, d, tsl), AF.Square), r=("x",), w=(f"scr0_{d % 2}",),
                tag=f"{tag}sq")
            k.mm(bank[4][0:1, :], oc, xs(x, d, tsl), d == 0, d == NDT - 1,
                 r=("x",), w=("bank4",), tag=f"{tag}sum")
            k.mm(bank[5][0:1, :], oc, sq, d == 0, d == NDT - 1,
                 r=(f"scr0_{d % 2}",), w=("bank5",), tag=f"{tag}ssq")
        k.op("scalar", lambda e, tsl=tsl: e.activation(
            mu_r[:, tsl], bank[4][0:1, :], AF.Copy, scale=1.0 / D),
            r=("bank4",), w=("bc_mu",), tag=f"{tag}mu")
        k.op("scalar", lambda e, tsl=tsl: e.activation(
            row[:, tsl], bank[5][0:1, :], AF.Copy, scale=1.0 / D),
            r=("bank5",), w=("row",), tag=f"{tag}msq")
    k.op("vector", lambda e: e.tensor_tensor(tmp_r, mu_r, mu_r, ALU.mult),
         r=("bc_mu",), w=("bc_rs",), tag=f"{tag}mu2")
    k.op("vector", lambda e: e.tensor_sub(row, row, tmp_r),
         r=("row", "bc_rs"), w=("row",), tag=f"{tag}var")
    k.op("scalar", lambda e: e.activation(row, row, AF.Ln, bias=k.kcol(1e-5, 1)),
         r=("row",), w=("row",), tag=f"{tag}ln")
    k.op("scalar", lambda e: e.activation(row, row, AF.Exp, scale=-0.5),
         r=("row",), w=("row",), tag=f"{tag}rstd")
    for nch in range(4):
        tsl = slice(nch * 512, (nch + 1) * 512)
        k.mm(bank[5], k.orow(0), row[:, tsl], True, True,
             r=("row",), w=("bank5",), tag=f"{tag}brs")
        k.op("vector", lambda e, tsl=tsl: e.tensor_copy(bc_rs[:, tsl], bank[5]),
             r=("bank5",), w=("bc_rs",), tag=f"{tag}brse")
        k.mm(bank[4], k.orow(0), mu_r[:, tsl], True, True,
             r=("bc_mu",), w=("bank4",), tag=f"{tag}bmu")
        k.op("vector", lambda e, tsl=tsl: e.tensor_copy(bc_mu[:, tsl], bank[4]),
             r=("bank4",), w=("bc_mu",), tag=f"{tag}bmue")
    for d in range(NDT):
        scr = scr0[:, 0:T]
        k.op("vector", lambda e, d=d, scr=scr: e.tensor_sub(scr, xs(x, d), bc_mu),
             r=("x", "bc_mu"), w=("scr0_0", "scr0_1", "scr0_2", "scr0_3"),
             tag=f"{tag}xsub")
        k.op("vector", lambda e, d=d, scr=scr: e.tensor_mul(xs(xn, d), scr, bc_rs),
             r=("scr0_0", "scr0_1", "scr0_2", "scr0_3", "bc_rs"), w=("xn",),
             tag=f"{tag}xmul")



def emit_attn(k, wqkv_name, wout_name, causal, eps, bias_qkv, bias_out, tag):
    x, xn = k.t["x"], k.t["xn"]
    oc = k.t["ones_col"]
    identb, caus01 = k.t["ident_b"], k.t["causal01"]
    bank = k.t["banks"]
    bank_b = k.t["bank_b"]
    scr0 = k.t["scr0"]
    rows2 = k.t["row"]
    one1 = [k.t[f"one{i}"] for i in range(3)]
    col0 = k.t["col0"]
    Dh = 64
    scale = 1.0 / np.sqrt(Dh)
    wq = k.dram[wqkv_name]
    wo = k.dram[wout_name]

    k.push()
    kpair = k.carve(f"{tag}kp", [128, T], BF16)
    vpair = k.carve(f"{tag}vp", [128, T], BF16)
    qpair = k.carve(f"{tag}qp", [128, 1024], BF16)
    wpair = [k.carve(f"{tag}w0", [128, 3 * 1024], BF16),
             k.carve(f"{tag}w1", [128, 3 * 1024], BF16)]
    vtok = k.carve(f"{tag}vt", [128, 16 * 65], BF16)
    probs = [k.carve(f"{tag}pA", [128, 512], BF16),
             k.carve(f"{tag}pB", [128, 512], BF16)]
    zb = k.carve(f"{tag}zb", [128, 512], F32)
    ctx = k.carve(f"{tag}ctx", [128, NDT * 1024], BF16)
    wob = [k.carve(f"{tag}wo0", [128, 1024], BF16),
           k.carve(f"{tag}wo1", [128, 1024], BF16)]
    sqt = k.carve(f"{tag}sqt", [128, 1024 + T], F32)

    if bias_qkv is not None:
        bq = k.carve(f"{tag}bq", [128, 24], F32)
        bo = k.carve(f"{tag}bo", [128, 8], F32)
        k.dma("sync", bq, k.dram[bias_qkv], w=(f"{tag}bq",))
        k.dma("sync", bo, k.dram[bias_out], w=(f"{tag}bo",))
        bqs = k.carve(f"{tag}bqs", [128, 8], F32)
        k.op("vector", lambda e: e.tensor_scalar_mul(bqs, bq[:, 0:8], scale),
             r=(f"{tag}bq",), w=(f"{tag}bqs",), tag=f"{tag}bsc")

    for qh in range(2):
        q0 = qh * 1024
        kvT = T if (not causal or qh == 1) else 1024
        nkt = kvT // 128
        for hp in range(8):
            wb = wpair[hp % 2]
            wbr = f"{tag}w{hp % 2}"
            for which in range(3):
                src = wq[:, which * 1024 + hp * 128: which * 1024 + (hp + 1) * 128]
                k.dma("gpsimd", sb3(wb[:, which * 1024:(which + 1) * 1024], 8),
                      fm3(src),
                      r=(wbr,), w=(wbr,), tag=f"{tag}wld")
            for which, dstn, toks in ((0, f"{tag}qp", (q0, q0 + 1024)),
                                      (1, f"{tag}kp", (0, kvT)),
                                      (2, f"{tag}vp", (0, kvT))):
                dst = k.t[dstn]
                for tc in range((toks[1] - toks[0]) // 512):
                    gsl = slice(toks[0] + tc * 512, toks[0] + (tc + 1) * 512)
                    lsl = slice(gsl.start - toks[0] + (0 if which else 0),
                                gsl.stop - toks[0]) if which == 0 else gsl
                    bk = bank[tc % 2]
                    bkr = f"bank{tc % 2}"
                    for d in range(NDT):
                        k.mm(bk, wb[:, which * 1024 + d * 128: which * 1024 + (d + 1) * 128],
                             xs(xn, d, gsl), d == 0, d == NDT - 1,
                             r=(wbr, "xn"), w=(bkr,), tag=f"{tag}qkvm")
                    if which == 0:
                        if bias_qkv is None:
                            k.op("scalar", lambda e, bk=bk, dst=dst, lsl=lsl: e.activation(
                                dst[:, lsl], bk, AF.Copy, scale=scale),
                                r=(bkr,), w=(dstn,), tag=f"{tag}qe")
                        else:
                            k.op("vector", lambda e, bk=bk, dst=dst, lsl=lsl, hp=hp: e.tensor_scalar(
                                dst[:, lsl], bk, scale, bqs[:, hp:hp + 1],
                                ALU.mult, ALU.add),
                                r=(bkr, f"{tag}bqs"), w=(dstn,), tag=f"{tag}qe")
                    else:
                        if bias_qkv is None:
                            k.op("scalar", lambda e, bk=bk, dst=dst, lsl=lsl: e.copy(
                                dst[:, lsl], bk), r=(bkr,), w=(dstn,), tag=f"{tag}kve")
                        else:
                            k.op("vector", lambda e, bk=bk, dst=dst, lsl=lsl, hp=hp,
                                 which=which: e.tensor_single_scalar(
                                dst[:, lsl], bk,
                                bq[:, which * 8 + hp:which * 8 + hp + 1], ALU.add),
                                r=(bkr, f"{tag}bq"), w=(dstn,), tag=f"{tag}kve")
            k.op("scalar", lambda e: e.activation(sqt[:, 0:1024], qpair, AF.Square),
                 r=(f"{tag}qp",), w=(f"{tag}sqq",), tag=f"{tag}sqq")
            k.op("scalar", lambda e, kvT=kvT: e.activation(
                sqt[:, 1024:1024 + kvT], kpair[:, 0:kvT], AF.Square),
                r=(f"{tag}kp",), w=(f"{tag}sqk",), tag=f"{tag}sqk")
            for h2 in range(2):
                head = hp * 2 + h2
                po = h2 * 64
                for (ssl, srcres, oi) in ((slice(0, 1024), f"{tag}sqq", 0),
                                          (slice(1024, 1024 + kvT), f"{tag}sqk", 1)):
                    nrch = (ssl.stop - ssl.start) // 512
                    for nch in range(nrch):
                        tsl = slice(ssl.start + nch * 512, ssl.start + (nch + 1) * 512)
                        k.mm(bank[4][0:1, :], oc[po:po + 64, :], sqt[po:po + 64, tsl],
                             True, True, r=(srcres,), w=("bank4",), tag=f"{tag}n2")
                        k.op("vector", lambda e, nch=nch: e.tensor_copy(
                            rows2[:, nch * 512:(nch + 1) * 512], bank[4][0:1, :]),
                            r=("bank4",), w=("row",), tag=f"{tag}n2e")
                    k.op("vector", lambda e, oi=oi, nrch=nrch: e.reduce_max(
                        one1[oi], rows2[:, 0:nrch * 512], axis=AX.X),
                        r=("row",), w=(f"one{oi}",), tag=f"{tag}nmax")
                k.op("scalar", lambda e: e.activation(one1[0], one1[0], AF.Sqrt),
                     r=("one0",), w=("one0",), tag=f"{tag}sqr0")
                k.op("scalar", lambda e: e.activation(one1[1], one1[1], AF.Sqrt),
                     r=("one1",), w=("one1",), tag=f"{tag}sqr1")
                k.op("vector", lambda e: e.tensor_tensor(one1[2], one1[0], one1[1], ALU.mult),
                     r=("one0", "one1"), w=("one2",), tag=f"{tag}cmul")
                k.op("vector", lambda e: e.tensor_scalar_mul(one1[2], one1[2], -1.0),
                     r=("one2",), w=("one2",), tag=f"{tag}cneg")
                k.mm(bank[3][:, 0:1], k.orow(0), one1[2], True, True,
                     r=("one2",), w=("bank3",), tag=f"{tag}cb")
                k.op("vector", lambda e: e.tensor_copy(col0, bank[3][:, 0:1]),
                     r=("bank3",), w=("col0",), tag=f"{tag}cbe")
                k.op("gpsimd", lambda e: e.memset(vtok, 1.0), r=(),
                     w=(f"{tag}vt",), tag=f"{tag}vms")
                for kt in range(nkt):
                    k.op("tensor", lambda e, kt=kt, po=po: e.transpose(
                        bank_b[:, 0:64], vpair[po:po + 64, kt * 128:(kt + 1) * 128],
                        identb[po:po + 64, po:po + 64]),
                        r=(f"{tag}vp",), w=("bank_b",), tag=f"{tag}vT")
                    k.op("vector", lambda e, kt=kt: e.tensor_copy(
                        vtok[:, kt * 65:kt * 65 + 64], bank_b[:, 0:64]),
                        r=("bank_b",), w=(f"{tag}vt",), tag=f"{tag}vTe")
                for qc in range(2):
                    gq0 = q0 + qc * 512
                    ktm = nkt if not causal else min(nkt, (gq0 + 512) // 128)
                    for kt in range(ktm):
                        sb = bank[kt % 2]
                        sbr = f"bank{kt % 2}"
                        k.mm(sb, kpair[po:po + 64, kt * 128:(kt + 1) * 128],
                             qpair[po:po + 64, qc * 512:(qc + 1) * 512],
                             True, True, r=(f"{tag}kp", f"{tag}qp"), w=(sbr,),
                             tag=f"{tag}sc")
                        pb = probs[kt % 2]
                        pbr = f"{tag}p{'A' if kt % 2 == 0 else 'B'}"
                        k.op("scalar", lambda e, sb=sb, pb=pb: e.activation(
                            pb, sb, AF.Exp, bias=col0),
                            r=(sbr, "col0"), w=(pbr,), tag=f"{tag}exp")
                        if causal:
                            for j in range(4):
                                Qj = (gq0 + j * 128) // 128
                                if kt == Qj:
                                    k.op("vector", lambda e, pb=pb, j=j: e.tensor_mul(
                                        pb[:, j * 128:(j + 1) * 128],
                                        pb[:, j * 128:(j + 1) * 128], caus01),
                                        r=(pbr,), w=(pbr,), tag=f"{tag}diag")
                                elif kt > Qj:
                                    k.op("gpsimd", lambda e, pb=pb, j=j: e.memset(
                                        pb[:, j * 128:(j + 1) * 128], 0.0),
                                        r=(), w=(pbr,), tag=f"{tag}mz")
                        k.mm(bank[2][0:65, :], vtok[:, kt * 65:(kt + 1) * 65], pb,
                             kt == 0, kt == ktm - 1,
                             r=(f"{tag}vt", pbr), w=("bank2",), tag=f"{tag}pv")
                    k.op("vector", lambda e: e.tensor_copy(rows2[:, 0:512],
                                                           bank[2][64:65, :]),
                         r=("bank2",), w=("row",), tag=f"{tag}z")
                    if eps:
                        k.op("vector", lambda e: e.tensor_scalar_add(
                            rows2[:, 0:512], rows2[:, 0:512], eps),
                            r=("row",), w=("row",), tag=f"{tag}zeps")
                    k.op("vector", lambda e: e.reciprocal(rows2[:, 0:512],
                                                          rows2[:, 0:512]),
                         r=("row",), w=("row",), tag=f"{tag}zi")
                    k.mm(bank[3], k.orow(0), rows2[:, 0:512], True, True,
                         r=("row",), w=("bank3",), tag=f"{tag}zbm")
                    k.op("vector", lambda e: e.tensor_copy(zb, bank[3]),
                         r=("bank3",), w=(f"{tag}zbuf",), tag=f"{tag}zbe")
                    dd, rr = divmod(head * 64, 128)
                    k.op("vector", lambda e, dd=dd, rr=rr, qc=qc: e.tensor_tensor(
                        ctx[rr:rr + 64, dd * 1024 + qc * 512: dd * 1024 + (qc + 1) * 512],
                        bank[2][0:64, :], zb[0:64, :], ALU.mult),
                        r=("bank2", f"{tag}zbuf"), w=(f"{tag}ctx",), tag=f"{tag}ctxw")
        for fc in range(8):
            wbuf = wob[fc % 2]
            wbr = f"{tag}wo{fc % 2}"
            k.dma("gpsimd", sb3(wbuf, 8),
                  fm3(wo[:, fc * 128:(fc + 1) * 128]),
                  r=(wbr,), w=(wbr,), tag=f"{tag}wold")
            for tc in range(2):
                gsl = slice(q0 + tc * 512, q0 + (tc + 1) * 512)
                bk = bank[4 + tc % 2]
                bkr = f"bank{4 + tc % 2}"
                for d in range(NDT):
                    k.mm(bk, wbuf[:, d * 128:(d + 1) * 128],
                         ctx[:, d * 1024 + tc * 512: d * 1024 + (tc + 1) * 512],
                         d == 0, d == NDT - 1, r=(wbr, f"{tag}ctx"), w=(bkr,),
                         tag=f"{tag}opm")
                if bias_out is None:
                    k.op("vector", lambda e, bk=bk, fc=fc, gsl=gsl: e.tensor_add(
                        xs(x, fc, gsl), xs(x, fc, gsl), bk),
                        r=(bkr, "x"), w=("x",), tag=f"{tag}oadd")
                else:
                    k.op("vector", lambda e, bk=bk, fc=fc, gsl=gsl: e.scalar_tensor_tensor(
                        xs(x, fc, gsl), bk, bo[:, fc:fc + 1], xs(x, fc, gsl),
                        ALU.add, ALU.add),
                        r=(bkr, "x", f"{tag}bo"), w=("x",), tag=f"{tag}oaddb")
    k.pop(f"{tag}done")



def emit_l0_ffn(k):
    x, xn = k.t["x"], k.t["xn"]
    oc = k.t["ones_col"]
    bank = k.t["banks"]
    scr0 = k.t["scr0"]
    col0 = k.t["col0"]
    one1 = [k.t[f"one{i}"] for i in range(3)]
    row = k.t["row"]
    bc_mu, bc_rs = k.t["bc_mu"], k.t["bc_rs"]
    ocb = k.t["ones_col_b"]

    k.push()
    fw = [k.carve("fw0", [128, 4096], F32), k.carve("fw1", [128, 4096], F32)]
    fq = k.carve("fqz", [128, 4096], BF16)
    facc = k.carve("facc", [128, 1], F32)
    sgn = k.carve("sgn", [128, 4096], F32)
    for wnm, qnm, nrow, ncol in (("l0_wfc_t", "wq_fc", D, 4096),
                                 ("l0_wproj_t", "wq_proj", 4096, D)):
        wd = k.dram[wnm]
        qd = k.dram[qnm]
        ntile = nrow // 128
        for i in range(ntile):
            buf = fw[i % 2]
            bnm = f"fw{i % 2}"
            k.dma("sync", buf[:, 0:ncol], wd[i * 128:(i + 1) * 128, :],
                  r=(bnm,), w=(bnm,), tag="q1ld")
            k.op("vector", lambda e, buf=buf, ncol=ncol: e.reduce_sum(
                col0, buf[:, 0:ncol], axis=AX.X, apply_absolute_value=True),
                r=(bnm,), w=("col0",), tag="q1abs")
            if i == 0:
                k.op("vector", lambda e: e.tensor_copy(facc, col0),
                     r=("col0",), w=("facc",), tag="q1cp")
            else:
                k.op("vector", lambda e: e.tensor_add(facc, facc, col0),
                     r=("col0", "facc"), w=("facc",), tag="q1acc")
        k.mm(bank[6][0:1, 0:1], oc, facc, True, True,
             r=("facc",), w=("bank6",), tag="q1red")
        nel = float(nrow * ncol)
        k.op("scalar", lambda e, nel=nel: e.activation(
            one1[0], bank[6][0:1, 0:1], AF.Copy, scale=0.5 / nel, bias=0.5e-6),
            r=("bank6",), w=("one0",), tag="q1th")
        k.mm(bank[6][:, 0:1], k.orow(0), one1[0], True, True,
             r=("one0",), w=("bank6",), tag="q1thb")
        k.op("vector", lambda e: e.tensor_copy(col0, bank[6][:, 0:1]),
             r=("bank6",), w=("col0",), tag="q1the")
        for i in range(ntile):
            buf = fw[i % 2]
            bnm = f"fw{i % 2}"
            k.dma("sync", buf[:, 0:ncol], wd[i * 128:(i + 1) * 128, :],
                  r=(bnm,), w=(bnm,), tag="q2ld")
            k.op("scalar", lambda e, buf=buf, ncol=ncol: e.activation(
                sgn[:, 0:ncol], buf[:, 0:ncol], AF.Sign),
                r=(bnm,), w=("sgn",), tag="q2sgn")
            k.op("scalar", lambda e, buf=buf, ncol=ncol: e.activation(
                buf[:, 0:ncol], buf[:, 0:ncol], AF.Abs),
                r=(bnm,), w=(bnm,), tag="q2abs")
            k.op("vector", lambda e, buf=buf, ncol=ncol: e.tensor_single_scalar(
                buf[:, 0:ncol], buf[:, 0:ncol], col0, ALU.is_gt),
                r=(bnm, "col0"), w=(bnm,), tag="q2cmp")
            k.op("vector", lambda e, buf=buf, ncol=ncol: e.tensor_mul(
                fq[:, 0:ncol], sgn[:, 0:ncol], buf[:, 0:ncol]),
                r=(bnm, "sgn"), w=("fqz",), tag="q2mul")
            k.dma("sync", qd[i * 128:(i + 1) * 128, :], fq[:, 0:ncol],
                  r=("fqz",), w=(qnm,), tag="q2st")
    k.pop("q_done")

    qfc = k.dram["wq_fc"]
    qpr = k.dram["wq_proj"]
    lnw_d, lnb_d = k.dram["l0_lnw"], k.dram["l0_lnb"]
    k.push()
    g32 = k.carve("g32", [128, 32 * 512], BF16)
    pw = [k.carve("pw0", [128, 1024], BF16), k.carve("pw1", [128, 1024], BF16)]
    hw = [k.carve("hw0", [128, 1024], BF16), k.carve("hw1", [128, 1024], BF16)]
    lnw = k.carve("lnw", [128, 32], F32)
    lnb = k.carve("lnb", [128, 32], F32)
    k.dma("sync", lnw, lnw_d, w=("lnw",))
    k.dma("sync", lnb, lnb_d, w=("lnb",))
    for tq in range(4):
        tsl = slice(tq * 512, (tq + 1) * 512)
        for fc in range(32):
            wbuf = hw[fc % 2]
            wbr = f"hw{fc % 2}"
            k.dma("sync", sb3(wbuf, 8),
                  fm3(qfc[:, fc * 128:(fc + 1) * 128]),
                  r=(wbr,), w=(wbr,), tag="hld")
            bk = bank[fc % 2]
            bkr = f"bank{fc % 2}"
            for d in range(NDT):
                k.mm(bk, wbuf[:, d * 128:(d + 1) * 128], xs(xn, d, tsl),
                     d == 0, d == NDT - 1, r=(wbr, "xn"), w=(bkr,), tag="hmm")
            k.op("scalar", lambda e, bk=bk, fc=fc: e.activation(
                g32[:, fc * 512:(fc + 1) * 512], bk, AF.Gelu),
                r=(bkr,), w=("g32",), tag="hgelu")
        for fc in range(32):
            gv = g32[:, fc * 512:(fc + 1) * 512]
            k.mm(bank[4][0:1, :], ocb, gv, fc == 0, fc == 31,
                 r=("g32",), w=("bank4",), tag="lns")
            sq = scr0[:, (fc % 2) * 512:(fc % 2) * 512 + 512]
            k.op("scalar", lambda e, gv=gv, sq=sq: e.activation(
                sq, gv, AF.Square), r=("g32",), w=(f"scr0_{fc % 2}",), tag="lnsq")
            k.mm(bank[5][0:1, :], oc, sq, fc == 0, fc == 31,
                 r=(f"scr0_{fc % 2}",), w=("bank5",), tag="lnss")
        rsl = slice(0, 512)
        mu_r = bc_mu[0:1, 0:512]
        tmp_r = bc_rs[0:1, 0:512]
        rr = row[:, rsl]
        k.op("scalar", lambda e: e.activation(mu_r, bank[4][0:1, :],
                                              AF.Copy, scale=1.0 / 4096),
             r=("bank4",), w=("bc_mu",), tag="lnmu")
        k.op("scalar", lambda e: e.activation(rr, bank[5][0:1, :],
                                              AF.Copy, scale=1.0 / 4096),
             r=("bank5",), w=("row",), tag="lnms")
        k.op("vector", lambda e: e.tensor_tensor(tmp_r, mu_r, mu_r, ALU.mult),
             r=("bc_mu",), w=("bc_rs",), tag="lnm2")
        k.op("vector", lambda e: e.tensor_sub(rr, rr, tmp_r),
             r=("row", "bc_rs"), w=("row",), tag="lnvr")
        k.op("scalar", lambda e: e.activation(rr, rr, AF.Ln, bias=k.kcol(1e-5, 1)),
             r=("row",), w=("row",), tag="lnl")
        k.op("scalar", lambda e: e.activation(rr, rr, AF.Exp, scale=-0.5),
             r=("row",), w=("row",), tag="lnr")
        k.mm(bank[5], k.orow(0), rr, True, True, r=("row",),
             w=("bank5",), tag="lbr")
        k.op("vector", lambda e: e.tensor_copy(bc_rs[:, 0:512], bank[5]),
             r=("bank5",), w=("bc_rs",), tag="lbre")
        k.mm(bank[4], k.orow(0), mu_r, True, True, r=("bc_mu",),
             w=("bank4",), tag="lbm")
        k.op("vector", lambda e: e.tensor_copy(bc_mu[:, 0:512], bank[4]),
             r=("bank4",), w=("bc_mu",), tag="lbme")
        for fc in range(32):
            gv = g32[:, fc * 512:(fc + 1) * 512]
            sq = scr0[:, (fc % 2) * 512:(fc % 2) * 512 + 512]
            k.op("vector", lambda e, gv=gv, sq=sq: e.tensor_sub(sq, gv, bc_mu[:, 0:512]),
                 r=("g32", "bc_mu"), w=(f"scr0_{fc % 2}",), tag="lhs")
            k.op("vector", lambda e, sq=sq: e.tensor_mul(sq, sq, bc_rs[:, 0:512]),
                 r=(f"scr0_{fc % 2}", "bc_rs"), w=(f"scr0_{fc % 2}",), tag="lhm")
            k.op("vector", lambda e, gv=gv, sq=sq, fc=fc: e.tensor_scalar(
                gv, sq, lnw[:, fc:fc + 1], lnb[:, fc:fc + 1], ALU.mult, ALU.add),
                r=(f"scr0_{fc % 2}", "lnw", "lnb"), w=("g32",), tag="lhwb")
        for fo in range(8):
            wbuf = pw[fo % 2]
            wbr = f"pw{fo % 2}"
            for hh in range(4):
                k.dma("sync", sb3(wbuf, 8),
                      fm3(qpr[hh * 1024:(hh + 1) * 1024, fo * 128:(fo + 1) * 128]),
                      r=(wbr,), w=(wbr,), tag="pld")
                bk = bank[2 + fo % 2]
                bkr = f"bank{2 + fo % 2}"
                for d8 in range(8):
                    hc = hh * 8 + d8
                    k.mm(bk, wbuf[:, d8 * 128:(d8 + 1) * 128],
                         g32[:, hc * 512:(hc + 1) * 512],
                         hc == 0, hc == 31, r=(wbr, "g32"), w=(bkr,), tag="pmm")
            bk = bank[2 + fo % 2]
            bkr = f"bank{2 + fo % 2}"
            k.op("vector", lambda e, bk=bk, fo=fo, tsl=tsl: e.tensor_add(
                xs(x, fo, tsl), xs(x, fo, tsl), bk),
                r=(bkr, "x"), w=("x",), tag="padd")
    k.pop("ffn_done")


def emit_layer0(k):
    emit_ln(k, "l0ln_")
    emit_attn(k, "l0_wqkv_t", "l0_wout_t", causal=False, eps=1e-6,
              bias_qkv=None, bias_out=None, tag="l0a_")
    emit_l0_ffn(k)


L1_PARTS = "am"


def emit_layer1(k):
    emit_ln(k, "l1ln_")
    if "a" in L1_PARTS:
        emit_attn(k, "l1_wqkv", "l1_wout", causal=True, eps=0.0,
                  bias_qkv="l1_bqkv", bias_out="l1_bout", tag="l1a_")
    if "m" in L1_PARTS:
        emit_l1_moe(k)


def emit_l1_moe(k):
    x, xn = k.t["x"], k.t["xn"]
    bank = k.t["banks"]
    bank_b = k.t["bank_b"]
    identb = k.t["ident_b"]
    col0 = k.t["col0"]
    w1d = k.dram["l1_w1"]
    w2d = k.dram["l1_w2"]

    k.push()
    act = k.carve("m_act", [128, 8 * T], BF16)
    w1b = [k.carve(f"m_w1{i}", [128, 2 * 2048], BF16) for i in range(2)]
    w2b = [k.carve(f"m_w2{i}", [128, 1024], BF16) for i in range(2)]
    bcm = k.carve("m_bcm", [128, T], F32)
    lgt = k.carve("m_lgt", [128, 16 * 8], F32)
    mrow = k.carve("m_mrow", [8, T], F32)
    mv1 = k.t["row"]
    rw = k.carve("m_rw", [128, 64], BF16)
    rb = k.carve("m_rb", [128, 1], F32)
    mscr = k.carve("m_scr", [128, 8], F32)
    sb16 = k.carve("m_sb16", [128, 512], BF16)

    k.dma("gpsimd", sb3(rw, 8), fm3(k.dram["l1_rw"]), r=("m_rw",), w=("m_rw",),
          tag="rt_ld")
    k.dma("sync", rb, k.dram["l1_rb"], w=("m_rb",), tag="rt_lb")
    for tc in range(4):
        tsl = slice(tc * 512, (tc + 1) * 512)
        for d in range(NDT):
            k.mm(bank[0][0:8, :], rw[:, d * 8:(d + 1) * 8], xs(xn, d, tsl),
                 d == 0, d == NDT - 1, r=("m_rw", "xn"), w=("bank0",), tag="rt_mm")
        k.op("vector", lambda e, tsl=tsl: e.tensor_single_scalar(
            k.t["scr0"][0:8, 0:512], bank[0][0:8, :], rb[0:8, :], ALU.add),
            r=("bank0", "m_rb"), w=("scr0_0",), tag="rt_b")
        for c in range(4):
            ch = tc * 4 + c
            k.op("tensor", lambda e, c=c: e.transpose(
                bank[5][:, 0:8],
                k.t["scr0"][0:8, c * 128:(c + 1) * 128], k.t["ident_f"][0:8, 0:8]),
                r=("scr0_0", "ident_f"), w=("bank5",), tag="rt_T")
            k.op("vector", lambda e, ch=ch: e.tensor_copy(
                lgt[:, ch * 8:(ch + 1) * 8], bank[5][:, 0:8]),
                r=("bank5",), w=("m_lgt",), tag="rt_Te")
    for ch in range(16):
        lg = lgt[:, ch * 8:(ch + 1) * 8]
        k.op("vector", lambda e, lg=lg: e.reduce_max(col0, lg, axis=AX.X),
             r=("m_lgt",), w=("col0",), tag="mx")
        k.op("vector", lambda e, lg=lg: e.tensor_single_scalar(
            mscr, lg, col0, ALU.subtract), r=("m_lgt", "col0"), w=("m_scr",),
            tag="msub")
        k.op("scalar", lambda e: e.activation(mscr, mscr, AF.Exp),
             r=("m_scr",), w=("m_scr",), tag="mexp")
        k.op("vector", lambda e: e.reduce_sum(col0, mscr, axis=AX.X),
             r=("m_scr",), w=("col0",), tag="msum")
        k.op("vector", lambda e: e.reciprocal(col0, col0),
             r=("col0",), w=("col0",), tag="mrec")
        k.op("vector", lambda e, lg=lg: e.tensor_single_scalar(
            mscr, mscr, 1.0, ALU.is_ge), r=("m_scr",), w=("m_scr",), tag="mge")
        k.op("vector", lambda e: e.tensor_single_scalar(
            mscr, mscr, col0, ALU.mult), r=("m_scr", "col0"), w=("m_scr",),
            tag="mmp")
        k.op("tensor", lambda e, ch=ch: e.transpose(
            bank[5][0:8, 0:128], mscr, k.t["ident_f"]),
            r=("m_scr", "ident_f"), w=("bank5",), tag="mT")
        k.op("vector", lambda e, ch=ch: e.tensor_copy(
            mrow[:, ch * 128:(ch + 1) * 128], bank[5][0:8, 0:128]),
            r=("bank5",), w=("m_mrow",), tag="mTe")

    for e_i in range(8):
        k.dma("gpsimd", mv1, mrow[e_i:e_i + 1, :], r=("m_mrow", "row"),
              w=("row",), tag="mvst")
        for c4 in range(4):
            tsl = slice(c4 * 512, (c4 + 1) * 512)
            k.mm(bank[6], k.orow(0), mv1[:, tsl], True, True,
                 r=("row",), w=("bank6",), tag="mvb")
            k.op("vector", lambda e, tsl=tsl: e.tensor_copy(bcm[:, tsl], bank[6]),
                 r=("bank6",), w=("m_bcm",), tag="mvbe")
        for jg in range(4):
            for j4 in range(4):
                j = jg * 4 + j4
                wb = w1b[j % 2]
                wbr = f"m_w1{j % 2}"
                k.dma("gpsimd", sb3(wb[:, 0:2048], 8),
                      fm3(w1d[e_i, :, j * 256:(j + 1) * 256]),
                      r=(wbr,), w=(wbr,), tag="w1la")
                k.dma("gpsimd", sb3(wb[:, 2048:4096], 8),
                      fm3(w1d[e_i, :, 4096 + j * 256: 4096 + (j + 1) * 256]),
                      r=(wbr,), w=(wbr,), tag="w1lg")
                for tc in range(4):
                    tsl = slice(tc * 512, (tc + 1) * 512)
                    for ms in range(2):
                        ba, bg = bank[tc % 2], bank[2 + tc % 2]
                        bar, bgr = f"bank{tc % 2}", f"bank{2 + tc % 2}"
                        for d in range(NDT):
                            k.mm(ba, wb[:, d * 256 + ms * 128: d * 256 + (ms + 1) * 128],
                                 xs(xn, d, tsl), d == 0, d == NDT - 1,
                                 r=(wbr, "xn"), w=(bar,), tag="amm")
                        for d in range(NDT):
                            k.mm(bg, wb[:, 2048 + d * 256 + ms * 128: 2048 + d * 256 + (ms + 1) * 128],
                                 xs(xn, d, tsl), d == 0, d == NDT - 1,
                                 r=(wbr, "xn"), w=(bgr,), tag="gmm")
                        k.op("scalar", lambda e, ba=ba: e.activation(
                            sb16, ba, AF.Silu), r=(bar,), w=("m_sb16",),
                            tag="asilu")
                        ht = j4 * 2 + ms
                        k.op("vector", lambda e, bg=bg, ht=ht, tc=tc: e.tensor_mul(
                            act[:, ht * T + tc * 512: ht * T + (tc + 1) * 512],
                            sb16, bg), r=("m_sb16", bgr), w=("m_act",),
                            tag="actw")
            for fo in range(8):
                wb = w2b[fo % 2]
                wbr = f"m_w2{fo % 2}"
                k.dma("gpsimd", sb3(wb, 8),
                      fm3(w2d[e_i, jg * 1024:(jg + 1) * 1024, fo * 128:(fo + 1) * 128]),
                      r=(wbr,), w=(wbr,), tag="w2ld")
                for tc in range(4):
                    for ht in range(8):
                        k.mm(bank[4],
                             wb[:, ht * 128:(ht + 1) * 128],
                             act[:, ht * T + tc * 512: ht * T + (tc + 1) * 512],
                             ht == 0, ht == 7, r=(wbr, "m_act"),
                             w=("bank4",), tag="ymm")
                    k.op("vector", lambda e, fo=fo, tc=tc: e.tensor_mul(
                        k.t["scr0"][:, 0:512], bank[4],
                        bcm[:, tc * 512:(tc + 1) * 512]),
                        r=("bank4", "m_bcm"), w=("scr0_0",), tag="ymask")
                    k.op("vector", lambda e, fo=fo, tc=tc: e.tensor_add(
                        xs(x, fo, slice(tc * 512, (tc + 1) * 512)),
                        xs(x, fo, slice(tc * 512, (tc + 1) * 512)),
                        k.t["scr0"][:, 0:512]),
                        r=("scr0_0", "x"), w=("x",), tag="yadd")
    k.pop("moe_done")


def emit_rms_bcast(k, src_tiles_fn, nd, eps, bc_dst, bcres, tag):
    bank = k.t["banks"]
    scr0 = k.t["scr0"]
    row = k.t["row"]
    oc = k.t["ones_col"]
    for nch in range(4):
        tsl = slice(nch * 512, (nch + 1) * 512)
        for d in range(nd):
            src, srcres = src_tiles_fn(d, tsl)
            sq = scr0[:, (d % 2) * 512:(d % 2) * 512 + 512]
            k.op("scalar", lambda e, src=src, sq=sq: e.activation(
                sq, src, AF.Square), r=(srcres,), w=(f"scr0_{d % 2}",),
                tag=f"{tag}sq")
            k.mm(bank[5][0:1, :], oc, sq, d == 0, d == nd - 1,
                 r=(f"scr0_{d % 2}",), w=("bank5",), tag=f"{tag}ssq")
        k.op("scalar", lambda e, tsl=tsl: e.activation(
            row[:, tsl], bank[5][0:1, :], AF.Copy, scale=1.0 / (nd * 128)),
            r=("bank5",), w=("row",), tag=f"{tag}ms")
    k.op("scalar", lambda e: e.activation(row, row, AF.Ln, bias=k.kcol(eps, 1)),
         r=("row",), w=("row",), tag=f"{tag}ln")
    k.op("scalar", lambda e: e.activation(row, row, AF.Exp, scale=-0.5),
         r=("row",), w=("row",), tag=f"{tag}ex")
    for nch in range(4):
        tsl = slice(nch * 512, (nch + 1) * 512)
        k.mm(bank[5], k.orow(0), row[:, tsl], True, True,
             r=("row",), w=("bank5",), tag=f"{tag}bm")
        k.op("vector", lambda e, tsl=tsl: e.tensor_copy(bc_dst[:, tsl], bank[5]),
             r=("bank5",), w=(bcres,), tag=f"{tag}be")


def emit_l2_ffn(k):
    x, xn = k.t["x"], k.t["xn"]
    bank = k.t["banks"]

    k.push()
    fwb = [k.carve("f2w0", [128, 4096], F32), k.carve("f2w1", [128, 4096], F32)]
    fqz = k.carve("f2q", [128, 4096], BF16)
    a1 = k.carve("f2a1", [128, 32], F32)
    a2 = k.carve("f2a2", [128, 8], F32)
    b1 = k.carve("f2b1", [128, 32], F32)
    b2 = k.carve("f2b2", [128, 8], F32)
    k.dma("sync", b1, k.dram["l2_fb1"], w=("f2b1",))
    k.dma("sync", b2, k.dram["l2_fb2"], w=("f2b2",))
    for nm, dst, dstres, ntile, ncol in (("l2_fw1n", a1, "f2a1", 32, 1024),
                                         ("l2_fw2n", a2, "f2a2", 8, 4096)):
        wd = k.dram[nm]
        for i in range(ntile):
            buf = fwb[i % 2]
            bnm = f"f2w{i % 2}"
            k.dma("sync", buf[:, 0:ncol], wd[i * 128:(i + 1) * 128, :],
                  r=(bnm,), w=(bnm,), tag="a_ld")
            k.op("vector", lambda e, buf=buf, dst=dst, i=i, ncol=ncol: e.reduce_sum(
                dst[:, i:i + 1], buf[:, 0:ncol], axis=AX.X,
                apply_absolute_value=True),
                r=(bnm,), w=(dstres,), tag="a_red")
        k.op("vector", lambda e, dst=dst, ntile=ntile, ncol=ncol: e.tensor_scalar_mul(
            dst[:, 0:ntile], dst[:, 0:ntile], 1.0 / ncol),
            r=(dstres,), w=(dstres,), tag="a_sc")
    for nm, qnm, ntile, ncol in (("l2_fw1t", "sgn_w1", 8, 4096),
                                 ("l2_fw2t", "sgn_w2", 32, 1024)):
        wd, qd = k.dram[nm], k.dram[qnm]
        for i in range(ntile):
            buf = fwb[i % 2]
            bnm = f"f2w{i % 2}"
            k.dma("sync", buf[:, 0:ncol], wd[i * 128:(i + 1) * 128, :],
                  r=(bnm,), w=(bnm,), tag="s_ld")
            k.op("scalar", lambda e, buf=buf, ncol=ncol: e.activation(
                fqz[:, 0:ncol], buf[:, 0:ncol], AF.Sign),
                r=(bnm,), w=("f2q",), tag="s_sgn")
            k.dma("sync", qd[i * 128:(i + 1) * 128, :], fqz[:, 0:ncol],
                  r=("f2q",), w=(qnm,), tag="s_st")
    k.pop("f2q_done")

    sg1 = k.dram["sgn_w1"]
    sg2 = k.dram["sgn_w2"]
    k.push()
    hbuf = k.carve("f2h", [128, 32 * 512], BF16)
    hw = [k.carve("f2hw0", [128, 1024], BF16), k.carve("f2hw1", [128, 1024], BF16)]
    for tq in range(4):
        tsl = slice(tq * 512, (tq + 1) * 512)
        for fc in range(32):
            wbuf = hw[fc % 2]
            wbr = f"f2hw{fc % 2}"
            k.dma("sync", sb3(wbuf, 8), fm3(sg1[:, fc * 128:(fc + 1) * 128]),
                  r=(wbr,), w=(wbr,), tag="f2hld")
            bk = bank[fc % 2]
            bkr = f"bank{fc % 2}"
            for d in range(NDT):
                k.mm(bk, wbuf[:, d * 128:(d + 1) * 128], xs(xn, d, tsl),
                     d == 0, d == NDT - 1, r=(wbr, "xn"), w=(bkr,), tag="f2hmm")
            k.op("scalar", lambda e, bk=bk, fc=fc: e.activation(
                hbuf[:, fc * 512:(fc + 1) * 512], bk, AF.Gelu,
                scale=a1[:, fc:fc + 1], bias=b1[:, fc:fc + 1]),
                r=(bkr, "f2a1", "f2b1"), w=("f2h",), tag="f2gelu")
        for fo in range(8):
            for hh in range(4):
                wbuf = hw[hh % 2]
                wbr = f"f2hw{hh % 2}"
                k.dma("sync", sb3(wbuf, 8),
                      fm3(sg2[hh * 1024:(hh + 1) * 1024, fo * 128:(fo + 1) * 128]),
                      r=(wbr,), w=(wbr,), tag="f2pld")
                for d8 in range(8):
                    hc = hh * 8 + d8
                    k.mm(bank[2], wbuf[:, d8 * 128:(d8 + 1) * 128],
                         hbuf[:, hc * 512:(hc + 1) * 512],
                         hc == 0, hc == 31, r=(wbr, "f2h"), w=("bank2",),
                         tag="f2pmm")
            k.op("vector", lambda e, fo=fo: e.tensor_scalar(
                k.t["scr0"][:, 0:512], bank[2], a2[:, fo:fo + 1], b2[:, fo:fo + 1],
                ALU.mult, ALU.add),
                r=("bank2", "f2a2", "f2b2"), w=("scr0_0",), tag="f2ev")
            k.op("vector", lambda e, fo=fo, tsl=tsl: e.tensor_add(
                xs(x, fo, tsl), xs(x, fo, tsl), k.t["scr0"][:, 0:512]),
                r=("scr0_0", "x"), w=("x",), tag="f2add")
    k.pop("f2_done")


L2_STAGE = 99


def emit_layer2(k):
    emit_ln(k, "l2ln_")
    if L2_STAGE >= 1:
        emit_l2_ffn(k)
    if L2_STAGE >= 2:
        emit_l2_attn(k)


def emit_l2_attn(k):
    x, xn = k.t["x"], k.t["xn"]
    bank = k.t["banks"]
    scr0 = k.t["scr0"]
    identf = k.t["ident_f"]

    k.push()
    rwidx = k.carve("l2idx", [128, 16 * 4], F32)
    featsA = k.t["bc_mu"]
    featsC = k.t["bc_rs"]

    k.push()
    ug = k.carve("l2u", [128, NDT * T], BF16)

    k.push()
    bc2 = k.carve("l2bc2", [128, T], F32)
    gb = k.carve("l2gb", [128, 8], F32)
    gwb = [k.carve("l2gw0", [128, 1024], BF16), k.carve("l2gw1", [128, 1024], BF16)]
    emit_rms_bcast(k, lambda d, tsl: (xs(xn, d, tsl), "xn"), NDT, 1e-6,
                   bc2, "l2bc2", "l2rms_")
    k.dma("sync", gb, k.dram["l2_gate_b"], w=("l2gb",))
    gwt = k.dram["l2_gate_wt"]
    for fc in range(NDT):
        wbuf = gwb[fc % 2]
        wbr = f"l2gw{fc % 2}"
        k.dma("gpsimd", sb3(wbuf, 8), fm3(gwt[:, fc * 128:(fc + 1) * 128]),
              r=(wbr,), w=(wbr,), tag="l2gld")
        for tc in range(4):
            tsl = slice(tc * 512, (tc + 1) * 512)
            bk = bank[tc % 2]
            bkr = f"bank{tc % 2}"
            for d in range(NDT):
                k.mm(bk, wbuf[:, d * 128:(d + 1) * 128], xs(xn, d, tsl),
                     d == 0, d == NDT - 1, r=(wbr, "xn"), w=(bkr,), tag="l2gmm")
            k.op("vector", lambda e, bk=bk, tsl=tsl: e.tensor_mul(
                scr0[:, 0:512], bk, bc2[:, tsl]),
                r=(bkr, "l2bc2"), w=("scr0_0",), tag="l2gsc")
            k.op("scalar", lambda e, fc=fc, tsl=tsl: e.activation(
                xs(ug, fc, tsl), scr0[:, 0:512], AF.Sigmoid,
                bias=gb[:, fc:fc + 1]),
                r=("scr0_0", "l2gb"), w=("l2u",), tag="l2gsig")
    cw = k.carve("l2cw", [128, 32], F32)
    k.dma("sync", cw, k.dram["l2_conv"], w=("l2cw",))
    x2t = k.carve("l2x2", [128, T], F32)
    hcv = k.carve("l2hc", [128, T], F32)
    for d in range(NDT):
        k.op("vector", lambda e, d=d: e.tensor_mul(x2t, xs(xn, d), bc2),
             r=("xn", "l2bc2"), w=("l2x2",), tag="l2x2")
        k.op("vector", lambda e, d=d: e.tensor_single_scalar(
            hcv, x2t, cw[:, d * 4 + 3: d * 4 + 4], ALU.mult),
            r=("l2x2", "l2cw"), w=("l2hc",), tag="l2cv3")
        for jj in range(3):
            sft = 3 - jj
            k.op("vector", lambda e, d=d, jj=jj, sft=sft: e.scalar_tensor_tensor(
                hcv[:, sft:T], x2t[:, 0:T - sft],
                cw[:, d * 4 + jj: d * 4 + jj + 1],
                hcv[:, sft:T], ALU.mult, ALU.add),
                r=("l2x2", "l2cw", "l2hc"), w=("l2hc",), tag="l2cvj")
        k.op("scalar", lambda e: e.activation(x2t, hcv, AF.Silu),
             r=("l2hc",), w=("l2x2",), tag="l2silu")
        k.op("vector", lambda e, d=d: e.tensor_mul(xs(ug, d), x2t, xs(ug, d)),
             r=("l2x2", "l2u"), w=("l2u",), tag="l2umul")
    k.pop("l2front_done")
    if L2_STAGE < 3:
        return

    k.push()
    swb = k.carve("l2sw", [128, 8 * 320], BF16)
    k.dma("gpsimd", sb3(swb, 8), fm3(k.dram["l2_smallw"]), r=("l2sw",),
          w=("l2sw",), tag="l2swld")
    for tc in range(4):
        tsl = slice(tc * 512, (tc + 1) * 512)
        for d in range(NDT):
            k.mm(bank[0], swb[:, d * 320:d * 320 + 128], xs(ug, d, tsl),
                 d == 0, d == NDT - 1, r=("l2sw", "l2u"), w=("bank0",),
                 tag="l2fAm")
        k.op("vector", lambda e, tsl=tsl: e.tensor_copy(featsA[:, tsl], bank[0]),
             r=("bank0",), w=("bc_mu",), tag="l2fAe")
        for d in range(NDT):
            k.mm(bank[1][0:64, :], swb[:, d * 320 + 128:d * 320 + 192],
                 xs(ug, d, tsl), d == 0, d == NDT - 1,
                 r=("l2sw", "l2u"), w=("bank1",), tag="l2fBm")
        k.op("vector", lambda e: e.tensor_single_scalar(
            scr0[0:64, 0:512], bank[1][0:64, :], 0.0, ALU.is_gt),
            r=("bank1",), w=("scr0_0",), tag="l2bits")
        k.mm(bank[2][0:2, :], k.t["pow2"][0:18, 0:2], scr0[0:18, 0:512],
             True, True, r=("scr0_0", "pow2"), w=("bank2",), tag="l2ri")
        k.mm(bank[3][0:2, :], k.t["pow2"][32:50, 2:4], scr0[32:50, 0:512],
             True, True, r=("scr0_0", "pow2"), w=("bank3",), tag="l2wi")
        k.op("vector", lambda e: e.tensor_copy(scr0[0:2, 512:1024], bank[2][0:2, :]),
             r=("bank2",), w=("scr0_1",), tag="l2riE")
        k.op("vector", lambda e: e.tensor_copy(scr0[0:2, 1024:1536], bank[3][0:2, :]),
             r=("bank3",), w=("scr0_1",), tag="l2wiE")
        for c in range(4):
            ch = tc * 4 + c
            k.op("tensor", lambda e, c=c: e.transpose(
                bank[4][:, 0:2], scr0[0:2, 512 + c * 128:512 + (c + 1) * 128],
                identf[0:2, 0:2]),
                r=("scr0_1", "ident_f"), w=("bank4",), tag="l2riT")
            k.op("vector", lambda e, ch=ch: e.tensor_copy(
                rwidx[:, ch * 4:ch * 4 + 2], bank[4][:, 0:2]),
                r=("bank4",), w=("l2idx",), tag="l2riTe")
            k.op("tensor", lambda e, c=c: e.transpose(
                bank[4][:, 2:4], scr0[0:2, 1024 + c * 128:1024 + (c + 1) * 128],
                identf[0:2, 0:2]),
                r=("scr0_1", "ident_f"), w=("bank4",), tag="l2wiT")
            k.op("vector", lambda e, ch=ch: e.tensor_copy(
                rwidx[:, ch * 4 + 2:ch * 4 + 4], bank[4][:, 2:4]),
                r=("bank4",), w=("l2idx",), tag="l2wiTe")
        for d in range(NDT):
            k.mm(bank[5], swb[:, d * 320 + 192:d * 320 + 320], xs(ug, d, tsl),
                 d == 0, d == NDT - 1, r=("l2sw", "l2u"), w=("bank5",),
                 tag="l2fCm")
        k.op("vector", lambda e, tsl=tsl: e.tensor_copy(featsC[:, tsl], bank[5]),
             r=("bank5",), w=("bc_rs",), tag="l2fCe")
    fcb = k.carve("l2fcb", [128, 1], F32)
    k.dma("sync", fcb, k.dram["l2_fcb"], w=("l2fcb",))
    for base in (0, 32, 64):
        k.op("scalar", lambda e, base=base: e.activation(
            featsC[base:base + 1, :], featsC[base:base + 1, :], AF.Sigmoid,
            bias=fcb[base:base + 1, :]),
            r=("bc_rs", "l2fcb"), w=("bc_rs",), tag="l2fCsig")
    k.pop("l2feats_done")
    if L2_STAGE < 4:
        return

    k.push()
    glb = k.carve("l2glb", [128, T], BF16)
    for tc in range(4):
        tsl = slice(tc * 512, (tc + 1) * 512)
        k.mm(bank[6], k.orow(32), featsC[32:33, tsl], True, True,
             r=("bc_rs",), w=("bank6",), tag="l2glbm")
        k.op("vector", lambda e, tsl=tsl: e.tensor_copy(glb[:, tsl], bank[6]),
             r=("bank6",), w=("l2glb",), tag="l2glbe")
    sseq = k.carve("l2ss", [128, 4 * T], BF16)
    carry = k.carve("l2cy", [128, 32], F32)
    dky = scr0[:, 1536:2048]
    dcol = k.carve("l2dc", [128, 4], F32)
    sinb = [k.carve("l2si0", [128, 1024], BF16)]
    sob = [k.carve("l2so0", [128, 512], BF16)]
    k.op("gpsimd", lambda e: e.memset(carry, 0.0), w=("l2cy",), tag="l2cy0")
    dl = k.carve("l2dl", [128, 1], F32)
    k.dma("sync", dl, k.dram["l2_decay"], w=("l2dl",))
    k.op("scalar", lambda e: e.activation(dl[0:4, :], dl[0:4, :], AF.Sigmoid),
         r=("l2dl",), w=("l2dl",), tag="l2dsig")
    k.op("tensor", lambda e: e.transpose(bank[6][0:1, 0:4], dl[0:4, :],
                                         identf[0:4, 0:4]),
         r=("l2dl", "ident_f"), w=("bank6",), tag="l2dT")
    k.op("vector", lambda e: e.tensor_copy(dcol[0:1, :], bank[6][0:1, 0:4]),
         r=("bank6",), w=("l2dc",), tag="l2dTe")
    sin = k.dram["l2_sin_wt"]
    sout = k.dram["l2_sout_wt"]
    dkc = k.carve("l2dkc", [128, 1], F32)
    for cg in range(8):
        kk = cg // 2
        k.mm(bank[6][:, 0:1], k.orow(0), dcol[0:1, kk:kk + 1], True, True,
             r=("l2dc",), w=("bank6",), tag="l2dkb")
        k.op("vector", lambda e: e.tensor_copy(dkc, bank[6][:, 0:1]),
             r=("bank6",), w=("l2dkc",), tag="l2dkc")
        k.op("vector", lambda e: e.tensor_scalar(
            dky[:, 0:512], k.t["iota512"], 0.0, dkc, ALU.mult, ALU.add),
            r=("l2dkc", "iota512"), w=("scr0_3",), tag="l2dke")
        for c4 in range(4):
            ct = cg * 4 + c4
            wbuf = sinb[0]
            wbr = "l2si0"
            k.dma("gpsimd", sb3(wbuf, 8), fm3(sin[:, ct * 128:(ct + 1) * 128]),
                  r=(wbr,), w=(wbr,), tag="l2sild")
            for tc in range(4):
                tsl = slice(tc * 512, (tc + 1) * 512)
                bk = bank[tc % 2]
                bkr = f"bank{tc % 2}"
                for d in range(NDT):
                    k.mm(bk, wbuf[:, d * 128:(d + 1) * 128], xs(ug, d, tsl),
                         d == 0, d == NDT - 1, r=(wbr, "l2u"), w=(bkr,),
                         tag="l2simm")
                k.op("vector", lambda e, bk=bk, c4=c4, tc=tc, ct=ct: e.tensor_tensor_scan(
                    sseq[:, c4 * T + tc * 512: c4 * T + (tc + 1) * 512],
                    dky[:, 0:512], bk, carry[:, ct:ct + 1], ALU.mult, ALU.add),
                    r=(bkr, "scr0_3", "l2cy"), w=("l2ss",), tag="l2scan")
                k.op("vector", lambda e, c4=c4, tc=tc, ct=ct: e.tensor_copy(
                    carry[:, ct:ct + 1],
                    sseq[:, c4 * T + tc * 512 + 511: c4 * T + tc * 512 + 512]),
                    r=("l2ss",), w=("l2cy",), tag="l2carry")
        for fo in range(8):
            wbuf = sob[0]
            wbr = "l2so0"
            k.dma("gpsimd", sb3(wbuf, 4),
                  fm3(sout[cg * 512:(cg + 1) * 512, fo * 128:(fo + 1) * 128]),
                  r=(wbr,), w=(wbr,), tag="l2sold")
            for tc in range(4):
                tsl = slice(tc * 512, (tc + 1) * 512)
                bk = bank[4 + tc % 2]
                bkr = f"bank{4 + tc % 2}"
                for c4 in range(4):
                    k.mm(bk, wbuf[:, c4 * 128:(c4 + 1) * 128],
                         sseq[:, c4 * T + tc * 512: c4 * T + (tc + 1) * 512],
                         c4 == 0, c4 == 3, r=(wbr, "l2ss"), w=(bkr,),
                         tag="l2somm")
                k.op("vector", lambda e, bk=bk, tsl=tsl: e.tensor_mul(
                    scr0[:, 0:512], bk, glb[:, tsl]),
                    r=(bkr, "l2glb"), w=("scr0_0",), tag="l2lgl")
                k.op("vector", lambda e, fo=fo, tsl=tsl: e.tensor_add(
                    xs(xn, fo, tsl), xs(xn, fo, tsl), scr0[:, 0:512]),
                    r=("scr0_0", "xn"), w=("xn",), tag="l2ladd")
    k.pop("l2ssm_done")
    k.pop("l2u_done")
    k.dbg_ckpt("ssm")
    if L2_STAGE < 5:
        return

    emit_l2_memscan(k, featsA, featsC, rwidx)
    k.dbg_ckpt("mem")
    k.pop("l2attnA_done")
    if L2_STAGE < 6:
        return

    k.push()
    bc3 = k.carve("l2bc3", [128, T], F32)
    if L2_STAGE < 6.05:
        k.pop("dbg")
        return
    if L2_STAGE < 6.08:
        for nch in range(4):
            tsl = slice(nch * 512, (nch + 1) * 512)
            for d in range(NDT):
                sq = scr0[:, (d % 2) * 512:(d % 2) * 512 + 512]
                k.op("scalar", lambda e, d=d, tsl=tsl, sq=sq: e.activation(
                    sq, xs(xn, d, tsl), AF.Square), r=("xn",),
                    w=(f"scr0_{d % 2}",), tag="dbgsq")
        k.pop("dbg")
        return
    emit_rms_bcast(k, lambda d, tsl: (xs(xn, d, tsl), "xn"), NDT, 1e-6,
                   bc3, "l2bc3", "l2yrms_")
    if L2_STAGE < 6.2:
        k.pop("dbg")
        return
    ub = k.carve("l2ub", [128, 32], F32)
    db = k.carve("l2db", [128, 8], F32)
    k.dma("sync", ub, k.dram["l2_up_b"], w=("l2ub",))
    k.dma("sync", db, k.dram["l2_down_b"], w=("l2db",))
    hbuf = k.carve("l2mh", [128, 32 * 512], BF16)
    hw = [k.carve("l2mw0", [128, 1024], BF16), k.carve("l2mw1", [128, 1024], BF16)]
    upw = k.dram["l2_up_wt"]
    dnw = k.dram["l2_down_wt"]
    for tq in range(4):
        tsl = slice(tq * 512, (tq + 1) * 512)
        for fc in range(32):
            wbuf = hw[fc % 2]
            wbr = f"l2mw{fc % 2}"
            k.dma("gpsimd", sb3(wbuf, 8), fm3(upw[:, fc * 128:(fc + 1) * 128]),
                  r=(wbr,), w=(wbr,), tag="l2uld")
            bk = bank[fc % 2]
            bkr = f"bank{fc % 2}"
            for d in range(NDT):
                k.mm(bk, wbuf[:, d * 128:(d + 1) * 128], xs(xn, d, tsl),
                     d == 0, d == NDT - 1, r=(wbr, "xn"), w=(bkr,), tag="l2umm")
            k.op("vector", lambda e, bk=bk, tsl=tsl: e.tensor_mul(
                scr0[:, 0:512], bk, bc3[:, tsl]),
                r=(bkr, "l2bc3"), w=("scr0_0",), tag="l2urs")
            k.op("scalar", lambda e, fc=fc: e.activation(
                hbuf[:, fc * 512:(fc + 1) * 512], scr0[:, 0:512], AF.Silu,
                bias=ub[:, fc:fc + 1]),
                r=("scr0_0", "l2ub"), w=("l2mh",), tag="l2usilu")
        for fo in range(8):
            for hh in range(4):
                wbuf = hw[hh % 2]
                wbr = f"l2mw{hh % 2}"
                k.dma("gpsimd", sb3(wbuf, 8),
                      fm3(dnw[hh * 1024:(hh + 1) * 1024, fo * 128:(fo + 1) * 128]),
                      r=(wbr,), w=(wbr,), tag="l2dld")
                for d8 in range(8):
                    hc = hh * 8 + d8
                    k.mm(bank[2], wbuf[:, d8 * 128:(d8 + 1) * 128],
                         hbuf[:, hc * 512:(hc + 1) * 512],
                         hc == 0, hc == 31, r=(wbr, "l2mh"), w=("bank2",),
                         tag="l2dmm")
            k.op("vector", lambda e, fo=fo, tsl=tsl: e.scalar_tensor_tensor(
                xs(x, fo, tsl), bank[2], db[:, fo:fo + 1], xs(x, fo, tsl),
                ALU.add, ALU.add),
                r=("bank2", "l2db", "x"), w=("x",), tag="l2dadd")
    if L2_STAGE >= 6.5:
        for fo in range(NDT):
            k.op("vector", lambda e, fo=fo: e.tensor_add(
                xs(x, fo), xs(x, fo), xs(xn, fo)),
                r=("x", "xn"), w=("x",), tag="l2yadd")
    k.pop("l2mlp_done")


def emit_l2_memscan(k, featsA, featsC, rwidx):
    xn = k.t["xn"]
    bank = k.t["banks"]
    bank_b = k.t["bank_b"]
    scr0 = k.t["scr0"]
    row = k.t["row"]
    oc, ocb = k.t["ones_col"], k.t["ones_col_b"]
    identb, identf = k.t["ident_b"], k.t["ident_f"]
    SLb, SUb = k.t["SLb"], k.t["SUb"]
    iota512, iotac = k.t["iota512"], k.t["iotacol"]
    parcol = k.t["parcol"]

    k.push()
    Km = k.carve("msK", [128, 512], BF16)
    Vm = k.carve("msV", [128, 512], BF16)
    lastm = k.carve("msL", [128, 8], BF16)
    readsT = k.carve("msRd", [32, T], BF16)
    OHT = k.carve("msOHT", [128, 512], BF16)
    OHRT = k.carve("msORT", [128, 512], BF16)
    OH = k.carve("msOH", [128, 512], BF16)
    OHR = k.carve("msOR", [128, 512], BF16)
    OHTw = k.carve("msOTW", [128, 512], BF16)
    W2W = k.carve("msW2W", [128, 128], BF16)
    W2WU = k.carve("msW2U", [128, 128], BF16)
    R2 = k.carve("msR2", [128, 128], BF16)
    R2L = k.carve("msR2L", [128, 128], BF16)
    OHN = k.carve("msOHN", [128, 128], BF16)
    EX = k.carve("msEX", [128, 128], BF16)
    CF = k.carve("msCF", [128, 128], BF16)
    ktok = k.carve("msk", [128, 32], BF16)
    vtok = k.carve("msv", [128, 32], BF16)
    qtok = k.carve("msq", [128, 32], F32)
    cols = k.carve("mscol", [128, 16], F32)
    colsb = k.carve("mscbb", [128, 8], BF16)
    rv = k.carve("msrv", [128, 128], F32)
    sc = k.carve("mssc", [128, 8], F32)
    rk = k.carve("msrk", [128, 32], F32)
    tt = k.carve("mstt", [128, 32], F32)
    tt4 = k.carve("mst4", [128, 8], F32)
    gmb = k.carve("msgmb", [128, T], BF16)
    mow = k.carve("msmw", [32, 1024], BF16)

    CG, CL, CLS, CC0, CPI, CSU = 0, 1, 2, 3, 4, 8
    CLMS, CGMS, CWC = 9, 10, 11
    BS1, BS0, BLMS, BIL = 0, 1, 2, 5

    k.op("gpsimd", lambda e: e.memset(Km, 0.0), w=("msK",), tag="msK0")
    k.op("gpsimd", lambda e: e.memset(Vm, 0.0), w=("msV",), tag="msV0")
    k.op("gpsimd", lambda e: e.memset(lastm, -1.0), w=("msL",), tag="msL0")
    k.dma("gpsimd", mow, k.dram["l2_mow"], w=("msmw",), tag="msmwl")
    for tc in range(4):
        tsl = slice(tc * 512, (tc + 1) * 512)
        k.mm(bank[6], k.orow(64), featsC[64:65, tsl], True, True,
             r=("bc_rs",), w=("bank6",), tag="msgmbm")
        k.op("vector", lambda e, tsl=tsl: e.tensor_copy(gmb[:, tsl], bank[6]),
             r=("bank6",), w=("msgmb",), tag="msgmbe")

    for c in range(16):
        csl = slice(c * 128, (c + 1) * 128)
        for (p0, dst, dstres, f32out) in ((32, ktok, "msk", False),
                                          (64, vtok, "msv", False),
                                          (0, qtok, "msq", True)):
            k.op("tensor", lambda e, p0=p0, csl=csl: e.transpose(
                bank[4][:, 0:32], featsA[p0:p0 + 32, csl],
                identf[p0:p0 + 32, p0:p0 + 32]),
                r=("bc_mu", "ident_f"), w=("bank4",), tag="mskvT")
            if f32out:
                k.op("scalar", lambda e, dst=dst: e.activation(
                    dst, bank[4][:, 0:32], AF.Copy, scale=1.0 / np.sqrt(32.0)),
                    r=("bank4",), w=(dstres,), tag="msqE")
            else:
                k.op("vector", lambda e, dst=dst: e.tensor_copy(dst, bank[4][:, 0:32]),
                     r=("bank4",), w=(dstres,), tag="mskvE")
        k.op("tensor", lambda e, csl=csl: e.transpose(
            bank[4][:, 0:1], featsC[0:1, csl], identf[0:1, 0:1]),
            r=("bc_rs", "ident_f"), w=("bank4",), tag="msgT")
        k.op("vector", lambda e: e.tensor_copy(cols[:, CG:CG + 1], bank[4][:, 0:1]),
             r=("bank4",), w=("mscol",), tag="msgE")
        k.op("vector", lambda e: e.tensor_scalar(
            cols[:, CL:CL + 1], cols[:, CG:CG + 1], -1.0, 1.0, ALU.mult, ALU.add),
            r=("mscol",), w=("mscol",), tag="msL1")
        k.op("vector", lambda e: e.tensor_scalar_max(
            cols[:, CL:CL + 1], cols[:, CL:CL + 1], 1e-30),
            r=("mscol",), w=("mscol",), tag="msL2")
        k.op("scalar", lambda e: e.activation(
            cols[:, CL:CL + 1], cols[:, CL:CL + 1], AF.Ln),
            r=("mscol",), w=("mscol",), tag="msL3")

        for h in range(2):
            wcol = rwidx[:, c * 4 + 2 + h: c * 4 + 3 + h]
            rcol = rwidx[:, c * 4 + h: c * 4 + 1 + h]
            k.op("vector", lambda e, wcol=wcol: e.tensor_single_scalar(
                OHT, iota512, wcol, ALU.is_equal),
                r=("l2idx", "iota512"), w=("msOHT",), tag="msOHT")
            k.op("vector", lambda e, rcol=rcol: e.tensor_single_scalar(
                OHRT, iota512, rcol, ALU.is_equal),
                r=("l2idx", "iota512"), w=("msORT",), tag="msORT")
            for bt in range(4):
                k.op("tensor", lambda e, bt=bt: e.transpose(
                    bank_b[:, 0:128], OHT[:, bt * 128:(bt + 1) * 128], identb),
                    r=("msOHT", "ident_b"), w=("bank_b",), tag="msOHTt")
                k.op("vector", lambda e, bt=bt: e.tensor_copy(
                    OH[:, bt * 128:(bt + 1) * 128], bank_b[:, 0:128]),
                    r=("bank_b",), w=("msOH",), tag="msOHe")
                k.op("tensor", lambda e, bt=bt: e.transpose(
                    bank_b[:, 128:256], OHRT[:, bt * 128:(bt + 1) * 128], identb),
                    r=("msORT", "ident_b"), w=("bank_b",), tag="msORTt")
                k.op("vector", lambda e, bt=bt: e.tensor_copy(
                    OHR[:, bt * 128:(bt + 1) * 128], bank_b[:, 128:256]),
                    r=("bank_b",), w=("msOR",), tag="msORe")
            for bt in range(4):
                k.mm(bank[0][:, 0:128], OH[:, bt * 128:(bt + 1) * 128],
                     OH[:, bt * 128:(bt + 1) * 128], bt == 0, bt == 3,
                     r=("msOH",), w=("bank0",), tag="msM")
            k.op("vector", lambda e: e.tensor_mul(W2W, bank[0][:, 0:128], SLb),
                 r=("bank0", "SLb"), w=("msW2W",), tag="msW2W")
            k.op("vector", lambda e: e.tensor_mul(W2WU, bank[0][:, 0:128], SUb),
                 r=("bank0", "SUb"), w=("msW2U",), tag="msW2U")
            for bt in range(4):
                k.mm(bank[1][:, 0:128], OH[:, bt * 128:(bt + 1) * 128],
                     OHR[:, bt * 128:(bt + 1) * 128], bt == 0, bt == 3,
                     r=("msOH", "msOR"), w=("bank1",), tag="msR2m")
            k.op("vector", lambda e: e.tensor_mul(R2, bank[1][:, 0:128], SLb),
                 r=("bank1", "SLb"), w=("msR2",), tag="msR2")
            k.mm(bank[2][0:1, 0:128], ocb, W2W, True, True,
                 r=("msW2W",), w=("bank2",), tag="msn")
            k.op("vector", lambda e: e.tensor_copy(row[:, 0:128], bank[2][0:1, 0:128]),
                 r=("bank2",), w=("row",), tag="msnE")
            k.mm(bank[3][:, 0:128], k.orow(0), row[:, 0:128], True, True,
                 r=("row",), w=("bank3",), tag="msnB")
            k.op("vector", lambda e: e.tensor_single_scalar(
                OHN, bank[3][:, 0:128], iotac, ALU.is_equal),
                r=("bank3", "iotacol"), w=("msOHN",), tag="msOHN")
            k.mm(bank[2][0:1, 0:128], parcol, OHN, True, True,
                 r=("msOHN", "parcol"), w=("bank2",), tag="mspn")
            for bt in range(4):
                k.mm(bank[4][0:1, 0:128], lastm[:, h * 4 + bt: h * 4 + bt + 1],
                     OH[:, bt * 128:(bt + 1) * 128], bt == 0, bt == 3,
                     r=("msL", "msOH"), w=("bank4",), tag="msbl")
            k.op("vector", lambda e: e.tensor_single_scalar(
                row[:, 128:256], bank[4][0:1, 0:128], 0.0, ALU.is_equal),
                r=("bank4",), w=("row",), tag="mspb")
            k.op("vector", lambda e: e.tensor_tensor(
                row[:, 256:384], row[:, 128:256], bank[2][0:1, 0:128], ALU.mult),
                r=("row", "bank2"), w=("row",), tag="mspp")
            k.op("vector", lambda e: e.tensor_tensor(
                row[:, 384:512], row[:, 128:256], bank[2][0:1, 0:128], ALU.add),
                r=("row", "bank2"), w=("row",), tag="msps")
            k.op("vector", lambda e: e.scalar_tensor_tensor(
                row[:, 0:128], row[:, 256:384], -2.0, row[:, 384:512],
                ALU.mult, ALU.add),
                r=("row",), w=("row",), tag="msslot")
            k.op("tensor", lambda e: e.transpose(
                bank[4][:, 0:1], row[:, 0:128], identf[0:1, 0:1]),
                r=("row", "ident_f"), w=("bank4",), tag="msslT")
            k.op("vector", lambda e: e.tensor_copy(cols[:, CLS:CLS + 1],
                                                   bank[4][:, 0:1]),
                 r=("bank4",), w=("mscol",), tag="msslE")
            k.op("vector", lambda e: e.tensor_copy(colsb[:, BS1:BS1 + 1],
                                                   cols[:, CLS:CLS + 1]),
                 r=("mscol",), w=("mscbb",), tag="mss1")
            k.op("vector", lambda e: e.tensor_scalar(
                colsb[:, BS0:BS0 + 1], cols[:, CLS:CLS + 1], -1.0, 1.0,
                ALU.mult, ALU.add),
                r=("mscol",), w=("mscbb",), tag="mss0")

            for s in range(2):
                smask = colsb[:, (BS1 if s == 1 else BS0):(BS1 if s == 1 else BS0) + 1]
                off = h * 256 + s * 128
                k.op("vector", lambda e, smask=smask: e.tensor_tensor(
                    cols[:, CLMS:CLMS + 1], cols[:, CL:CL + 1], smask, ALU.mult),
                    r=("mscol", "mscbb"), w=("mscol",), tag="mslms")
                k.op("vector", lambda e: e.tensor_copy(
                    colsb[:, BLMS:BLMS + 1], cols[:, CLMS:CLMS + 1]),
                    r=("mscol",), w=("mscbb",), tag="mslmsb")
                k.op("vector", lambda e, smask=smask: e.tensor_tensor(
                    cols[:, CGMS:CGMS + 1], cols[:, CG:CG + 1], smask, ALU.mult),
                    r=("mscol", "mscbb"), w=("mscol",), tag="msgms")
                k.op("vector", lambda e: e.tensor_single_scalar(
                    R2L, R2, cols[:, CLMS:CLMS + 1], ALU.mult),
                    r=("msR2", "mscol"), w=("msR2L",), tag="msR2L")
                k.mm(bank[5][:, 0:1], R2, colsb[:, BLMS:BLMS + 1], True, True,
                     r=("msR2", "mscbb"), w=("bank5",), tag="msc0")
                k.op("scalar", lambda e: e.activation(
                    cols[:, CC0:CC0 + 1], bank[5][:, 0:1], AF.Exp),
                    r=("bank5",), w=("mscol",), tag="msc0e")
                k.mm(bank[3][:, 0:128], SUb, R2L, True, True,
                     r=("msR2L", "SUb"), w=("bank3",), tag="msSsuf")
                k.op("scalar", lambda e: e.activation(EX, bank[3][:, 0:128], AF.Exp),
                     r=("bank3",), w=("msEX",), tag="msEXe")
                k.op("vector", lambda e: e.tensor_single_scalar(
                    CF, R2, cols[:, CGMS:CGMS + 1], ALU.mult),
                    r=("msR2", "mscol"), w=("msCF",), tag="msCF1")
                k.op("vector", lambda e: e.tensor_mul(CF, CF, EX),
                     r=("msCF", "msEX"), w=("msCF",), tag="msCF2")
                for (tokb, mem, memres, outdst, outres) in (
                        (ktok, Km, "msK", rk, "msrk"),
                        (vtok, Vm, "msV", None, None)):
                    k.mm(bank[2][:, 0:32], CF, tokb, True, True,
                         r=("msCF", "msk" if tokb is ktok else "msv"),
                         w=("bank2",), tag="msrx")
                    for bt in range(4):
                        k.mm(bank[4][:, 0:32], OHR[:, bt * 128:(bt + 1) * 128],
                             mem[:, off + bt * 32: off + (bt + 1) * 32],
                             bt == 0, bt == 3, r=("msOR", memres),
                             w=("bank4",), tag="msbx")
                    k.op("vector", lambda e: e.tensor_single_scalar(
                        tt, bank[4][:, 0:32], cols[:, CC0:CC0 + 1], ALU.mult),
                        r=("bank4", "mscol"), w=("mstt",), tag="msbc")
                    if outdst is not None:
                        k.op("vector", lambda e, outdst=outdst: e.tensor_tensor(
                            outdst, tt, bank[2][:, 0:32], ALU.add),
                            r=("mstt", "bank2"), w=(outres,), tag="msrkE")
                    else:
                        sl_rv = slice((h * 2 + s) * 32, (h * 2 + s + 1) * 32)
                        k.op("vector", lambda e, sl_rv=sl_rv: e.tensor_tensor(
                            rv[:, sl_rv], tt, bank[2][:, 0:32], ALU.add),
                            r=("mstt", "bank2"), w=("msrv",), tag="msrvE")
                k.op("vector", lambda e: e.tensor_mul(tt, rk, qtok),
                     r=("msrk", "msq"), w=("mstt",), tag="mssc1")
                k.op("vector", lambda e, h=h, s=s: e.reduce_sum(
                    sc[:, h * 2 + s: h * 2 + s + 1], tt, axis=AX.X),
                    r=("mstt",), w=("mssc",), tag="mssc2")
                for bt in range(4):
                    k.mm(bank[6][:, bt:bt + 1], OHT[:, bt * 128:(bt + 1) * 128],
                         colsb[:, BLMS:BLMS + 1], True, True,
                         r=("msOHT", "mscbb"), w=("bank6",), tag="msPi")
                k.op("scalar", lambda e: e.activation(
                    cols[:, CPI:CPI + 4], bank[6][:, 0:4], AF.Exp),
                    r=("bank6",), w=("mscol",), tag="msPie")
                k.mm(bank[5][:, 0:1], W2WU, colsb[:, BLMS:BLMS + 1], True, True,
                     r=("msW2U", "mscbb"), w=("bank5",), tag="msSa")
                k.op("scalar", lambda e: e.activation(
                    cols[:, CSU:CSU + 1], bank[5][:, 0:1], AF.Exp),
                    r=("bank5",), w=("mscol",), tag="msSae")
                k.op("vector", lambda e: e.tensor_tensor(
                    cols[:, CWC:CWC + 1], cols[:, CSU:CSU + 1],
                    cols[:, CGMS:CGMS + 1], ALU.mult),
                    r=("mscol",), w=("mscol",), tag="mswc")
                k.op("vector", lambda e: e.tensor_single_scalar(
                    OHTw, OHT, cols[:, CWC:CWC + 1], ALU.mult),
                    r=("msOHT", "mscol"), w=("msOTW",), tag="msOTW")
                for (tokb, tokres, mem, memres) in ((ktok, "msk", Km, "msK"),
                                                    (vtok, "msv", Vm, "msV")):
                    for bt in range(4):
                        k.mm(bank[2][:, 0:32], OHTw[:, bt * 128:(bt + 1) * 128],
                             tokb, True, True, r=("msOTW", tokres),
                             w=("bank2",), tag="msDx")
                        k.op("vector", lambda e, mem=mem, bt=bt, off=off: e.scalar_tensor_tensor(
                            mem[:, off + bt * 32: off + (bt + 1) * 32],
                            mem[:, off + bt * 32: off + (bt + 1) * 32],
                            cols[:, CPI + bt: CPI + bt + 1], bank[2][:, 0:32],
                            ALU.mult, ALU.add),
                            r=(memres, "mscol", "bank2"), w=(memres,), tag="msKu")
            k.mm(bank[5][:, 0:1], W2WU, ocb, True, True,
                 r=("msW2U",), w=("bank5",), tag="mssu")
            k.op("vector", lambda e: e.tensor_single_scalar(
                tt4[:, 0:1], bank[5][:, 0:1], 0.5, ALU.is_lt),
                r=("bank5",), w=("mst4",), tag="msil")
            k.op("vector", lambda e: e.tensor_tensor(
                colsb[:, BIL:BIL + 1], tt4[:, 0:1], cols[:, CLS:CLS + 1], ALU.mult),
                r=("mst4", "mscol"), w=("mscbb",), tag="msilb")
            for bt in range(4):
                k.mm(bank[6][:, bt:bt + 1], OHT[:, bt * 128:(bt + 1) * 128],
                     ocb, True, True, r=("msOHT",), w=("bank6",), tag="mscnt")
            k.op("vector", lambda e: e.tensor_scalar_min(
                tt4[:, 4:8], bank[6][:, 0:4], 1.0),
                r=("bank6",), w=("mst4",), tag="msanyw")
            for bt in range(4):
                k.mm(bank[3][:, bt:bt + 1], OHT[:, bt * 128:(bt + 1) * 128],
                     colsb[:, BIL:BIL + 1], True, True,
                     r=("msOHT", "mscbb"), w=("bank3",), tag="mslv")
            k.op("vector", lambda e, h=h: e.tensor_tensor(
                tt4[:, 0:4], bank[3][:, 0:4], lastm[:, h * 4:h * 4 + 4],
                ALU.subtract),
                r=("bank3", "msL"), w=("mst4",), tag="msld")
            k.op("vector", lambda e: e.tensor_mul(tt4[:, 0:4], tt4[:, 0:4],
                                                  tt4[:, 4:8]),
                 r=("mst4",), w=("mst4",), tag="msla")
            k.op("vector", lambda e, h=h: e.tensor_tensor(
                lastm[:, h * 4:h * 4 + 4], lastm[:, h * 4:h * 4 + 4],
                tt4[:, 0:4], ALU.add),
                r=("msL", "mst4"), w=("msL",), tag="mslu")
        k.op("vector", lambda e: e.reduce_max(tt4[:, 0:1], sc[:, 0:4], axis=AX.X),
             r=("mssc",), w=("mst4",), tag="msmx")
        k.op("vector", lambda e: e.tensor_single_scalar(
            sc[:, 0:4], sc[:, 0:4], tt4[:, 0:1], ALU.subtract),
            r=("mssc", "mst4"), w=("mssc",), tag="msms")
        k.op("scalar", lambda e: e.activation(sc[:, 0:4], sc[:, 0:4], AF.Exp),
             r=("mssc",), w=("mssc",), tag="msme")
        k.op("vector", lambda e: e.reduce_sum(tt4[:, 0:1], sc[:, 0:4], axis=AX.X),
             r=("mssc",), w=("mst4",), tag="msmsum")
        k.op("vector", lambda e: e.reciprocal(tt4[:, 0:1], tt4[:, 0:1]),
             r=("mst4",), w=("mst4",), tag="msmz")
        k.op("vector", lambda e: e.tensor_single_scalar(
            sc[:, 0:4], sc[:, 0:4], tt4[:, 0:1], ALU.mult),
            r=("mssc", "mst4"), w=("mssc",), tag="msmn")
        k.op("vector", lambda e: e.tensor_single_scalar(
            rk, rv[:, 0:32], sc[:, 0:1], ALU.mult),
            r=("msrv", "mssc"), w=("msrk",), tag="msrd0")
        for a in range(1, 4):
            k.op("vector", lambda e, a=a: e.scalar_tensor_tensor(
                rk, rv[:, a * 32:(a + 1) * 32], sc[:, a:a + 1], rk,
                ALU.mult, ALU.add),
                r=("msrv", "mssc", "msrk"), w=("msrk",), tag="msrdA")
        k.op("tensor", lambda e, csl=csl: e.transpose(
            bank[4][0:32, 0:128], rk, identf),
            r=("msrk", "ident_f"), w=("bank4",), tag="msrdT")
        k.op("vector", lambda e, csl=csl: e.tensor_copy(
            readsT[:, csl], bank[4][0:32, 0:128]),
            r=("bank4",), w=("msRd",), tag="msrdE")

    for fo in range(8):
        for tc in range(4):
            tsl = slice(tc * 512, (tc + 1) * 512)
            k.mm(bank[2], mow[:, fo * 128:(fo + 1) * 128], readsT[:, tsl],
                 True, True, r=("msmw", "msRd"), w=("bank2",), tag="msom")
            k.op("vector", lambda e, tsl=tsl: e.tensor_mul(
                scr0[:, 0:512], bank[2], gmb[:, tsl]),
                r=("bank2", "msgmb"), w=("scr0_0",), tag="msogm")
            k.op("vector", lambda e, fo=fo, tsl=tsl: e.tensor_add(
                xs(xn, fo, tsl), xs(xn, fo, tsl), scr0[:, 0:512]),
                r=("scr0_0", "xn"), w=("xn",), tag="msoadd")
    k.pop("l2ms_done")


L3_PARTS = "af"


def emit_layer3(k):
    emit_ln(k, "l3ln_")
    if "f" in L3_PARTS:
        emit_l3_ffn(k)
    if "a" in L3_PARTS:
        emit_l3_attn(k)


def emit_l3_ffn(k):
    x, xn = k.t["x"], k.t["xn"]
    bank = k.t["banks"]

    k.push()
    bd = k.carve("r_bd", [128, 32 * 512], BF16)
    sel = k.t["c_sel"]

    k.dma("gpsimd", k.dram["wc_bf"], k.dram["l3_wct"], w=("wc_bf",), tag="r_pre")

    k.push()
    rmts = k.carve("r_rm", [16, 1024], BF16)
    rupts = k.carve("r_ru", [16, 4096], BF16)
    stg = k.carve("r_st", [16, 512], BF16)
    for h in range(4):
        k.dma("gpsimd", rmts, k.dram["l3_rmt"][h], r=("r_rm",), w=("r_rm",),
              tag="r_rml")
        k.dma("gpsimd", rupts, k.dram["l3_rut"][h], r=("r_ru",), w=("r_ru",),
              tag="r_rul")
        for dt in range(8):
            for r8 in range(8):
                r = dt * 8 + r8
                k.mm(bank[0][0:16, r8 * 64:(r8 + 1) * 64],
                     rmts[:, r * 16:(r + 1) * 16],
                     rupts[:, r * 64:(r + 1) * 64], True, True,
                     r=("r_rm", "r_ru"), w=("bank0",), tag="r_cmb")
            k.op("vector", lambda e: e.tensor_copy(stg, bank[0][0:16, 0:512]),
                 r=("bank0",), w=("r_st",), tag="r_stge")
            for r8 in range(8):
                k.mm(bank[1][:, r8 * 64:(r8 + 1) * 64],
                     sel[:, r8 * 128:(r8 + 1) * 128],
                     stg[:, r8 * 64:(r8 + 1) * 64], True, True,
                     r=("r_st", "c_sel"), w=("bank1",), tag="r_sct")
            k.op("vector", lambda e, h=h, dt=dt: e.tensor_copy(
                bd[:, (h * 8 + dt) * 512:(h * 8 + dt + 1) * 512], bank[1]),
                r=("bank1",), w=("r_bd",), tag="r_bde")

    k.pop("r_bd_done")
    hz = k.carve("r_hz", [128, 32 * 512], BF16)
    wcb = [k.carve("r_w0", [128, 1024], BF16), k.carve("r_w1", [128, 1024], BF16)]
    wc = k.dram["wc_bf"]
    for tc in range(4):
        tsl = slice(tc * 512, (tc + 1) * 512)
        for h in range(4):
            for dt in range(8):
                for mc in range(4):
                    ktl = dt * 4 + mc
                    k.mm(bank[2],
                         bd[:, (h * 8 + dt) * 512 + mc * 128:
                            (h * 8 + dt) * 512 + (mc + 1) * 128],
                         xs(xn, dt, tsl), True, True,
                         r=("r_bd", "xn"), w=("bank2",), tag="r_zmm")
                    k.op("scalar", lambda e, ktl=ktl: e.activation(
                        hz[:, ktl * 512:(ktl + 1) * 512], bank[2], AF.Gelu),
                        r=("bank2",), w=("r_hz",), tag="r_zg")
            for kt in range(32):
                wbuf = wcb[kt % 2]
                wbr = f"r_w{kt % 2}"
                gkt = h * 32 + kt
                k.dma("sync", wbuf, wc[gkt * 128:(gkt + 1) * 128, :],
                      r=(wbr,), w=(wbr,), tag="r_wld")
                for fo in range(8):
                    if fo < 7:
                        bko, bkor = bank[fo], f"bank{fo}"
                    else:
                        bko, bkor = k.t["bank_b"].bitcast(F32), "bank_b"
                    k.mm(bko[:, 0:512], wbuf[:, fo * 128:(fo + 1) * 128],
                         hz[:, kt * 512:(kt + 1) * 512],
                         kt == 0, kt == 31, r=(wbr, "r_hz"),
                         w=(bkor,), tag="r_cmm")
            for fo in range(8):
                if fo < 7:
                    bko, bkor = bank[fo], f"bank{fo}"
                else:
                    bko, bkor = k.t["bank_b"].bitcast(F32), "bank_b"
                k.op("vector", lambda e, fo=fo, tsl=tsl, bko=bko: e.tensor_add(
                    xs(x, fo, tsl), xs(x, fo, tsl), bko[:, 0:512]),
                    r=(bkor, "x"), w=("x",), tag="r_cadd")
    k.pop("r_done")


def emit_l3_attn(k):
    x, xn = k.t["x"], k.t["xn"]
    bank = k.t["banks"]
    bank_b = k.t["bank_b"]
    scr0 = k.t["scr0"]
    identb = k.t["ident_b"]
    caus01 = k.t["causal01"]
    oc = k.t["ones_col"]
    bc_mu = k.t["bc_mu"]
    row = k.t["row"]

    k.push()
    oa = k.carve("g_oa", [128, NDT * T], BF16)
    qh = k.carve("g_qh", [128, T], BF16)
    kh = k.carve("g_kh2", [128, T], BF16)
    vh = k.carve("g_vh", [128, 2 * T], BF16)
    gh = k.carve("g_gh", [128, T], BF16)
    gth = k.carve("g_gth", [128, 2 * T], BF16)
    t1b = k.carve("g_t1", [16, T], BF16)
    S = k.carve("g_S", [128, 256], F32)
    Sb = k.carve("g_Sb", [128, 256], BF16)
    qtb = k.carve("g_qt", [128, 64], BF16)
    ktb = k.carve("g_ktl", [128, 64], BF16)
    amb = k.carve("g_am", [64, 64], BF16)
    vtk = k.carve("g_vt", [64, 256], BF16)
    kht = k.carve("g_khx", [128, 64], BF16)
    khtk = k.carve("g_khT", [64, 128], BF16)
    eecol = k.carve("g_ee", [128, 1], F32)
    zs64 = k.carve("g_zs", [128, 64], F32)
    wvb = [k.carve("g_wv0", [128, 1024], BF16), k.carve("g_wv1", [128, 1024], BF16)]
    gk1 = k.carve("g_k1", [128, 128], BF16)
    gk2 = k.carve("g_k2", [16, 512], BF16)
    bg2 = k.carve("g_b2", [128, 4], F32)
    rmw = k.carve("g_rw", [128, 2], F32)

    k.dma("gpsimd", sb3(gk1, 8), fm3(k.dram["l3_wgk1"]), w=("g_k1",), tag="g_k1l")
    k.dma("gpsimd", gk2, k.dram["l3_wgk2"], w=("g_k2",), tag="g_k2l")
    k.dma("sync", bg2, k.dram["l3_bgk2"], w=("g_b2",), tag="g_b2l")
    k.dma("sync", rmw, k.dram["l3_rmsw"], w=("g_rw",), tag="g_rwl")
    k.op("gpsimd", lambda e: e.memset(zs64, 0.0), w=("g_zs",), tag="g_z0")

    for tc2 in range(4):
        tsl = slice(tc2 * 512, (tc2 + 1) * 512)
        for d in range(NDT):
            k.mm(bank[3][0:16, :], gk1[:, d * 16:(d + 1) * 16],
                 xs(xn, d, tsl), d == 0, d == NDT - 1,
                 r=("g_k1", "xn"), w=("bank3",), tag="g_t1m")
        k.op("vector", lambda e, tsl=tsl: e.tensor_copy(
            t1b[:, tsl], bank[3][0:16, :]),
            r=("bank3",), w=("g_t1",), tag="g_t1e")

    for h in range(4):
        projs = ((k.dram["l3_wq"][:, h * 128:(h + 1) * 128], qh, "g_qh", 1, None),
                 (k.dram["l3_wk"][:, h * 128:(h + 1) * 128], kh, "g_kh2", 1, None),
                 (k.dram["l3_wv"][:, h * 256:(h + 1) * 256], vh, "g_vh", 2, None),
                 (k.dram["l3_wg"][:, h * 256:(h + 1) * 256], gth, "g_gth", 2, AF.Silu))
        for (src, dst, dres, nfc, act) in projs:
            for fc in range(nfc):
                wbuf = wvb[fc % 2]
                wbr = f"g_wv{fc % 2}"
                k.dma("gpsimd", sb3(wbuf, 8),
                      fm3(src[:, fc * 128:(fc + 1) * 128]),
                      r=(wbr,), w=(wbr,), tag="g_wpl")
                for tc2 in range(4):
                    tsl = slice(tc2 * 512, (tc2 + 1) * 512)
                    bk = bank[tc2 % 2]
                    bkr = f"bank{tc2 % 2}"
                    for d in range(NDT):
                        k.mm(bk, wbuf[:, d * 128:(d + 1) * 128],
                             xs(xn, d, tsl), d == 0, d == NDT - 1,
                             r=(wbr, "xn"), w=(bkr,), tag="g_pm")
                    if act is None:
                        k.op("scalar", lambda e, bk=bk, dst=dst, fc=fc, tsl=tsl: e.copy(
                            dst[:, fc * T + tsl.start: fc * T + tsl.stop], bk),
                            r=(bkr,), w=(dres,), tag="g_pe")
                    else:
                        k.op("scalar", lambda e, bk=bk, dst=dst, fc=fc, tsl=tsl,
                             act=act: e.activation(
                            dst[:, fc * T + tsl.start: fc * T + tsl.stop], bk, act),
                            r=(bkr,), w=(dres,), tag="g_pea")
        for tc2 in range(4):
            tsl = slice(tc2 * 512, (tc2 + 1) * 512)
            k.mm(bank[4], gk2[:, h * 128:(h + 1) * 128], t1b[:, tsl],
                 True, True, r=("g_k2", "g_t1"), w=("bank4",), tag="g_t2m")
            sA = scr0[:, 0:512]
            sB = scr0[:, 512:1024]
            sC = scr0[:, 1024:1536]
            k.op("vector", lambda e, h=h: e.tensor_single_scalar(
                sA, bank[4], bg2[:, h:h + 1], ALU.add),
                r=("bank4", "g_b2"), w=("scr0_0",), tag="g_t2b")
            k.op("scalar", lambda e: e.activation(sB, sA, AF.Abs),
                 r=("scr0_0",), w=("scr0_1",), tag="g_abs")
            k.op("scalar", lambda e: e.activation(sB, sB, AF.Exp, scale=-1.0),
                 r=("scr0_1",), w=("scr0_1",), tag="g_expx")
            k.op("scalar", lambda e: e.activation(sB, sB, AF.Ln,
                                                  bias=k.kcol(1.0)),
                 r=("scr0_1",), w=("scr0_1",), tag="g_lnx")
            k.op("vector", lambda e: e.tensor_scalar_min(sC, sA, 0.0),
                 r=("scr0_0",), w=("scr0_2",), tag="g_minx")
            k.op("vector", lambda e: e.tensor_sub(sC, sC, sB),
                 r=("scr0_2", "scr0_1"), w=("scr0_2",), tag="g_subx")
            k.op("vector", lambda e, tsl=tsl: e.tensor_scalar(
                gh[:, tsl], sC, 1.0 / 16.0, -3.0, ALU.mult, ALU.max),
                r=("scr0_2",), w=("g_gh",), tag="g_gfin")
        k.op("gpsimd", lambda e: e.memset(S, 0.0), r=("g_S",), w=("g_S",),
             tag="g_S0")
        k.op("gpsimd", lambda e: e.memset(Sb, 0.0), r=("g_Sb",), w=("g_Sb",),
             tag="g_Sb0")
        for c in range(32):
            csl = slice(c * 64, (c + 1) * 64)
            gc = scr0[:, 0:64]
            egs = scr0[:, 64:128]
            ens = scr0[:, 128:192]
            k.op("vector", lambda e, csl=csl: e.tensor_tensor_scan(
                gc, gh[:, csl], zs64, 0.0, ALU.add, ALU.add),
                r=("g_gh", "g_zs"), w=("scr0_0",), tag="g_cum")
            k.op("scalar", lambda e: e.activation(egs, gc, AF.Exp),
                 r=("scr0_0",), w=("scr0_1",), tag="g_eg")
            k.op("vector", lambda e: e.tensor_scalar(
                ens, gc, -1.0, 60.0, ALU.mult, ALU.min),
                r=("scr0_0",), w=("scr0_2",), tag="g_en1")
            k.op("scalar", lambda e: e.activation(ens, ens, AF.Exp),
                 r=("scr0_2",), w=("scr0_2",), tag="g_en2")
            k.op("vector", lambda e, csl=csl: e.tensor_mul(qtb, qh[:, csl], egs),
                 r=("g_qh", "scr0_1"), w=("g_qt",), tag="g_qtm")
            k.op("vector", lambda e, csl=csl: e.tensor_mul(ktb, kh[:, csl], ens),
                 r=("g_kh2", "scr0_2"), w=("g_ktl",), tag="g_ktm")
            k.mm(bank[2][0:64, 0:64], ktb, qtb, True, True,
                 r=("g_qt", "g_ktl"), w=("bank2",), tag="g_amm")
            k.op("vector", lambda e: e.tensor_mul(
                amb, bank[2][0:64, 0:64], caus01[0:64, 0:64]),
                r=("bank2",), w=("g_am",), tag="g_amsk")
            for dvc in range(2):
                k.op("tensor", lambda e, dvc=dvc, csl=csl: e.transpose(
                    bank_b[0:64, 0:128],
                    vh[:, dvc * T + csl.start: dvc * T + csl.stop], identb),
                    r=("g_vh", "ident_b"), w=("bank_b",), tag="g_vT")
                k.op("vector", lambda e, dvc=dvc: e.tensor_copy(
                    vtk[0:64, dvc * 128:(dvc + 1) * 128], bank_b[0:64, 0:128]),
                    r=("bank_b",), w=("g_vt",), tag="g_vTe")
            for dvc in range(2):
                bko = bank[4 + dvc]
                bkor = f"bank{4 + dvc}"
                k.mm(bko[:, 0:64], Sb[:, dvc * 128:(dvc + 1) * 128], qtb,
                     True, False, r=("g_Sb", "g_qt"), w=(bkor,), tag="g_oin")
                k.mm(bko[:, 0:64], vtk[0:64, dvc * 128:(dvc + 1) * 128], amb,
                     False, True, r=("g_vt", "g_am"), w=(bkor,), tag="g_oia")
                k.op("scalar", lambda e, bko=bko, dvc=dvc, h=h, csl=csl: e.copy(
                    xs(oa, 2 * h + dvc, csl), bko[:, 0:64]),
                    r=(bkor,), w=("g_oa",), tag="g_oe")
            k.op("scalar", lambda e: e.activation(eecol, gc[:, 63:64], AF.Exp),
                 r=("scr0_0",), w=("g_ee",), tag="g_eec")
            k.op("vector", lambda e: e.tensor_single_scalar(
                kht, ktb, eecol, ALU.mult),
                r=("g_ktl", "g_ee"), w=("g_khx",), tag="g_khm")
            k.op("tensor", lambda e: e.transpose(bank_b[0:64, 128:256], kht,
                                                 identb),
                 r=("g_khx", "ident_b"), w=("bank_b",), tag="g_khT")
            k.op("vector", lambda e: e.tensor_copy(khtk, bank_b[0:64, 128:256]),
                 r=("bank_b",), w=("g_khT2",), tag="g_khTe")
            k.mm(bank[6][:, 0:256], khtk, vtk[0:64, :], True, True,
                 r=("g_khT2", "g_vt"), w=("bank6",), tag="g_dS")
            k.op("vector", lambda e: e.scalar_tensor_tensor(
                S, S, eecol, bank[6][:, 0:256], ALU.mult, ALU.add),
                r=("g_S", "g_ee", "bank6"), w=("g_S",), tag="g_Sup")
            k.op("vector", lambda e: e.tensor_copy(Sb, S),
                 r=("g_S",), w=("g_Sb",), tag="g_Sbc")
        for nch in range(4):
            tsl = slice(nch * 512, (nch + 1) * 512)
            for dvc in range(2):
                sq = scr0[:, (dvc % 2) * 512:(dvc % 2) * 512 + 512]
                k.op("scalar", lambda e, h=h, dvc=dvc, tsl=tsl, sq=sq: e.activation(
                    sq, xs(oa, 2 * h + dvc, tsl), AF.Square),
                    r=("g_oa",), w=(f"scr0_{dvc % 2}",), tag="g_rsq")
                k.mm(bank[5][0:1, :], oc, sq, dvc == 0, dvc == 1,
                     r=(f"scr0_{dvc % 2}",), w=("bank5",), tag="g_rssq")
            k.op("scalar", lambda e, tsl=tsl: e.activation(
                row[:, tsl], bank[5][0:1, :], AF.Copy, scale=1.0 / 256),
                r=("bank5",), w=("row",), tag="g_rmsc")
        k.op("scalar", lambda e: e.activation(row, row, AF.Ln,
                                              bias=k.kcol(1e-5, 1)),
             r=("row",), w=("row",), tag="g_rln")
        k.op("scalar", lambda e: e.activation(row, row, AF.Exp, scale=-0.5),
             r=("row",), w=("row",), tag="g_rex")
        for nch in range(4):
            tsl = slice(nch * 512, (nch + 1) * 512)
            k.mm(bank[5], k.orow(0), row[:, tsl], True, True,
                 r=("row",), w=("bank5",), tag="g_rbm")
            k.op("vector", lambda e, tsl=tsl: e.tensor_copy(bc_mu[:, tsl],
                                                            bank[5]),
                 r=("bank5",), w=("bc_mu",), tag="g_rbe")
        for dvc in range(2):
            dv = 2 * h + dvc
            k.op("vector", lambda e, dv=dv: e.tensor_mul(
                scr0[:, 0:T], xs(oa, dv), bc_mu),
                r=("g_oa", "bc_mu"),
                w=("scr0_0", "scr0_1", "scr0_2", "scr0_3"), tag="g_on1")
            k.op("vector", lambda e, dvc=dvc: e.tensor_single_scalar(
                scr0[:, 0:T], scr0[:, 0:T], rmw[:, dvc:dvc + 1], ALU.mult),
                r=("scr0_0", "scr0_1", "scr0_2", "scr0_3", "g_rw"),
                w=("scr0_0", "scr0_1", "scr0_2", "scr0_3"), tag="g_on2")
            k.op("vector", lambda e, dv=dv, dvc=dvc: e.tensor_mul(
                xs(oa, dv), scr0[:, 0:T], gth[:, dvc * T:(dvc + 1) * T]),
                r=("scr0_0", "scr0_1", "scr0_2", "scr0_3", "g_gth"),
                w=("g_oa",), tag="g_on3")

    for fo in range(8):
        wbuf = wvb[fo % 2]
        wbr = f"g_wv{fo % 2}"
        k.dma("gpsimd", sb3(wbuf, 8),
              fm3(k.dram["l3_wout"][:, fo * 128:(fo + 1) * 128]),
              r=(wbr,), w=(wbr,), tag="g_wol")
        for tc2 in range(4):
            tsl = slice(tc2 * 512, (tc2 + 1) * 512)
            bk = bank[2 + tc2 % 2]
            bkr = f"bank{2 + tc2 % 2}"
            for d in range(NDT):
                k.mm(bk, wbuf[:, d * 128:(d + 1) * 128], xs(oa, d, tsl),
                     d == 0, d == NDT - 1, r=(wbr, "g_oa"), w=(bkr,),
                     tag="g_wom")
            k.op("vector", lambda e, bk=bk, fo=fo, tsl=tsl: e.tensor_add(
                xs(x, fo, tsl), xs(x, fo, tsl), bk),
                r=(f"bank{2 + tc2 % 2}", "x"), w=("x",), tag="g_woa")
    k.pop("g_done")



LAST_K = [None]


def build_program(layers, load_x=True, store_x=True):
    nc = bass.Bass("TRN2", target_bir_lowering=False, debug=False)
    sched = Sched()
    k = K(nc, sched)
    LAST_K[0] = k

    xin = k.din("xin", [D, T], F32)
    xout = k.dout("xout", [D, T], F32)
    k.din("c_ones_mat", [128, 128], F32)
    k.din("c_ident_b", [128, 128], BF16)
    k.din("c_ident_f", [128, 128], F32)
    k.din("c_causal01", [128, 128], BF16)
    k.din("c_pow2", [128, 4], F32)
    k.din("c_SLb", [128, 128], BF16)
    k.din("c_SUb", [128, 128], BF16)
    k.din("c_iota512", [128, 512], F32)
    k.din("c_iotacol", [128, 1], F32)
    k.din("c_parcol", [128, 1], BF16)

    if 0 in layers:
        k.din("l0_wqkv_t", [D, 3 * D], F32)
        k.din("l0_wout_t", [D, D], F32)
        k.din("l0_wfc_t", [D, 4096], F32)
        k.din("l0_wproj_t", [4096, D], F32)
        k.din("l0_lnw", [128, 32], F32)
        k.din("l0_lnb", [128, 32], F32)
        k.dtmp("wq_fc", [D, 4096], BF16)
        k.dtmp("wq_proj", [4096, D], BF16)
    if 1 in layers:
        k.din("l1_wqkv", [D, 3 * D], F32)
        k.din("l1_wout", [D, D], F32)
        k.din("l1_bqkv", [128, 24], F32)
        k.din("l1_bout", [128, 8], F32)
        k.din("l1_rw", [D, 8], F32)
        k.din("l1_rb", [128, 1], F32)
        k.din("l1_w1", [8, D, 8192], F32)
        k.din("l1_w2", [8, 4096, D], F32)
    if 2 in layers:
        k.din("l2_gate_wt", [D, D], F32)
        k.din("l2_gate_b", [128, 8], F32)
        k.din("l2_up_wt", [D, 4096], F32)
        k.din("l2_up_b", [128, 32], F32)
        k.din("l2_down_wt", [4096, D], F32)
        k.din("l2_down_b", [128, 8], F32)
        k.din("l2_sin_wt", [D, 4096], F32)
        k.din("l2_sout_wt", [4096, D], F32)
        k.din("l2_decay", [128, 1], F32)
        k.din("l2_conv", [128, 32], F32)
        k.din("l2_smallw", [D, 320], F32)
        k.din("l2_fcb", [128, 1], F32)
        k.din("l2_mow", [32, D], F32)
        k.din("l2_fw1t", [D, 4096], F32)
        k.din("l2_fw1n", [4096, D], F32)
        k.din("l2_fw2t", [4096, D], F32)
        k.din("l2_fw2n", [D, 4096], F32)
        k.din("l2_fb1", [128, 32], F32)
        k.din("l2_fb2", [128, 8], F32)
        k.dtmp("sgn_w1", [D, 4096], BF16)
        k.dtmp("sgn_w2", [4096, D], BF16)
    if 3 in layers:
        k.din("l3_wq", [D, 512], F32)
        k.din("l3_wk", [D, 512], F32)
        k.din("l3_wv", [D, D], F32)
        k.din("l3_wg", [D, D], F32)
        k.din("l3_wgk1", [D, 16], F32)
        k.din("l3_wgk2", [16, 512], F32)
        k.din("l3_bgk2", [128, 4], F32)
        k.din("l3_wout", [D, D], F32)
        k.din("l3_rmsw", [128, 2], F32)
        k.din("l3_rmt", [4, 16, 1024], F32)
        k.din("l3_rut", [4, 16, 4096], F32)
        k.din("l3_wct", [16384, D], F32)
        k.din("c_sel", [16, 1024], BF16)
        k.dtmp("wc_bf", [16384, D], BF16)

    k.sb("x", [128, NDT * T], F32)
    k.sb("xn", [128, NDT * T], BF16)
    k.arena = k.sb("arena", [128, ARENA_F32], F32)
    k.sb("row", [1, T], F32)
    k.sb("scr0", [128, T], F32)
    k.sb("bc_mu", [128, T], F32)
    k.sb("bc_rs", [128, T], F32)
    k.sb("ones_mat", [128, 128], F32)
    k.t["ones_col"] = k.t["ones_mat"][:, 0:1]
    k.sb("ones_col_b", [128, 1], BF16)
    k.sb("ident_b", [128, 128], BF16)
    k.sb("ident_f", [128, 128], F32)
    k.sb("causal01", [128, 128], BF16)
    k.sb("pow2", [128, 4], F32)
    k.sb("SLb", [128, 128], BF16)
    k.sb("SUb", [128, 128], BF16)
    k.sb("iota512", [128, 512], F32)
    k.sb("iotacol", [128, 1], F32)
    k.sb("parcol", [128, 1], BF16)
    k.sb("col0", [128, 1], F32)
    for i in range(3):
        k.sb(f"one{i}", [1, 1], F32)

    k.t["banks"] = [k.ps(f"bank{i}", [128, 512], F32) for i in range(7)]
    k.t["bank_b"] = k.ps("bank_b", [128, 1024], BF16)

    for nm, dst in (("c_ones_mat", "ones_mat"), ("c_ident_f", "ident_f"),
                    ("c_ident_b", "ident_b"), ("c_causal01", "causal01"),
                    ("c_pow2", "pow2"), ("c_SLb", "SLb"), ("c_SUb", "SUb"),
                    ("c_iota512", "iota512"), ("c_iotacol", "iotacol"),
                    ("c_parcol", "parcol")):
        k.dma("sync", k.t[dst], k.dram[nm], w=(dst,), tag=f"ld{nm}")
    k.op("vector", lambda e: e.tensor_copy(k.t["ones_col_b"], k.t["ones_col"]),
         r=("ones_mat",), w=("ones_col_b",))

    k.kcol(1e-5)
    k.kcol(1e-6)
    k.kcol(1.0)
    if load_x:
        k.dma("sync", sb3(k.t["x"], NDT), fm3(xin), w=("x",), tag="ld_x")
    k.s.barrier("init")
    k.init_done = True

    if 3 in layers:
        k.sb("c_sel_s", [16, 1024], BF16)
        k.dma("sync", k.t["c_sel_s"], k.dram["c_sel"], w=("c_sel",), tag="selld")
        k.t["c_sel"] = k.t["c_sel_s"]
        k.s.barrier("init3")
    emitters = {0: emit_layer0, 1: emit_layer1, 2: emit_layer2, 3: emit_layer3}
    for li in layers:
        emitters[li](k)

    if store_x:
        k.dma("sync", fm3(xout), sb3(k.t["x"], NDT),
              r=("x",), w=("xout",), tag="st_x")

    with nc.Block() as block:
        sched.finalize(nc, block)
    return nc



def _f32c(a):
    return np.ascontiguousarray(a, dtype=np.float32)


def make_consts():
    c = {}
    c["c_ones_mat"] = np.ones((128, 128), np.float32)
    c["c_ident_b"] = np.eye(128, dtype=ml_dtypes.bfloat16)
    c["c_ident_f"] = np.eye(128, dtype=np.float32)
    kk = np.arange(128)[:, None]
    qq = np.arange(128)[None, :]
    c["c_causal01"] = (kk <= qq).astype(ml_dtypes.bfloat16)
    p2 = np.zeros((128, 4), np.float32)
    for h in range(2):
        for i in range(9):
            p2[9 * h + i, h] = float(2 ** i)
            p2[32 + 9 * h + i, 2 + h] = float(2 ** i)
    c["c_pow2"] = p2
    c["c_SLb"] = (kk < qq).astype(ml_dtypes.bfloat16)
    c["c_SUb"] = (kk > qq).astype(ml_dtypes.bfloat16)
    c["c_iota512"] = np.broadcast_to(np.arange(512, dtype=np.float32),
                                     (128, 512)).copy()
    c["c_iotacol"] = np.arange(128, dtype=np.float32).reshape(128, 1)
    c["c_parcol"] = (np.arange(128) % 2).astype(ml_dtypes.bfloat16).reshape(128, 1)
    return c


def make_weight_arrays(inputs, layers):
    w = {}
    if 0 in layers:
        w["l0_wqkv_t"] = _f32c(inputs["L0_attn_Wqkv"].T)
        w["l0_wout_t"] = _f32c(inputs["L0_attn_Wout"].T)
        w["l0_wfc_t"] = _f32c(inputs["L0_ffn_W_fc"].T)
        w["l0_wproj_t"] = _f32c(inputs["L0_ffn_W_proj"].T)
        w["l0_lnw"] = _f32c(np.asarray(inputs["L0_ffn_ln_w"]).reshape(32, 128).T)
        w["l0_lnb"] = _f32c(np.asarray(inputs["L0_ffn_ln_b"]).reshape(32, 128).T)
    if 1 in layers:
        w["l1_wqkv"] = _f32c(inputs["L1_attn_w_qkv"])
        w["l1_wout"] = _f32c(inputs["L1_attn_w_out"])
        w["l1_bqkv"] = _f32c(np.asarray(inputs["L1_attn_b_qkv"]).reshape(3, 8, 128)
                             .transpose(2, 0, 1).reshape(128, 24))
        w["l1_bout"] = _f32c(np.asarray(inputs["L1_attn_b_out"]).reshape(8, 128).T)
        w["l1_rw"] = _f32c(inputs["L1_ffn_router_w"])
        rb = np.zeros((128, 1), np.float32)
        rb[0:8, 0] = np.asarray(inputs["L1_ffn_router_b"], np.float32)
        w["l1_rb"] = rb
        w["l1_w1"] = _f32c(inputs["L1_ffn_w1"])
        w["l1_w2"] = _f32c(inputs["L1_ffn_w2"])
    if 2 in layers:
        w["l2_gate_wt"] = _f32c(np.asarray(inputs["L2_attn_gate_w"]).T)
        w["l2_gate_b"] = _f32c(np.asarray(inputs["L2_attn_gate_b"]).reshape(8, 128).T)
        w["l2_up_wt"] = _f32c(np.asarray(inputs["L2_attn_mlp_up_w"]).T)
        w["l2_up_b"] = _f32c(np.asarray(inputs["L2_attn_mlp_up_b"]).reshape(32, 128).T)
        w["l2_down_wt"] = _f32c(np.asarray(inputs["L2_attn_mlp_down_w"]).T)
        w["l2_down_b"] = _f32c(np.asarray(inputs["L2_attn_mlp_down_b"]).reshape(8, 128).T)
        w["l2_sin_wt"] = _f32c(np.asarray(inputs["L2_attn_state_in_w"]).T)
        w["l2_sout_wt"] = _f32c(np.asarray(inputs["L2_attn_state_out_w"]).T)
        dc = np.zeros((128, 1), np.float32)
        dc[0:4, 0] = np.asarray(inputs["L2_attn_state_decay_logit"], np.float32)
        w["l2_decay"] = dc
        w["l2_conv"] = _f32c(np.asarray(inputs["L2_attn_conv_w"])
                             .reshape(8, 128, 4).transpose(1, 0, 2).reshape(128, 32))
        sw = np.zeros((1024, 320), np.float32)
        sw[:, 0:32] = np.asarray(inputs["L2_attn_qkey_w"]).T
        sw[:, 32:64] = np.asarray(inputs["L2_attn_wkey_w"]).T
        sw[:, 64:96] = np.asarray(inputs["L2_attn_value_w"]).T
        sw[:, 128:146] = np.asarray(inputs["L2_attn_read_bits_w"]).T
        sw[:, 160:178] = np.asarray(inputs["L2_attn_write_bits_w"]).T
        sw[:, 192:193] = np.asarray(inputs["L2_attn_write_gate_w"]).T
        sw[:, 224:225] = np.asarray(inputs["L2_attn_gate_long_w"]).T
        sw[:, 256:257] = np.asarray(inputs["L2_attn_gate_mem_w"]).T
        w["l2_smallw"] = sw
        fcb = np.zeros((128, 1), np.float32)
        fcb[0, 0] = np.asarray(inputs["L2_attn_write_gate_b"]).reshape(-1)[0]
        fcb[32, 0] = np.asarray(inputs["L2_attn_gate_long_b"]).reshape(-1)[0]
        fcb[64, 0] = np.asarray(inputs["L2_attn_gate_mem_b"]).reshape(-1)[0]
        w["l2_fcb"] = fcb
        w["l2_mow"] = _f32c(np.asarray(inputs["L2_attn_mem_out_w"]).T)
        w["l2_fw1t"] = _f32c(np.asarray(inputs["L2_ffn_w1"]).T)
        w["l2_fw1n"] = _f32c(inputs["L2_ffn_w1"])
        w["l2_fw2t"] = _f32c(np.asarray(inputs["L2_ffn_w2"]).T)
        w["l2_fw2n"] = _f32c(inputs["L2_ffn_w2"])
        w["l2_fb1"] = _f32c(np.asarray(inputs["L2_ffn_b1"]).reshape(32, 128).T)
        w["l2_fb2"] = _f32c(np.asarray(inputs["L2_ffn_b2"]).reshape(8, 128).T)
    if 3 in layers:
        w["l3_wq"] = _f32c(inputs["L3_attn_Wq"])
        w["l3_wk"] = _f32c(inputs["L3_attn_Wk"])
        w["l3_wv"] = _f32c(inputs["L3_attn_Wv"])
        w["l3_wg"] = _f32c(inputs["L3_attn_Wg"])
        w["l3_wgk1"] = _f32c(inputs["L3_attn_Wgk1"])
        w["l3_wgk2"] = _f32c(inputs["L3_attn_Wgk2"])
        w["l3_bgk2"] = _f32c(np.asarray(inputs["L3_attn_bgk2"]).reshape(4, 128).T)
        w["l3_wout"] = _f32c(inputs["L3_attn_Wout"])
        w["l3_rmsw"] = _f32c(np.asarray(inputs["L3_attn_rms_w"]).reshape(2, 128).T)
        w["l3_rmt"] = _f32c(np.transpose(np.asarray(inputs["L3_ffn_rail_maker"]),
                                         (0, 3, 1, 2)).reshape(4, 16, 1024))
        w["l3_rut"] = _f32c(np.transpose(np.asarray(inputs["L3_ffn_rail_up_proj"]),
                                         (0, 3, 1, 2)).reshape(4, 16, 4096))
        w["l3_wct"] = _f32c(np.asarray(inputs["L3_ffn_w_cproj"]).T)
        sel = np.zeros((16, 1024), ml_dtypes.bfloat16)
        for r8 in range(8):
            for i in range(16):
                sel[i, r8 * 128 + r8 * 16 + i] = 1
        w["c_sel"] = sel
    return w


def run_layers(inputs, x_in, layers):
    nc = build_program(layers)
    consts = make_consts()
    warr = make_weight_arrays(inputs, layers)
    in_maps = []
    for b in range(NCORES):
        m = {"xin": _f32c(x_in[b])}
        m.update(consts)
        m.update(warr)
        in_maps.append(m)
    res = run_bass_kernel_spmd(nc, in_maps, core_ids=list(range(NCORES)))
    return np.stack([res.results[b]["xout"] for b in range(NCORES)], axis=0)


def kernel(**inputs):
    x = np.asarray(inputs["L0_attn_x"])
    x_fm = np.ascontiguousarray(np.transpose(x, (0, 2, 1)))
    out_fm = run_layers(inputs, x_fm, [0, 1, 2, 3])
    return np.ascontiguousarray(np.transpose(out_fm, (0, 2, 1)).astype(np.float32))


# version 02 (final)
# speedup vs baseline: 1.0709x; 1.0709x over previous
import numpy as np
import ml_dtypes
import concourse.bass as bass
import concourse.mybir as mybir
from concourse.bass_utils import run_bass_kernel_spmd

F32 = mybir.dt.float32
BF16 = mybir.dt.bfloat16
AF = mybir.ActivationFunctionType
ALU = mybir.AluOpType
AX = mybir.AxisListType

NCORES = 8
T = 2048
D = 1024
NDT = D // 128
DEBUG_CKPT = False
ARENA_F32 = 18560

ENGINES = ("tensor", "vector", "scalar", "gpsimd", "sync")
TRACKS = ("tensor", "vector", "scalar", "gpsimd", "sync", "gq", "sq")


NSLOT = 8
DMA_TRACKS = ("sq", "gq")


def _slot_key(track, pos):
    return (track, pos % NSLOT)


class Op:
    __slots__ = ("issue", "track", "fn", "dma", "pos", "know", "inc", "waits",
                 "tag", "barrier")

    def __init__(self, issue, track, fn, dma, pos, tag, barrier=False):
        self.issue = issue
        self.track = track
        self.fn = fn
        self.dma = dma
        self.pos = pos
        self.know = None
        self.inc = dma
        self.waits = []
        self.tag = tag
        self.barrier = barrier


class Sched:

    def __init__(self):
        self.ops = []
        self.track_ops = {t: [] for t in TRACKS}
        self.last_writer = {}
        self.readers = {}
        self.know = {e: {} for e in ENGINES}

    def _dep_key_val(self, dt, dp):
        if dt in DMA_TRACKS:
            return _slot_key(dt, dp), dp // NSLOT + 1
        return dt, dp + 1

    def emit(self, issue, fn, reads=(), writes=(), dma=False, tag=""):
        if dma:
            track = {"sync": "sq", "gpsimd": "gq", "scalar": "sq"}[issue]
        else:
            track = issue
        pos = len(self.track_ops[track])
        op = Op(issue, track, fn, dma, pos, tag)
        if track in ("vector", "scalar", "gpsimd"):
            op.inc = True
        deps = {}

        def add(t, p):
            key = (t, p % NSLOT) if t in DMA_TRACKS else t
            cur = deps.get(key)
            if cur is None or cur[1] < p:
                deps[key] = (t, p)

        for r in reads:
            lw = self.last_writer.get(r)
            if lw is not None:
                add(*lw)
        for w in writes:
            lw = self.last_writer.get(w)
            if lw is not None:
                add(*lw)
            for (rt, rp) in self.readers.get(w, ()):
                add(rt, rp)
        know = self.know[issue]
        if dma and pos >= NSLOT:
            add(track, pos - NSLOT)
        for dkey in sorted(deps, key=str):
            dt, dp = deps[dkey]
            if dt == "tensor" and (dt == track or dt == issue):
                continue
            key, val = self._dep_key_val(dt, dp)
            if know.get(key, 0) >= val:
                continue
            dep_op = self.track_ops[dt][dp]
            dep_op.inc = True
            op.waits.append((key, val, dt, dp))
            for u, c in dep_op.know.items():
                if c > know.get(u, 0):
                    know[u] = c
            know[key] = max(know.get(key, 0), val)
        self.ops.append(op)
        self.track_ops[track].append(op)
        op.know = dict(know)
        okey, oval = self._dep_key_val(track, pos)
        op.know[okey] = max(op.know.get(okey, 0), oval)
        if not dma and track == "tensor":
            self.know[issue][track] = pos + 1
        for r in reads:
            self.readers.setdefault(r, []).append((track, pos))
        for w in writes:
            self.last_writer[w] = (track, pos)
            self.readers[w] = []
        return op

    def _all_dep_targets(self):
        out = []
        for t in TRACKS:
            n = len(self.track_ops[t])
            if t in DMA_TRACKS:
                for s in range(min(n, NSLOT)):
                    p = (n - 1 - s)
                    out.append((t, p))
            else:
                p = n - 1
                while p >= 0 and self.track_ops[t][p].fn is None:
                    p -= 1
                if p >= 0:
                    out.append((t, p))
        return out

    def barrier(self, tag="bar"):
        targets = self._all_dep_targets()
        for e in ENGINES:
            know = self.know[e]
            waits = []
            for (t, dp) in targets:
                key, val = self._dep_key_val(t, dp)
                if know.get(key, 0) >= val:
                    continue
                dep_op = self.track_ops[t][dp]
                dep_op.inc = True
                waits.append((key, val, t, dp))
                for u, c in dep_op.know.items():
                    if c > know.get(u, 0):
                        know[u] = c
                know[key] = max(know.get(key, 0), val)
            if waits:
                pos = len(self.track_ops[e])
                op = Op(e, e, None, False, pos, tag, barrier=True)
                op.waits = waits
                op.know = dict(know)
                self.ops.append(op)
                self.track_ops[e].append(op)
                self.know[e][e] = pos + 1

    def finalize(self, nc, block, total_wait_engine="sync"):
        semkeys = list(ENGINES)
        for q in DMA_TRACKS:
            for s in range(NSLOT):
                semkeys.append((q, s))
        sems = {kk: nc.ctx.enter_context(
            nc.semaphore(f"sem_{kk if isinstance(kk, str) else kk[0] + str(kk[1])}"))
            for kk in semkeys}
        inc_no = {}
        for t in TRACKS:
            if t in DMA_TRACKS:
                continue
            c = 0
            for op in self.track_ops[t]:
                if op.inc:
                    c += 1
                inc_no[(t, op.pos)] = c

        def wait_val(key, val, dt, dp):
            if dt in DMA_TRACKS:
                return 16 * val
            return inc_no[(dt, dp)]

        tagmap = {}

        def run_engine(eng, name):
            for op in self.ops:
                if op.issue != name:
                    continue
                for (key, val, dt, dp) in op.waits:
                    eng.wait_ge(sems[key], wait_val(key, val, dt, dp))
                if op.fn is None:
                    continue
                try:
                    inst = op.fn(eng)
                except Exception:
                    print(f"EMIT FAIL at op tag={op.tag} track={op.track} "
                          f"pos={op.pos}")
                    raise
                try:
                    tagmap[inst.ins.name] = op.tag
                except Exception:
                    pass
                if op.inc:
                    if op.dma:
                        inst.then_inc(sems[_slot_key(op.track, op.pos)], 16)
                    else:
                        inst.then_inc(sems[op.track], 1)
            if name == total_wait_engine:
                for (t, dp) in self._all_dep_targets():
                    key, val = self._dep_key_val(t, dp)
                    eng.wait_ge(sems[key], wait_val(key, val, t, dp))

        @block.sync
        def _(eng):
            run_engine(eng, "sync")

        @block.tensor
        def _(eng):
            run_engine(eng, "tensor")

        @block.vector
        def _(eng):
            run_engine(eng, "vector")

        @block.scalar
        def _(eng):
            run_engine(eng, "scalar")

        @block.gpsimd
        def _(eng):
            run_engine(eng, "gpsimd")

        try:
            import json as _json
            with open("/tmp/ktagmap.json", "w") as _f:
                _json.dump(tagmap, _f)
        except Exception:
            pass


class K:
    def __init__(self, nc, sched):
        self.nc = nc
        self.s = sched
        self.t = {}
        self.dram = {}
        self.arena = None
        self.aoff = 0
        self.scopes = []
        self.carve_log = {}

    def sb(self, name, shape, dtype):
        h = self.nc.ctx.enter_context(self.nc.sbuf_tensor(name, shape, dtype))
        self.t[name] = h[:]
        return self.t[name]

    def ps(self, name, shape, dtype=F32):
        h = self.nc.ctx.enter_context(self.nc.psum_tensor(name, shape, dtype))
        self.t[name] = h[:]
        return self.t[name]

    def carve(self, name, shape, dtype):
        p, cols = shape
        f32cols = (cols * mybir.dt.size(dtype) + 3) // 4
        f32cols = (f32cols + 7) & ~7
        assert self.aoff + f32cols <= ARENA_F32, \
            f"arena overflow at {name}: {self.aoff}+{f32cols}"
        view = self.arena[:, self.aoff:self.aoff + f32cols]
        if dtype != F32:
            view = view.bitcast(dtype)
        view = view[0:p, 0:cols]
        self.aoff += f32cols
        self.t[name] = view
        self.carve_log[name] = (self.aoff - f32cols, p, cols, dtype)
        return view

    def push(self):
        self.scopes.append(self.aoff)

    def pop(self, tag="pop"):
        self.aoff = self.scopes.pop()
        self.s.barrier(tag)

    def din(self, name, shape, dtype):
        ap = self.nc.dram_tensor(name, shape, dtype, kind="ExternalInput").ap()
        self.dram[name] = ap
        return ap

    def dout(self, name, shape, dtype):
        ap = self.nc.dram_tensor(name, shape, dtype, kind="ExternalOutput").ap()
        self.dram[name] = ap
        return ap

    def dtmp(self, name, shape, dtype):
        ap = self.nc.dram_tensor(name, shape, dtype, kind="Internal").ap()
        self.dram[name] = ap
        return ap

    def op(self, issue, fn, r=(), w=(), dma=False, tag=""):
        self.s.emit(issue, fn, r, w, dma, tag)

    def dbg_ckpt(self, name):
        if not DEBUG_CKPT:
            return
        ap = self.dtmp(f"ckpt_{name}", [128, NDT * T], F32)
        self.dram[f"ckpt_{name}"] = ap
        xnv = self.t["xn"]
        scr = self.t["scr0"]
        for d2 in range(NDT):
            for nch2 in range(4):
                tsl2 = slice(nch2 * 512, (nch2 + 1) * 512)
                self.op("vector", lambda e, d2=d2, tsl2=tsl2: e.tensor_copy(
                    scr[:, tsl2], xs(self.t["xn"], d2, tsl2)),
                    r=("xn",), w=(f"scr0_{nch2}",), tag=f"ck{name}")
            self.dma("sync", ap[:, d2 * T:(d2 + 1) * T], scr[:, 0:T],
                     r=("scr0_0", "scr0_1", "scr0_2", "scr0_3"),
                     w=(f"ckpt_{name}",), tag=f"ckd{name}")

    def dma(self, queue, out, in_, r=(), w=(), tag=""):
        self.op(queue, lambda e: e.dma_start(out=out, in_=in_), r, w, dma=True,
                tag=tag)

    def orow(self, base=0):
        return self.t["ones_mat"][base:base + 1, 0:128]

    def kcol(self, value, p=128):
        key = f"kc_{value!r}"
        if key not in self.t:
            assert not getattr(self, "init_done", False), \
                f"kcol({value}) created after init barrier"
            col = self.sb(key, [128, 1], F32)
            self.op("gpsimd", lambda e: e.memset(col, float(value)),
                    r=(), w=(key,), tag=key)
        return self.t[key][0:p, :]

    def mm(self, out, lhsT, rhs, start, stop, r=(), w=(), tag=""):
        self.op("tensor",
                lambda e: e.matmul(out, lhsT, rhs, start=start, stop=stop),
                r, w, tag=tag)


def xs(x, d, tsl=None):
    if tsl is None:
        return x[:, d * T:(d + 1) * T]
    return x[:, d * T + tsl.start: d * T + tsl.stop]


def fm3(dram2d, p=128):
    return dram2d.rearrange("(n p) f -> p n f", p=p)


def sb3(view, n):
    return view.rearrange("p (n f) -> p n f", n=n)



def emit_ln(k, tag):
    x, xn = k.t["x"], k.t["xn"]
    bc_mu, bc_rs = k.t["bc_mu"], k.t["bc_rs"]
    scr0 = k.t["scr0"]
    oc = k.t["ones_col"]
    bank = k.t["banks"]
    row = k.t["row"]
    mu_r = bc_mu[0:1, :]
    tmp_r = bc_rs[0:1, :]
    for nch in range(4):
        tsl = slice(nch * 512, (nch + 1) * 512)
        for d in range(NDT):
            sq = scr0[:, (d % 2) * 512:(d % 2) * 512 + 512]
            k.op("scalar", lambda e, d=d, tsl=tsl, sq=sq: e.activation(
                sq, xs(x, d, tsl), AF.Square), r=("x",), w=(f"scr0_{d % 2}",),
                tag=f"{tag}sq")
            k.mm(bank[4][0:1, :], oc, xs(x, d, tsl), d == 0, d == NDT - 1,
                 r=("x",), w=("bank4",), tag=f"{tag}sum")
            k.mm(bank[5][0:1, :], oc, sq, d == 0, d == NDT - 1,
                 r=(f"scr0_{d % 2}",), w=("bank5",), tag=f"{tag}ssq")
        k.op("scalar", lambda e, tsl=tsl: e.activation(
            mu_r[:, tsl], bank[4][0:1, :], AF.Copy, scale=1.0 / D),
            r=("bank4",), w=("bc_mu",), tag=f"{tag}mu")
        k.op("scalar", lambda e, tsl=tsl: e.activation(
            row[:, tsl], bank[5][0:1, :], AF.Copy, scale=1.0 / D),
            r=("bank5",), w=("row",), tag=f"{tag}msq")
    k.op("vector", lambda e: e.tensor_tensor(tmp_r, mu_r, mu_r, ALU.mult),
         r=("bc_mu",), w=("bc_rs",), tag=f"{tag}mu2")
    k.op("vector", lambda e: e.tensor_sub(row, row, tmp_r),
         r=("row", "bc_rs"), w=("row",), tag=f"{tag}var")
    k.op("scalar", lambda e: e.activation(row, row, AF.Ln, bias=k.kcol(1e-5, 1)),
         r=("row",), w=("row",), tag=f"{tag}ln")
    k.op("scalar", lambda e: e.activation(row, row, AF.Exp, scale=-0.5),
         r=("row",), w=("row",), tag=f"{tag}rstd")
    for nch in range(4):
        tsl = slice(nch * 512, (nch + 1) * 512)
        k.mm(bank[5], k.orow(0), row[:, tsl], True, True,
             r=("row",), w=("bank5",), tag=f"{tag}brs")
        k.op("vector", lambda e, tsl=tsl: e.tensor_copy(bc_rs[:, tsl], bank[5]),
             r=("bank5",), w=("bc_rs",), tag=f"{tag}brse")
        k.mm(bank[4], k.orow(0), mu_r[:, tsl], True, True,
             r=("bc_mu",), w=("bank4",), tag=f"{tag}bmu")
        k.op("vector", lambda e, tsl=tsl: e.tensor_copy(bc_mu[:, tsl], bank[4]),
             r=("bank4",), w=("bc_mu",), tag=f"{tag}bmue")
    for d in range(NDT):
        scr = scr0[:, 0:T]
        k.op("vector", lambda e, d=d, scr=scr: e.tensor_sub(scr, xs(x, d), bc_mu),
             r=("x", "bc_mu"), w=("scr0_0", "scr0_1", "scr0_2", "scr0_3"),
             tag=f"{tag}xsub")
        k.op("vector", lambda e, d=d, scr=scr: e.tensor_mul(xs(xn, d), scr, bc_rs),
             r=("scr0_0", "scr0_1", "scr0_2", "scr0_3", "bc_rs"), w=("xn",),
             tag=f"{tag}xmul")



def emit_attn(k, wqkv_name, wout_name, causal, eps, bias_qkv, bias_out, tag):
    x, xn = k.t["x"], k.t["xn"]
    oc = k.t["ones_col"]
    identb, caus01 = k.t["ident_b"], k.t["causal01"]
    bank = k.t["banks"]
    bank_b = k.t["bank_b"]
    scr0 = k.t["scr0"]
    rows2 = k.t["row"]
    one1 = [k.t[f"one{i}"] for i in range(3)]
    col0 = k.t["col0"]
    Dh = 64
    scale = 1.0 / np.sqrt(Dh)
    wq = k.dram[wqkv_name]
    wo = k.dram[wout_name]

    k.push()
    kpair = k.carve(f"{tag}kp", [128, T], BF16)
    vpair = k.carve(f"{tag}vp", [128, T], BF16)
    qpair = k.carve(f"{tag}qp", [128, 1024], BF16)
    wpair = [k.carve(f"{tag}w0", [128, 3 * 1024], BF16),
             k.carve(f"{tag}w1", [128, 3 * 1024], BF16)]
    vtok = k.carve(f"{tag}vt", [128, 16 * 65], BF16)
    probs = [k.carve(f"{tag}pA", [128, 512], BF16),
             k.carve(f"{tag}pB", [128, 512], BF16)]
    zb = k.carve(f"{tag}zb", [128, 512], F32)
    ctx = k.carve(f"{tag}ctx", [128, NDT * 1024], BF16)
    wob = [k.carve(f"{tag}wo0", [128, 1024], BF16),
           k.carve(f"{tag}wo1", [128, 1024], BF16)]
    sqt = k.carve(f"{tag}sqt", [128, 1024 + T], F32)

    if bias_qkv is not None:
        bq = k.carve(f"{tag}bq", [128, 24], F32)
        bo = k.carve(f"{tag}bo", [128, 8], F32)
        k.dma("sync", bq, k.dram[bias_qkv], w=(f"{tag}bq",))
        k.dma("sync", bo, k.dram[bias_out], w=(f"{tag}bo",))
        bqs = k.carve(f"{tag}bqs", [128, 8], F32)
        k.op("vector", lambda e: e.tensor_scalar_mul(bqs, bq[:, 0:8], scale),
             r=(f"{tag}bq",), w=(f"{tag}bqs",), tag=f"{tag}bsc")

    for qh in range(2):
        q0 = qh * 1024
        kvT = T if (not causal or qh == 1) else 1024
        nkt = kvT // 128
        for hp in range(8):
            wb = wpair[hp % 2]
            wbr = f"{tag}w{hp % 2}"
            for which in range(3):
                src = wq[:, which * 1024 + hp * 128: which * 1024 + (hp + 1) * 128]
                k.dma("gpsimd", sb3(wb[:, which * 1024:(which + 1) * 1024], 8),
                      fm3(src),
                      r=(wbr,), w=(wbr,), tag=f"{tag}wld")
            for which, dstn, toks in ((0, f"{tag}qp", (q0, q0 + 1024)),
                                      (1, f"{tag}kp", (0, kvT)),
                                      (2, f"{tag}vp", (0, kvT))):
                dst = k.t[dstn]
                for tc in range((toks[1] - toks[0]) // 512):
                    gsl = slice(toks[0] + tc * 512, toks[0] + (tc + 1) * 512)
                    lsl = slice(gsl.start - toks[0] + (0 if which else 0),
                                gsl.stop - toks[0]) if which == 0 else gsl
                    bk = bank[tc % 2]
                    bkr = f"bank{tc % 2}"
                    for d in range(NDT):
                        k.mm(bk, wb[:, which * 1024 + d * 128: which * 1024 + (d + 1) * 128],
                             xs(xn, d, gsl), d == 0, d == NDT - 1,
                             r=(wbr, "xn"), w=(bkr,), tag=f"{tag}qkvm")
                    if which == 0:
                        if bias_qkv is None:
                            k.op("scalar", lambda e, bk=bk, dst=dst, lsl=lsl: e.activation(
                                dst[:, lsl], bk, AF.Copy, scale=scale),
                                r=(bkr,), w=(dstn,), tag=f"{tag}qe")
                        else:
                            k.op("vector", lambda e, bk=bk, dst=dst, lsl=lsl, hp=hp: e.tensor_scalar(
                                dst[:, lsl], bk, scale, bqs[:, hp:hp + 1],
                                ALU.mult, ALU.add),
                                r=(bkr, f"{tag}bqs"), w=(dstn,), tag=f"{tag}qe")
                    else:
                        if bias_qkv is None:
                            k.op("scalar", lambda e, bk=bk, dst=dst, lsl=lsl: e.copy(
                                dst[:, lsl], bk), r=(bkr,), w=(dstn,), tag=f"{tag}kve")
                        else:
                            k.op("vector", lambda e, bk=bk, dst=dst, lsl=lsl, hp=hp,
                                 which=which: e.tensor_single_scalar(
                                dst[:, lsl], bk,
                                bq[:, which * 8 + hp:which * 8 + hp + 1], ALU.add),
                                r=(bkr, f"{tag}bq"), w=(dstn,), tag=f"{tag}kve")
            k.op("scalar", lambda e: e.activation(sqt[:, 0:1024], qpair, AF.Square),
                 r=(f"{tag}qp",), w=(f"{tag}sqq",), tag=f"{tag}sqq")
            k.op("scalar", lambda e, kvT=kvT: e.activation(
                sqt[:, 1024:1024 + kvT], kpair[:, 0:kvT], AF.Square),
                r=(f"{tag}kp",), w=(f"{tag}sqk",), tag=f"{tag}sqk")
            for h2 in range(2):
                head = hp * 2 + h2
                po = h2 * 64
                for (ssl, srcres, oi) in ((slice(0, 1024), f"{tag}sqq", 0),
                                          (slice(1024, 1024 + kvT), f"{tag}sqk", 1)):
                    nrch = (ssl.stop - ssl.start) // 512
                    for nch in range(nrch):
                        tsl = slice(ssl.start + nch * 512, ssl.start + (nch + 1) * 512)
                        k.mm(bank[4][0:1, :], oc[po:po + 64, :], sqt[po:po + 64, tsl],
                             True, True, r=(srcres,), w=("bank4",), tag=f"{tag}n2")
                        k.op("vector", lambda e, nch=nch: e.tensor_copy(
                            rows2[:, nch * 512:(nch + 1) * 512], bank[4][0:1, :]),
                            r=("bank4",), w=("row",), tag=f"{tag}n2e")
                    k.op("vector", lambda e, oi=oi, nrch=nrch: e.reduce_max(
                        one1[oi], rows2[:, 0:nrch * 512], axis=AX.X),
                        r=("row",), w=(f"one{oi}",), tag=f"{tag}nmax")
                k.op("scalar", lambda e: e.activation(one1[0], one1[0], AF.Sqrt),
                     r=("one0",), w=("one0",), tag=f"{tag}sqr0")
                k.op("scalar", lambda e: e.activation(one1[1], one1[1], AF.Sqrt),
                     r=("one1",), w=("one1",), tag=f"{tag}sqr1")
                k.op("vector", lambda e: e.tensor_tensor(one1[2], one1[0], one1[1], ALU.mult),
                     r=("one0", "one1"), w=("one2",), tag=f"{tag}cmul")
                k.op("vector", lambda e: e.tensor_scalar_mul(one1[2], one1[2], -1.0),
                     r=("one2",), w=("one2",), tag=f"{tag}cneg")
                k.mm(bank[3][:, 0:1], k.orow(0), one1[2], True, True,
                     r=("one2",), w=("bank3",), tag=f"{tag}cb")
                k.op("vector", lambda e: e.tensor_copy(col0, bank[3][:, 0:1]),
                     r=("bank3",), w=("col0",), tag=f"{tag}cbe")
                k.op("gpsimd", lambda e: e.memset(vtok, 1.0), r=(),
                     w=(f"{tag}vt",), tag=f"{tag}vms")
                for kt in range(nkt):
                    k.op("tensor", lambda e, kt=kt, po=po: e.transpose(
                        bank_b[:, 0:64], vpair[po:po + 64, kt * 128:(kt + 1) * 128],
                        identb[po:po + 64, po:po + 64]),
                        r=(f"{tag}vp",), w=("bank_b",), tag=f"{tag}vT")
                    k.op("vector", lambda e, kt=kt: e.tensor_copy(
                        vtok[:, kt * 65:kt * 65 + 64], bank_b[:, 0:64]),
                        r=("bank_b",), w=(f"{tag}vt",), tag=f"{tag}vTe")
                for qc in range(2):
                    gq0 = q0 + qc * 512
                    ktm = nkt if not causal else min(nkt, (gq0 + 512) // 128)
                    for kt in range(ktm):
                        sb = bank[kt % 2]
                        sbr = f"bank{kt % 2}"
                        k.mm(sb, kpair[po:po + 64, kt * 128:(kt + 1) * 128],
                             qpair[po:po + 64, qc * 512:(qc + 1) * 512],
                             True, True, r=(f"{tag}kp", f"{tag}qp"), w=(sbr,),
                             tag=f"{tag}sc")
                        pb = probs[kt % 2]
                        pbr = f"{tag}p{'A' if kt % 2 == 0 else 'B'}"
                        k.op("scalar", lambda e, sb=sb, pb=pb: e.activation(
                            pb, sb, AF.Exp, bias=col0),
                            r=(sbr, "col0"), w=(pbr,), tag=f"{tag}exp")
                        if causal:
                            for j in range(4):
                                Qj = (gq0 + j * 128) // 128
                                if kt == Qj:
                                    k.op("vector", lambda e, pb=pb, j=j: e.tensor_mul(
                                        pb[:, j * 128:(j + 1) * 128],
                                        pb[:, j * 128:(j + 1) * 128], caus01),
                                        r=(pbr,), w=(pbr,), tag=f"{tag}diag")
                                elif kt > Qj:
                                    k.op("gpsimd", lambda e, pb=pb, j=j: e.memset(
                                        pb[:, j * 128:(j + 1) * 128], 0.0),
                                        r=(), w=(pbr,), tag=f"{tag}mz")
                        pvb = bank[2] if qc % 2 == 0 else bank[6]
                        pvbr = "bank2" if qc % 2 == 0 else "bank6"
                        k.mm(pvb[0:65, :], vtok[:, kt * 65:(kt + 1) * 65], pb,
                             kt == 0, kt == ktm - 1,
                             r=(f"{tag}vt", pbr), w=(pvbr,), tag=f"{tag}pv")
                    pvb = bank[2] if qc % 2 == 0 else bank[6]
                    pvbr = "bank2" if qc % 2 == 0 else "bank6"
                    k.op("vector", lambda e, pvb=pvb: e.tensor_copy(
                        rows2[:, 0:512], pvb[64:65, :]),
                        r=(pvbr,), w=("row",), tag=f"{tag}z")
                    if eps:
                        k.op("vector", lambda e: e.tensor_scalar_add(
                            rows2[:, 0:512], rows2[:, 0:512], eps),
                            r=("row",), w=("row",), tag=f"{tag}zeps")
                    k.op("vector", lambda e: e.reciprocal(rows2[:, 0:512],
                                                          rows2[:, 0:512]),
                         r=("row",), w=("row",), tag=f"{tag}zi")
                    k.mm(bank[3], k.orow(0), rows2[:, 0:512], True, True,
                         r=("row",), w=("bank3",), tag=f"{tag}zbm")
                    k.op("vector", lambda e: e.tensor_copy(zb, bank[3]),
                         r=("bank3",), w=(f"{tag}zbuf",), tag=f"{tag}zbe")
                    dd, rr = divmod(head * 64, 128)
                    k.op("vector", lambda e, dd=dd, rr=rr, qc=qc, pvb=pvb: e.tensor_tensor(
                        ctx[rr:rr + 64, dd * 1024 + qc * 512: dd * 1024 + (qc + 1) * 512],
                        pvb[0:64, :], zb[0:64, :], ALU.mult),
                        r=(pvbr, f"{tag}zbuf"), w=(f"{tag}ctx",), tag=f"{tag}ctxw")
        for fc in range(8):
            wbuf = wob[fc % 2]
            wbr = f"{tag}wo{fc % 2}"
            k.dma("gpsimd", sb3(wbuf, 8),
                  fm3(wo[:, fc * 128:(fc + 1) * 128]),
                  r=(wbr,), w=(wbr,), tag=f"{tag}wold")
            for tc in range(2):
                gsl = slice(q0 + tc * 512, q0 + (tc + 1) * 512)
                bk = bank[4 + tc % 2]
                bkr = f"bank{4 + tc % 2}"
                for d in range(NDT):
                    k.mm(bk, wbuf[:, d * 128:(d + 1) * 128],
                         ctx[:, d * 1024 + tc * 512: d * 1024 + (tc + 1) * 512],
                         d == 0, d == NDT - 1, r=(wbr, f"{tag}ctx"), w=(bkr,),
                         tag=f"{tag}opm")
                if bias_out is None:
                    k.op("vector", lambda e, bk=bk, fc=fc, gsl=gsl: e.tensor_add(
                        xs(x, fc, gsl), xs(x, fc, gsl), bk),
                        r=(bkr, "x"), w=("x",), tag=f"{tag}oadd")
                else:
                    k.op("vector", lambda e, bk=bk, fc=fc, gsl=gsl: e.scalar_tensor_tensor(
                        xs(x, fc, gsl), bk, bo[:, fc:fc + 1], xs(x, fc, gsl),
                        ALU.add, ALU.add),
                        r=(bkr, "x", f"{tag}bo"), w=("x",), tag=f"{tag}oaddb")
    k.pop(f"{tag}done")



def emit_l0_ffn(k):
    x, xn = k.t["x"], k.t["xn"]
    oc = k.t["ones_col"]
    bank = k.t["banks"]
    scr0 = k.t["scr0"]
    col0 = k.t["col0"]
    one1 = [k.t[f"one{i}"] for i in range(3)]
    row = k.t["row"]
    bc_mu, bc_rs = k.t["bc_mu"], k.t["bc_rs"]
    ocb = k.t["ones_col_b"]

    k.push()
    fw = [k.carve("fw0", [128, 4096], F32), k.carve("fw1", [128, 4096], F32)]
    fq = k.carve("fqz", [128, 4096], BF16)
    facc = k.carve("facc", [128, 1], F32)
    sgn = k.carve("sgn", [128, 4096], F32)
    for wnm, qnm, nrow, ncol in (("l0_wfc_t", "wq_fc", D, 4096),
                                 ("l0_wproj_t", "wq_proj", 4096, D)):
        wd = k.dram[wnm]
        qd = k.dram[qnm]
        ntile = nrow // 128
        for i in range(ntile):
            buf = fw[i % 2]
            bnm = f"fw{i % 2}"
            k.dma("sync", buf[:, 0:ncol], wd[i * 128:(i + 1) * 128, :],
                  r=(bnm,), w=(bnm,), tag="q1ld")
            k.op("vector", lambda e, buf=buf, ncol=ncol: e.reduce_sum(
                col0, buf[:, 0:ncol], axis=AX.X, apply_absolute_value=True),
                r=(bnm,), w=("col0",), tag="q1abs")
            if i == 0:
                k.op("vector", lambda e: e.tensor_copy(facc, col0),
                     r=("col0",), w=("facc",), tag="q1cp")
            else:
                k.op("vector", lambda e: e.tensor_add(facc, facc, col0),
                     r=("col0", "facc"), w=("facc",), tag="q1acc")
        k.mm(bank[6][0:1, 0:1], oc, facc, True, True,
             r=("facc",), w=("bank6",), tag="q1red")
        nel = float(nrow * ncol)
        k.op("scalar", lambda e, nel=nel: e.activation(
            one1[0], bank[6][0:1, 0:1], AF.Copy, scale=0.5 / nel, bias=0.5e-6),
            r=("bank6",), w=("one0",), tag="q1th")
        k.mm(bank[6][:, 0:1], k.orow(0), one1[0], True, True,
             r=("one0",), w=("bank6",), tag="q1thb")
        k.op("vector", lambda e: e.tensor_copy(col0, bank[6][:, 0:1]),
             r=("bank6",), w=("col0",), tag="q1the")
        for i in range(ntile):
            buf = fw[i % 2]
            bnm = f"fw{i % 2}"
            k.dma("sync", buf[:, 0:ncol], wd[i * 128:(i + 1) * 128, :],
                  r=(bnm,), w=(bnm,), tag="q2ld")
            k.op("scalar", lambda e, buf=buf, ncol=ncol: e.activation(
                sgn[:, 0:ncol], buf[:, 0:ncol], AF.Sign),
                r=(bnm,), w=("sgn",), tag="q2sgn")
            k.op("scalar", lambda e, buf=buf, ncol=ncol: e.activation(
                buf[:, 0:ncol], buf[:, 0:ncol], AF.Abs),
                r=(bnm,), w=(bnm,), tag="q2abs")
            k.op("vector", lambda e, buf=buf, ncol=ncol: e.tensor_single_scalar(
                buf[:, 0:ncol], buf[:, 0:ncol], col0, ALU.is_gt),
                r=(bnm, "col0"), w=(bnm,), tag="q2cmp")
            k.op("vector", lambda e, buf=buf, ncol=ncol: e.tensor_mul(
                fq[:, 0:ncol], sgn[:, 0:ncol], buf[:, 0:ncol]),
                r=(bnm, "sgn"), w=("fqz",), tag="q2mul")
            k.dma("sync", qd[i * 128:(i + 1) * 128, :], fq[:, 0:ncol],
                  r=("fqz",), w=(qnm,), tag="q2st")
    k.pop("q_done")

    qfc = k.dram["wq_fc"]
    qpr = k.dram["wq_proj"]
    lnw_d, lnb_d = k.dram["l0_lnw"], k.dram["l0_lnb"]
    k.push()
    g32 = k.carve("g32", [128, 32 * 512], BF16)
    pw = [k.carve("pw0", [128, 1024], BF16), k.carve("pw1", [128, 1024], BF16)]
    hw = [k.carve("hw0", [128, 1024], BF16), k.carve("hw1", [128, 1024], BF16)]
    lnw = k.carve("lnw", [128, 32], F32)
    lnb = k.carve("lnb", [128, 32], F32)
    k.dma("sync", lnw, lnw_d, w=("lnw",))
    k.dma("sync", lnb, lnb_d, w=("lnb",))
    for tq in range(4):
        tsl = slice(tq * 512, (tq + 1) * 512)
        for fc in range(32):
            wbuf = hw[fc % 2]
            wbr = f"hw{fc % 2}"
            k.dma("sync", sb3(wbuf, 8),
                  fm3(qfc[:, fc * 128:(fc + 1) * 128]),
                  r=(wbr,), w=(wbr,), tag="hld")
            bk = bank[fc % 2]
            bkr = f"bank{fc % 2}"
            for d in range(NDT):
                k.mm(bk, wbuf[:, d * 128:(d + 1) * 128], xs(xn, d, tsl),
                     d == 0, d == NDT - 1, r=(wbr, "xn"), w=(bkr,), tag="hmm")
            k.op("scalar", lambda e, bk=bk, fc=fc: e.activation(
                g32[:, fc * 512:(fc + 1) * 512], bk, AF.Gelu),
                r=(bkr,), w=("g32",), tag="hgelu")
        for fc in range(32):
            gv = g32[:, fc * 512:(fc + 1) * 512]
            k.mm(bank[4][0:1, :], ocb, gv, fc == 0, fc == 31,
                 r=("g32",), w=("bank4",), tag="lns")
            sq = scr0[:, (fc % 2) * 512:(fc % 2) * 512 + 512]
            k.op("scalar", lambda e, gv=gv, sq=sq: e.activation(
                sq, gv, AF.Square), r=("g32",), w=(f"scr0_{fc % 2}",), tag="lnsq")
            k.mm(bank[5][0:1, :], oc, sq, fc == 0, fc == 31,
                 r=(f"scr0_{fc % 2}",), w=("bank5",), tag="lnss")
        rsl = slice(0, 512)
        mu_r = bc_mu[0:1, 0:512]
        tmp_r = bc_rs[0:1, 0:512]
        rr = row[:, rsl]
        k.op("scalar", lambda e: e.activation(mu_r, bank[4][0:1, :],
                                              AF.Copy, scale=1.0 / 4096),
             r=("bank4",), w=("bc_mu",), tag="lnmu")
        k.op("scalar", lambda e: e.activation(rr, bank[5][0:1, :],
                                              AF.Copy, scale=1.0 / 4096),
             r=("bank5",), w=("row",), tag="lnms")
        k.op("vector", lambda e: e.tensor_tensor(tmp_r, mu_r, mu_r, ALU.mult),
             r=("bc_mu",), w=("bc_rs",), tag="lnm2")
        k.op("vector", lambda e: e.tensor_sub(rr, rr, tmp_r),
             r=("row", "bc_rs"), w=("row",), tag="lnvr")
        k.op("scalar", lambda e: e.activation(rr, rr, AF.Ln, bias=k.kcol(1e-5, 1)),
             r=("row",), w=("row",), tag="lnl")
        k.op("scalar", lambda e: e.activation(rr, rr, AF.Exp, scale=-0.5),
             r=("row",), w=("row",), tag="lnr")
        k.mm(bank[5], k.orow(0), rr, True, True, r=("row",),
             w=("bank5",), tag="lbr")
        k.op("vector", lambda e: e.tensor_copy(bc_rs[:, 0:512], bank[5]),
             r=("bank5",), w=("bc_rs",), tag="lbre")
        k.mm(bank[4], k.orow(0), mu_r, True, True, r=("bc_mu",),
             w=("bank4",), tag="lbm")
        k.op("vector", lambda e: e.tensor_copy(bc_mu[:, 0:512], bank[4]),
             r=("bank4",), w=("bc_mu",), tag="lbme")
        for fc in range(32):
            gv = g32[:, fc * 512:(fc + 1) * 512]
            sq = scr0[:, (fc % 2) * 512:(fc % 2) * 512 + 512]
            k.op("vector", lambda e, gv=gv, sq=sq: e.tensor_sub(sq, gv, bc_mu[:, 0:512]),
                 r=("g32", "bc_mu"), w=(f"scr0_{fc % 2}",), tag="lhs")
            k.op("vector", lambda e, sq=sq: e.tensor_mul(sq, sq, bc_rs[:, 0:512]),
                 r=(f"scr0_{fc % 2}", "bc_rs"), w=(f"scr0_{fc % 2}",), tag="lhm")
            k.op("vector", lambda e, gv=gv, sq=sq, fc=fc: e.tensor_scalar(
                gv, sq, lnw[:, fc:fc + 1], lnb[:, fc:fc + 1], ALU.mult, ALU.add),
                r=(f"scr0_{fc % 2}", "lnw", "lnb"), w=("g32",), tag="lhwb")
        for fo in range(8):
            wbuf = pw[fo % 2]
            wbr = f"pw{fo % 2}"
            for hh in range(4):
                k.dma("sync", sb3(wbuf, 8),
                      fm3(qpr[hh * 1024:(hh + 1) * 1024, fo * 128:(fo + 1) * 128]),
                      r=(wbr,), w=(wbr,), tag="pld")
                bk = bank[2 + fo % 2]
                bkr = f"bank{2 + fo % 2}"
                for d8 in range(8):
                    hc = hh * 8 + d8
                    k.mm(bk, wbuf[:, d8 * 128:(d8 + 1) * 128],
                         g32[:, hc * 512:(hc + 1) * 512],
                         hc == 0, hc == 31, r=(wbr, "g32"), w=(bkr,), tag="pmm")
            bk = bank[2 + fo % 2]
            bkr = f"bank{2 + fo % 2}"
            k.op("vector", lambda e, bk=bk, fo=fo, tsl=tsl: e.tensor_add(
                xs(x, fo, tsl), xs(x, fo, tsl), bk),
                r=(bkr, "x"), w=("x",), tag="padd")
    k.pop("ffn_done")


def emit_layer0(k):
    emit_ln(k, "l0ln_")
    emit_attn(k, "l0_wqkv_t", "l0_wout_t", causal=False, eps=1e-6,
              bias_qkv=None, bias_out=None, tag="l0a_")
    emit_l0_ffn(k)


L1_PARTS = "am"


def emit_layer1(k):
    emit_ln(k, "l1ln_")
    if "a" in L1_PARTS:
        emit_attn(k, "l1_wqkv", "l1_wout", causal=True, eps=0.0,
                  bias_qkv="l1_bqkv", bias_out="l1_bout", tag="l1a_")
    if "m" in L1_PARTS:
        emit_l1_moe(k)


def emit_l1_moe(k):
    x, xn = k.t["x"], k.t["xn"]
    bank = k.t["banks"]
    bank_b = k.t["bank_b"]
    identb = k.t["ident_b"]
    col0 = k.t["col0"]
    w1d = k.dram["l1_w1"]
    w2d = k.dram["l1_w2"]

    k.push()
    act = k.carve("m_act", [128, 8 * T], BF16)
    w1b = [k.carve(f"m_w1{i}", [128, 2 * 2048], BF16) for i in range(2)]
    w2b = [k.carve(f"m_w2{i}", [128, 1024], BF16) for i in range(2)]
    bcm = k.carve("m_bcm", [128, T], F32)
    lgt = k.carve("m_lgt", [128, 16 * 8], F32)
    mrow = k.carve("m_mrow", [8, T], F32)
    mv1 = k.t["row"]
    rw = k.carve("m_rw", [128, 64], BF16)
    rb = k.carve("m_rb", [128, 1], F32)
    mscr = k.carve("m_scr", [128, 8], F32)
    sb16 = k.carve("m_sb16", [128, 512], BF16)

    k.dma("gpsimd", sb3(rw, 8), fm3(k.dram["l1_rw"]), r=("m_rw",), w=("m_rw",),
          tag="rt_ld")
    k.dma("sync", rb, k.dram["l1_rb"], w=("m_rb",), tag="rt_lb")
    for tc in range(4):
        tsl = slice(tc * 512, (tc + 1) * 512)
        for d in range(NDT):
            k.mm(bank[0][0:8, :], rw[:, d * 8:(d + 1) * 8], xs(xn, d, tsl),
                 d == 0, d == NDT - 1, r=("m_rw", "xn"), w=("bank0",), tag="rt_mm")
        k.op("vector", lambda e, tsl=tsl: e.tensor_single_scalar(
            k.t["scr0"][0:8, 0:512], bank[0][0:8, :], rb[0:8, :], ALU.add),
            r=("bank0", "m_rb"), w=("scr0_0",), tag="rt_b")
        for c in range(4):
            ch = tc * 4 + c
            k.op("tensor", lambda e, c=c: e.transpose(
                bank[5][:, 0:8],
                k.t["scr0"][0:8, c * 128:(c + 1) * 128], k.t["ident_f"][0:8, 0:8]),
                r=("scr0_0", "ident_f"), w=("bank5",), tag="rt_T")
            k.op("vector", lambda e, ch=ch: e.tensor_copy(
                lgt[:, ch * 8:(ch + 1) * 8], bank[5][:, 0:8]),
                r=("bank5",), w=("m_lgt",), tag="rt_Te")
    for ch in range(16):
        lg = lgt[:, ch * 8:(ch + 1) * 8]
        k.op("vector", lambda e, lg=lg: e.reduce_max(col0, lg, axis=AX.X),
             r=("m_lgt",), w=("col0",), tag="mx")
        k.op("vector", lambda e, lg=lg: e.tensor_single_scalar(
            mscr, lg, col0, ALU.subtract), r=("m_lgt", "col0"), w=("m_scr",),
            tag="msub")
        k.op("scalar", lambda e: e.activation(mscr, mscr, AF.Exp),
             r=("m_scr",), w=("m_scr",), tag="mexp")
        k.op("vector", lambda e: e.reduce_sum(col0, mscr, axis=AX.X),
             r=("m_scr",), w=("col0",), tag="msum")
        k.op("vector", lambda e: e.reciprocal(col0, col0),
             r=("col0",), w=("col0",), tag="mrec")
        k.op("vector", lambda e, lg=lg: e.tensor_single_scalar(
            mscr, mscr, 1.0, ALU.is_ge), r=("m_scr",), w=("m_scr",), tag="mge")
        k.op("vector", lambda e: e.tensor_single_scalar(
            mscr, mscr, col0, ALU.mult), r=("m_scr", "col0"), w=("m_scr",),
            tag="mmp")
        k.op("tensor", lambda e, ch=ch: e.transpose(
            bank[5][0:8, 0:128], mscr, k.t["ident_f"]),
            r=("m_scr", "ident_f"), w=("bank5",), tag="mT")
        k.op("vector", lambda e, ch=ch: e.tensor_copy(
            mrow[:, ch * 128:(ch + 1) * 128], bank[5][0:8, 0:128]),
            r=("bank5",), w=("m_mrow",), tag="mTe")

    for e_i in range(8):
        k.dma("gpsimd", mv1, mrow[e_i:e_i + 1, :], r=("m_mrow", "row"),
              w=("row",), tag="mvst")
        for c4 in range(4):
            tsl = slice(c4 * 512, (c4 + 1) * 512)
            k.mm(bank[6], k.orow(0), mv1[:, tsl], True, True,
                 r=("row",), w=("bank6",), tag="mvb")
            k.op("vector", lambda e, tsl=tsl: e.tensor_copy(bcm[:, tsl], bank[6]),
                 r=("bank6",), w=("m_bcm",), tag="mvbe")
        for jg in range(4):
            for j4 in range(4):
                j = jg * 4 + j4
                wb = w1b[j % 2]
                wbr = f"m_w1{j % 2}"
                k.dma("gpsimd", sb3(wb[:, 0:2048], 8),
                      fm3(w1d[e_i, :, j * 256:(j + 1) * 256]),
                      r=(wbr,), w=(wbr,), tag="w1la")
                k.dma("gpsimd", sb3(wb[:, 2048:4096], 8),
                      fm3(w1d[e_i, :, 4096 + j * 256: 4096 + (j + 1) * 256]),
                      r=(wbr,), w=(wbr,), tag="w1lg")
                for tc in range(4):
                    tsl = slice(tc * 512, (tc + 1) * 512)
                    for ms in range(2):
                        ba, bg = bank[tc % 2], bank[2 + tc % 2]
                        bar, bgr = f"bank{tc % 2}", f"bank{2 + tc % 2}"
                        for d in range(NDT):
                            k.mm(ba, wb[:, d * 256 + ms * 128: d * 256 + (ms + 1) * 128],
                                 xs(xn, d, tsl), d == 0, d == NDT - 1,
                                 r=(wbr, "xn"), w=(bar,), tag="amm")
                        for d in range(NDT):
                            k.mm(bg, wb[:, 2048 + d * 256 + ms * 128: 2048 + d * 256 + (ms + 1) * 128],
                                 xs(xn, d, tsl), d == 0, d == NDT - 1,
                                 r=(wbr, "xn"), w=(bgr,), tag="gmm")
                        k.op("scalar", lambda e, ba=ba: e.activation(
                            sb16, ba, AF.Silu), r=(bar,), w=("m_sb16",),
                            tag="asilu")
                        ht = j4 * 2 + ms
                        k.op("vector", lambda e, bg=bg, ht=ht, tc=tc: e.tensor_mul(
                            act[:, ht * T + tc * 512: ht * T + (tc + 1) * 512],
                            sb16, bg), r=("m_sb16", bgr), w=("m_act",),
                            tag="actw")
            for fo in range(8):
                wb = w2b[fo % 2]
                wbr = f"m_w2{fo % 2}"
                k.dma("gpsimd", sb3(wb, 8),
                      fm3(w2d[e_i, jg * 1024:(jg + 1) * 1024, fo * 128:(fo + 1) * 128]),
                      r=(wbr,), w=(wbr,), tag="w2ld")
                for tc in range(4):
                    ybk = bank[4 + tc % 2]
                    ybr = f"bank{4 + tc % 2}"
                    ysc = k.t["scr0"][:, (tc % 2) * 512:(tc % 2) * 512 + 512]
                    for ht in range(8):
                        k.mm(ybk,
                             wb[:, ht * 128:(ht + 1) * 128],
                             act[:, ht * T + tc * 512: ht * T + (tc + 1) * 512],
                             ht == 0, ht == 7, r=(wbr, "m_act"),
                             w=(ybr,), tag="ymm")
                    k.op("vector", lambda e, tc=tc, ybk=ybk, ysc=ysc: e.tensor_mul(
                        ysc, ybk, bcm[:, tc * 512:(tc + 1) * 512]),
                        r=(ybr, "m_bcm"), w=(f"scr0_{tc % 2}",), tag="ymask")
                    k.op("vector", lambda e, fo=fo, tc=tc, ysc=ysc: e.tensor_add(
                        xs(x, fo, slice(tc * 512, (tc + 1) * 512)),
                        xs(x, fo, slice(tc * 512, (tc + 1) * 512)), ysc),
                        r=(f"scr0_{tc % 2}", "x"), w=("x",), tag="yadd")
    k.pop("moe_done")


def emit_rms_bcast(k, src_tiles_fn, nd, eps, bc_dst, bcres, tag):
    bank = k.t["banks"]
    scr0 = k.t["scr0"]
    row = k.t["row"]
    oc = k.t["ones_col"]
    for nch in range(4):
        tsl = slice(nch * 512, (nch + 1) * 512)
        for d in range(nd):
            src, srcres = src_tiles_fn(d, tsl)
            sq = scr0[:, (d % 2) * 512:(d % 2) * 512 + 512]
            k.op("scalar", lambda e, src=src, sq=sq: e.activation(
                sq, src, AF.Square), r=(srcres,), w=(f"scr0_{d % 2}",),
                tag=f"{tag}sq")
            k.mm(bank[5][0:1, :], oc, sq, d == 0, d == nd - 1,
                 r=(f"scr0_{d % 2}",), w=("bank5",), tag=f"{tag}ssq")
        k.op("scalar", lambda e, tsl=tsl: e.activation(
            row[:, tsl], bank[5][0:1, :], AF.Copy, scale=1.0 / (nd * 128)),
            r=("bank5",), w=("row",), tag=f"{tag}ms")
    k.op("scalar", lambda e: e.activation(row, row, AF.Ln, bias=k.kcol(eps, 1)),
         r=("row",), w=("row",), tag=f"{tag}ln")
    k.op("scalar", lambda e: e.activation(row, row, AF.Exp, scale=-0.5),
         r=("row",), w=("row",), tag=f"{tag}ex")
    for nch in range(4):
        tsl = slice(nch * 512, (nch + 1) * 512)
        k.mm(bank[5], k.orow(0), row[:, tsl], True, True,
             r=("row",), w=("bank5",), tag=f"{tag}bm")
        k.op("vector", lambda e, tsl=tsl: e.tensor_copy(bc_dst[:, tsl], bank[5]),
             r=("bank5",), w=(bcres,), tag=f"{tag}be")


def emit_l2_ffn(k):
    x, xn = k.t["x"], k.t["xn"]
    bank = k.t["banks"]

    k.push()
    fwb = [k.carve("f2w0", [128, 4096], F32), k.carve("f2w1", [128, 4096], F32)]
    fqz = k.carve("f2q", [128, 4096], BF16)
    a1 = k.carve("f2a1", [128, 32], F32)
    a2 = k.carve("f2a2", [128, 8], F32)
    b1 = k.carve("f2b1", [128, 32], F32)
    b2 = k.carve("f2b2", [128, 8], F32)
    k.dma("sync", b1, k.dram["l2_fb1"], w=("f2b1",))
    k.dma("sync", b2, k.dram["l2_fb2"], w=("f2b2",))
    for nm, dst, dstres, ntile, ncol in (("l2_fw1n", a1, "f2a1", 32, 1024),
                                         ("l2_fw2n", a2, "f2a2", 8, 4096)):
        wd = k.dram[nm]
        for i in range(ntile):
            buf = fwb[i % 2]
            bnm = f"f2w{i % 2}"
            k.dma("sync", buf[:, 0:ncol], wd[i * 128:(i + 1) * 128, :],
                  r=(bnm,), w=(bnm,), tag="a_ld")
            k.op("vector", lambda e, buf=buf, dst=dst, i=i, ncol=ncol: e.reduce_sum(
                dst[:, i:i + 1], buf[:, 0:ncol], axis=AX.X,
                apply_absolute_value=True),
                r=(bnm,), w=(dstres,), tag="a_red")
        k.op("vector", lambda e, dst=dst, ntile=ntile, ncol=ncol: e.tensor_scalar_mul(
            dst[:, 0:ntile], dst[:, 0:ntile], 1.0 / ncol),
            r=(dstres,), w=(dstres,), tag="a_sc")
    for nm, qnm, ntile, ncol in (("l2_fw1t", "sgn_w1", 8, 4096),
                                 ("l2_fw2t", "sgn_w2", 32, 1024)):
        wd, qd = k.dram[nm], k.dram[qnm]
        for i in range(ntile):
            buf = fwb[i % 2]
            bnm = f"f2w{i % 2}"
            k.dma("sync", buf[:, 0:ncol], wd[i * 128:(i + 1) * 128, :],
                  r=(bnm,), w=(bnm,), tag="s_ld")
            k.op("scalar", lambda e, buf=buf, ncol=ncol: e.activation(
                fqz[:, 0:ncol], buf[:, 0:ncol], AF.Sign),
                r=(bnm,), w=("f2q",), tag="s_sgn")
            k.dma("sync", qd[i * 128:(i + 1) * 128, :], fqz[:, 0:ncol],
                  r=("f2q",), w=(qnm,), tag="s_st")
    k.pop("f2q_done")

    sg1 = k.dram["sgn_w1"]
    sg2 = k.dram["sgn_w2"]
    k.push()
    hbuf = k.carve("f2h", [128, 32 * 512], BF16)
    hw = [k.carve("f2hw0", [128, 1024], BF16), k.carve("f2hw1", [128, 1024], BF16)]
    for tq in range(4):
        tsl = slice(tq * 512, (tq + 1) * 512)
        for fc in range(32):
            wbuf = hw[fc % 2]
            wbr = f"f2hw{fc % 2}"
            k.dma("sync", sb3(wbuf, 8), fm3(sg1[:, fc * 128:(fc + 1) * 128]),
                  r=(wbr,), w=(wbr,), tag="f2hld")
            bk = bank[fc % 2]
            bkr = f"bank{fc % 2}"
            for d in range(NDT):
                k.mm(bk, wbuf[:, d * 128:(d + 1) * 128], xs(xn, d, tsl),
                     d == 0, d == NDT - 1, r=(wbr, "xn"), w=(bkr,), tag="f2hmm")
            k.op("scalar", lambda e, bk=bk, fc=fc: e.activation(
                hbuf[:, fc * 512:(fc + 1) * 512], bk, AF.Gelu,
                scale=a1[:, fc:fc + 1], bias=b1[:, fc:fc + 1]),
                r=(bkr, "f2a1", "f2b1"), w=("f2h",), tag="f2gelu")
        for fo in range(8):
            fbk = bank[2 + fo % 2]
            fbr = f"bank{2 + fo % 2}"
            fsc = k.t["scr0"][:, (fo % 2) * 512:(fo % 2) * 512 + 512]
            for hh in range(4):
                wbuf = hw[hh % 2]
                wbr = f"f2hw{hh % 2}"
                k.dma("sync", sb3(wbuf, 8),
                      fm3(sg2[hh * 1024:(hh + 1) * 1024, fo * 128:(fo + 1) * 128]),
                      r=(wbr,), w=(wbr,), tag="f2pld")
                for d8 in range(8):
                    hc = hh * 8 + d8
                    k.mm(fbk, wbuf[:, d8 * 128:(d8 + 1) * 128],
                         hbuf[:, hc * 512:(hc + 1) * 512],
                         hc == 0, hc == 31, r=(wbr, "f2h"), w=(fbr,),
                         tag="f2pmm")
            k.op("vector", lambda e, fo=fo, fbk=fbk, fsc=fsc: e.tensor_scalar(
                fsc, fbk, a2[:, fo:fo + 1], b2[:, fo:fo + 1],
                ALU.mult, ALU.add),
                r=(fbr, "f2a2", "f2b2"), w=(f"scr0_{fo % 2}",), tag="f2ev")
            k.op("vector", lambda e, fo=fo, tsl=tsl, fsc=fsc: e.tensor_add(
                xs(x, fo, tsl), xs(x, fo, tsl), fsc),
                r=(f"scr0_{fo % 2}", "x"), w=("x",), tag="f2add")
    k.pop("f2_done")


L2_STAGE = 99


def emit_layer2(k):
    emit_ln(k, "l2ln_")
    if L2_STAGE >= 1:
        emit_l2_ffn(k)
    if L2_STAGE >= 2:
        emit_l2_attn(k)


def emit_l2_attn(k):
    x, xn = k.t["x"], k.t["xn"]
    bank = k.t["banks"]
    scr0 = k.t["scr0"]
    identf = k.t["ident_f"]

    k.push()
    rwidx = k.carve("l2idx", [128, 16 * 4], F32)
    featsA = k.t["bc_mu"]
    featsC = k.t["bc_rs"]

    k.push()
    ug = k.carve("l2u", [128, NDT * T], BF16)

    k.push()
    bc2 = k.carve("l2bc2", [128, T], F32)
    gb = k.carve("l2gb", [128, 8], F32)
    gwb = [k.carve("l2gw0", [128, 1024], BF16), k.carve("l2gw1", [128, 1024], BF16)]
    emit_rms_bcast(k, lambda d, tsl: (xs(xn, d, tsl), "xn"), NDT, 1e-6,
                   bc2, "l2bc2", "l2rms_")
    k.dma("sync", gb, k.dram["l2_gate_b"], w=("l2gb",))
    gwt = k.dram["l2_gate_wt"]
    for fc in range(NDT):
        wbuf = gwb[fc % 2]
        wbr = f"l2gw{fc % 2}"
        k.dma("gpsimd", sb3(wbuf, 8), fm3(gwt[:, fc * 128:(fc + 1) * 128]),
              r=(wbr,), w=(wbr,), tag="l2gld")
        for tc in range(4):
            tsl = slice(tc * 512, (tc + 1) * 512)
            bk = bank[tc % 2]
            bkr = f"bank{tc % 2}"
            for d in range(NDT):
                k.mm(bk, wbuf[:, d * 128:(d + 1) * 128], xs(xn, d, tsl),
                     d == 0, d == NDT - 1, r=(wbr, "xn"), w=(bkr,), tag="l2gmm")
            k.op("vector", lambda e, bk=bk, tsl=tsl: e.tensor_mul(
                scr0[:, 0:512], bk, bc2[:, tsl]),
                r=(bkr, "l2bc2"), w=("scr0_0",), tag="l2gsc")
            k.op("scalar", lambda e, fc=fc, tsl=tsl: e.activation(
                xs(ug, fc, tsl), scr0[:, 0:512], AF.Sigmoid,
                bias=gb[:, fc:fc + 1]),
                r=("scr0_0", "l2gb"), w=("l2u",), tag="l2gsig")
    cw = k.carve("l2cw", [128, 32], F32)
    k.dma("sync", cw, k.dram["l2_conv"], w=("l2cw",))
    x2t = k.carve("l2x2", [128, T], F32)
    hcv = k.carve("l2hc", [128, T], F32)
    for d in range(NDT):
        k.op("vector", lambda e, d=d: e.tensor_mul(x2t, xs(xn, d), bc2),
             r=("xn", "l2bc2"), w=("l2x2",), tag="l2x2")
        k.op("vector", lambda e, d=d: e.tensor_single_scalar(
            hcv, x2t, cw[:, d * 4 + 3: d * 4 + 4], ALU.mult),
            r=("l2x2", "l2cw"), w=("l2hc",), tag="l2cv3")
        for jj in range(3):
            sft = 3 - jj
            k.op("vector", lambda e, d=d, jj=jj, sft=sft: e.scalar_tensor_tensor(
                hcv[:, sft:T], x2t[:, 0:T - sft],
                cw[:, d * 4 + jj: d * 4 + jj + 1],
                hcv[:, sft:T], ALU.mult, ALU.add),
                r=("l2x2", "l2cw", "l2hc"), w=("l2hc",), tag="l2cvj")
        k.op("scalar", lambda e: e.activation(x2t, hcv, AF.Silu),
             r=("l2hc",), w=("l2x2",), tag="l2silu")
        k.op("vector", lambda e, d=d: e.tensor_mul(xs(ug, d), x2t, xs(ug, d)),
             r=("l2x2", "l2u"), w=("l2u",), tag="l2umul")
    k.pop("l2front_done")
    if L2_STAGE < 3:
        return

    k.push()
    swb = k.carve("l2sw", [128, 8 * 320], BF16)
    k.dma("gpsimd", sb3(swb, 8), fm3(k.dram["l2_smallw"]), r=("l2sw",),
          w=("l2sw",), tag="l2swld")
    for tc in range(4):
        tsl = slice(tc * 512, (tc + 1) * 512)
        for d in range(NDT):
            k.mm(bank[0], swb[:, d * 320:d * 320 + 128], xs(ug, d, tsl),
                 d == 0, d == NDT - 1, r=("l2sw", "l2u"), w=("bank0",),
                 tag="l2fAm")
        k.op("vector", lambda e, tsl=tsl: e.tensor_copy(featsA[:, tsl], bank[0]),
             r=("bank0",), w=("bc_mu",), tag="l2fAe")
        for d in range(NDT):
            k.mm(bank[1][0:64, :], swb[:, d * 320 + 128:d * 320 + 192],
                 xs(ug, d, tsl), d == 0, d == NDT - 1,
                 r=("l2sw", "l2u"), w=("bank1",), tag="l2fBm")
        k.op("vector", lambda e: e.tensor_single_scalar(
            scr0[0:64, 0:512], bank[1][0:64, :], 0.0, ALU.is_gt),
            r=("bank1",), w=("scr0_0",), tag="l2bits")
        k.mm(bank[2][0:2, :], k.t["pow2"][0:18, 0:2], scr0[0:18, 0:512],
             True, True, r=("scr0_0", "pow2"), w=("bank2",), tag="l2ri")
        k.mm(bank[3][0:2, :], k.t["pow2"][32:50, 2:4], scr0[32:50, 0:512],
             True, True, r=("scr0_0", "pow2"), w=("bank3",), tag="l2wi")
        k.op("vector", lambda e: e.tensor_copy(scr0[0:2, 512:1024], bank[2][0:2, :]),
             r=("bank2",), w=("scr0_1",), tag="l2riE")
        k.op("vector", lambda e: e.tensor_copy(scr0[0:2, 1024:1536], bank[3][0:2, :]),
             r=("bank3",), w=("scr0_1",), tag="l2wiE")
        for c in range(4):
            ch = tc * 4 + c
            k.op("tensor", lambda e, c=c: e.transpose(
                bank[4][:, 0:2], scr0[0:2, 512 + c * 128:512 + (c + 1) * 128],
                identf[0:2, 0:2]),
                r=("scr0_1", "ident_f"), w=("bank4",), tag="l2riT")
            k.op("vector", lambda e, ch=ch: e.tensor_copy(
                rwidx[:, ch * 4:ch * 4 + 2], bank[4][:, 0:2]),
                r=("bank4",), w=("l2idx",), tag="l2riTe")
            k.op("tensor", lambda e, c=c: e.transpose(
                bank[4][:, 2:4], scr0[0:2, 1024 + c * 128:1024 + (c + 1) * 128],
                identf[0:2, 0:2]),
                r=("scr0_1", "ident_f"), w=("bank4",), tag="l2wiT")
            k.op("vector", lambda e, ch=ch: e.tensor_copy(
                rwidx[:, ch * 4 + 2:ch * 4 + 4], bank[4][:, 2:4]),
                r=("bank4",), w=("l2idx",), tag="l2wiTe")
        for d in range(NDT):
            k.mm(bank[5], swb[:, d * 320 + 192:d * 320 + 320], xs(ug, d, tsl),
                 d == 0, d == NDT - 1, r=("l2sw", "l2u"), w=("bank5",),
                 tag="l2fCm")
        k.op("vector", lambda e, tsl=tsl: e.tensor_copy(featsC[:, tsl], bank[5]),
             r=("bank5",), w=("bc_rs",), tag="l2fCe")
    fcb = k.carve("l2fcb", [128, 1], F32)
    k.dma("sync", fcb, k.dram["l2_fcb"], w=("l2fcb",))
    for base in (0, 32, 64):
        k.op("scalar", lambda e, base=base: e.activation(
            featsC[base:base + 1, :], featsC[base:base + 1, :], AF.Sigmoid,
            bias=fcb[base:base + 1, :]),
            r=("bc_rs", "l2fcb"), w=("bc_rs",), tag="l2fCsig")
    k.pop("l2feats_done")
    if L2_STAGE < 4:
        return

    k.push()
    glb = k.carve("l2glb", [128, T], BF16)
    for tc in range(4):
        tsl = slice(tc * 512, (tc + 1) * 512)
        k.mm(bank[6], k.orow(32), featsC[32:33, tsl], True, True,
             r=("bc_rs",), w=("bank6",), tag="l2glbm")
        k.op("vector", lambda e, tsl=tsl: e.tensor_copy(glb[:, tsl], bank[6]),
             r=("bank6",), w=("l2glb",), tag="l2glbe")
    sseq = k.carve("l2ss", [128, 4 * T], BF16)
    carry = k.carve("l2cy", [128, 32], F32)
    dky = scr0[:, 1536:2048]
    dcol = k.carve("l2dc", [128, 4], F32)
    sinb = [k.carve("l2si0", [128, 1024], BF16)]
    sob = [k.carve("l2so0", [128, 512], BF16)]
    k.op("gpsimd", lambda e: e.memset(carry, 0.0), w=("l2cy",), tag="l2cy0")
    dl = k.carve("l2dl", [128, 1], F32)
    k.dma("sync", dl, k.dram["l2_decay"], w=("l2dl",))
    k.op("scalar", lambda e: e.activation(dl[0:4, :], dl[0:4, :], AF.Sigmoid),
         r=("l2dl",), w=("l2dl",), tag="l2dsig")
    k.op("tensor", lambda e: e.transpose(bank[6][0:1, 0:4], dl[0:4, :],
                                         identf[0:4, 0:4]),
         r=("l2dl", "ident_f"), w=("bank6",), tag="l2dT")
    k.op("vector", lambda e: e.tensor_copy(dcol[0:1, :], bank[6][0:1, 0:4]),
         r=("bank6",), w=("l2dc",), tag="l2dTe")
    sin = k.dram["l2_sin_wt"]
    sout = k.dram["l2_sout_wt"]
    dkc = k.carve("l2dkc", [128, 1], F32)
    for cg in range(8):
        kk = cg // 2
        k.mm(bank[6][:, 0:1], k.orow(0), dcol[0:1, kk:kk + 1], True, True,
             r=("l2dc",), w=("bank6",), tag="l2dkb")
        k.op("vector", lambda e: e.tensor_copy(dkc, bank[6][:, 0:1]),
             r=("bank6",), w=("l2dkc",), tag="l2dkc")
        k.op("vector", lambda e: e.tensor_scalar(
            dky[:, 0:512], k.t["iota512"], 0.0, dkc, ALU.mult, ALU.add),
            r=("l2dkc", "iota512"), w=("scr0_3",), tag="l2dke")
        for c4 in range(4):
            ct = cg * 4 + c4
            wbuf = sinb[0]
            wbr = "l2si0"
            k.dma("gpsimd", sb3(wbuf, 8), fm3(sin[:, ct * 128:(ct + 1) * 128]),
                  r=(wbr,), w=(wbr,), tag="l2sild")
            for tc in range(4):
                tsl = slice(tc * 512, (tc + 1) * 512)
                bk = bank[tc % 2]
                bkr = f"bank{tc % 2}"
                for d in range(NDT):
                    k.mm(bk, wbuf[:, d * 128:(d + 1) * 128], xs(ug, d, tsl),
                         d == 0, d == NDT - 1, r=(wbr, "l2u"), w=(bkr,),
                         tag="l2simm")
                k.op("vector", lambda e, bk=bk, c4=c4, tc=tc, ct=ct: e.tensor_tensor_scan(
                    sseq[:, c4 * T + tc * 512: c4 * T + (tc + 1) * 512],
                    dky[:, 0:512], bk, carry[:, ct:ct + 1], ALU.mult, ALU.add),
                    r=(bkr, "scr0_3", "l2cy"), w=("l2ss",), tag="l2scan")
                k.op("vector", lambda e, c4=c4, tc=tc, ct=ct: e.tensor_copy(
                    carry[:, ct:ct + 1],
                    sseq[:, c4 * T + tc * 512 + 511: c4 * T + tc * 512 + 512]),
                    r=("l2ss",), w=("l2cy",), tag="l2carry")
        for fo in range(8):
            wbuf = sob[0]
            wbr = "l2so0"
            k.dma("gpsimd", sb3(wbuf, 4),
                  fm3(sout[cg * 512:(cg + 1) * 512, fo * 128:(fo + 1) * 128]),
                  r=(wbr,), w=(wbr,), tag="l2sold")
            for tc in range(4):
                tsl = slice(tc * 512, (tc + 1) * 512)
                bk = bank[4 + tc % 2]
                bkr = f"bank{4 + tc % 2}"
                for c4 in range(4):
                    k.mm(bk, wbuf[:, c4 * 128:(c4 + 1) * 128],
                         sseq[:, c4 * T + tc * 512: c4 * T + (tc + 1) * 512],
                         c4 == 0, c4 == 3, r=(wbr, "l2ss"), w=(bkr,),
                         tag="l2somm")
                k.op("vector", lambda e, bk=bk, tsl=tsl: e.tensor_mul(
                    scr0[:, 0:512], bk, glb[:, tsl]),
                    r=(bkr, "l2glb"), w=("scr0_0",), tag="l2lgl")
                k.op("vector", lambda e, fo=fo, tsl=tsl: e.tensor_add(
                    xs(xn, fo, tsl), xs(xn, fo, tsl), scr0[:, 0:512]),
                    r=("scr0_0", "xn"), w=("xn",), tag="l2ladd")
    k.pop("l2ssm_done")
    k.pop("l2u_done")
    k.dbg_ckpt("ssm")
    if L2_STAGE < 5:
        return

    emit_l2_memscan(k, featsA, featsC, rwidx)
    k.dbg_ckpt("mem")
    k.pop("l2attnA_done")
    if L2_STAGE < 6:
        return

    k.push()
    bc3 = k.carve("l2bc3", [128, T], F32)
    if L2_STAGE < 6.05:
        k.pop("dbg")
        return
    if L2_STAGE < 6.08:
        for nch in range(4):
            tsl = slice(nch * 512, (nch + 1) * 512)
            for d in range(NDT):
                sq = scr0[:, (d % 2) * 512:(d % 2) * 512 + 512]
                k.op("scalar", lambda e, d=d, tsl=tsl, sq=sq: e.activation(
                    sq, xs(xn, d, tsl), AF.Square), r=("xn",),
                    w=(f"scr0_{d % 2}",), tag="dbgsq")
        k.pop("dbg")
        return
    emit_rms_bcast(k, lambda d, tsl: (xs(xn, d, tsl), "xn"), NDT, 1e-6,
                   bc3, "l2bc3", "l2yrms_")
    if L2_STAGE < 6.2:
        k.pop("dbg")
        return
    ub = k.carve("l2ub", [128, 32], F32)
    db = k.carve("l2db", [128, 8], F32)
    k.dma("sync", ub, k.dram["l2_up_b"], w=("l2ub",))
    k.dma("sync", db, k.dram["l2_down_b"], w=("l2db",))
    hbuf = k.carve("l2mh", [128, 32 * 512], BF16)
    hw = [k.carve("l2mw0", [128, 1024], BF16), k.carve("l2mw1", [128, 1024], BF16)]
    upw = k.dram["l2_up_wt"]
    dnw = k.dram["l2_down_wt"]
    for tq in range(4):
        tsl = slice(tq * 512, (tq + 1) * 512)
        for fc in range(32):
            wbuf = hw[fc % 2]
            wbr = f"l2mw{fc % 2}"
            k.dma("gpsimd", sb3(wbuf, 8), fm3(upw[:, fc * 128:(fc + 1) * 128]),
                  r=(wbr,), w=(wbr,), tag="l2uld")
            bk = bank[fc % 2]
            bkr = f"bank{fc % 2}"
            for d in range(NDT):
                k.mm(bk, wbuf[:, d * 128:(d + 1) * 128], xs(xn, d, tsl),
                     d == 0, d == NDT - 1, r=(wbr, "xn"), w=(bkr,), tag="l2umm")
            k.op("vector", lambda e, bk=bk, tsl=tsl: e.tensor_mul(
                scr0[:, 0:512], bk, bc3[:, tsl]),
                r=(bkr, "l2bc3"), w=("scr0_0",), tag="l2urs")
            k.op("scalar", lambda e, fc=fc: e.activation(
                hbuf[:, fc * 512:(fc + 1) * 512], scr0[:, 0:512], AF.Silu,
                bias=ub[:, fc:fc + 1]),
                r=("scr0_0", "l2ub"), w=("l2mh",), tag="l2usilu")
        for fo in range(8):
            fbk = bank[2 + fo % 2]
            fbr = f"bank{2 + fo % 2}"
            for hh in range(4):
                wbuf = hw[hh % 2]
                wbr = f"l2mw{hh % 2}"
                k.dma("gpsimd", sb3(wbuf, 8),
                      fm3(dnw[hh * 1024:(hh + 1) * 1024, fo * 128:(fo + 1) * 128]),
                      r=(wbr,), w=(wbr,), tag="l2dld")
                for d8 in range(8):
                    hc = hh * 8 + d8
                    k.mm(fbk, wbuf[:, d8 * 128:(d8 + 1) * 128],
                         hbuf[:, hc * 512:(hc + 1) * 512],
                         hc == 0, hc == 31, r=(wbr, "l2mh"), w=(fbr,),
                         tag="l2dmm")
            k.op("vector", lambda e, fo=fo, tsl=tsl, fbk=fbk: e.scalar_tensor_tensor(
                xs(x, fo, tsl), fbk, db[:, fo:fo + 1], xs(x, fo, tsl),
                ALU.add, ALU.add),
                r=(fbr, "l2db", "x"), w=("x",), tag="l2dadd")
    if L2_STAGE >= 6.5:
        for fo in range(NDT):
            k.op("vector", lambda e, fo=fo: e.tensor_add(
                xs(x, fo), xs(x, fo), xs(xn, fo)),
                r=("x", "xn"), w=("x",), tag="l2yadd")
    k.pop("l2mlp_done")


def emit_l2_memscan(k, featsA, featsC, rwidx):
    xn = k.t["xn"]
    bank = k.t["banks"]
    bank_b = k.t["bank_b"]
    scr0 = k.t["scr0"]
    row = k.t["row"]
    oc, ocb = k.t["ones_col"], k.t["ones_col_b"]
    identb, identf = k.t["ident_b"], k.t["ident_f"]
    SLb, SUb = k.t["SLb"], k.t["SUb"]
    iota512, iotac = k.t["iota512"], k.t["iotacol"]
    parcol = k.t["parcol"]

    k.push()
    Km = k.carve("msK", [128, 512], BF16)
    Vm = k.carve("msV", [128, 512], BF16)
    lastm = k.carve("msL", [128, 8], BF16)
    readsT = k.carve("msRd", [32, T], BF16)
    OHT = k.carve("msOHT", [128, 512], BF16)
    OHRT = k.carve("msORT", [128, 512], BF16)
    OH = k.carve("msOH", [128, 512], BF16)
    OHR = k.carve("msOR", [128, 512], BF16)
    OHTw = k.carve("msOTW", [128, 512], BF16)
    W2W = k.carve("msW2W", [128, 128], BF16)
    W2WU = k.carve("msW2U", [128, 128], BF16)
    R2 = k.carve("msR2", [128, 128], BF16)
    R2L = k.carve("msR2L", [128, 128], BF16)
    OHN = k.carve("msOHN", [128, 128], BF16)
    EX = k.carve("msEX", [128, 128], BF16)
    CF = k.carve("msCF", [128, 128], BF16)
    ktok = k.carve("msk", [128, 32], BF16)
    vtok = k.carve("msv", [128, 32], BF16)
    qtok = k.carve("msq", [128, 32], F32)
    cols = k.carve("mscol", [128, 16], F32)
    colsb = k.carve("mscbb", [128, 8], BF16)
    rv = k.carve("msrv", [128, 128], F32)
    sc = k.carve("mssc", [128, 8], F32)
    rk = k.carve("msrk", [128, 32], F32)
    tt = k.carve("mstt", [128, 32], F32)
    tt4 = k.carve("mst4", [128, 8], F32)
    gmb = k.carve("msgmb", [128, T], BF16)
    mow = k.carve("msmw", [32, 1024], BF16)

    CG, CL, CLS, CC0, CPI, CSU = 0, 1, 2, 3, 4, 8
    CLMS, CGMS, CWC = 9, 10, 11
    BS1, BS0, BLMS, BIL = 0, 1, 2, 5

    k.op("gpsimd", lambda e: e.memset(Km, 0.0), w=("msK",), tag="msK0")
    k.op("gpsimd", lambda e: e.memset(Vm, 0.0), w=("msV",), tag="msV0")
    k.op("gpsimd", lambda e: e.memset(lastm, -1.0), w=("msL",), tag="msL0")
    k.dma("gpsimd", mow, k.dram["l2_mow"], w=("msmw",), tag="msmwl")
    for tc in range(4):
        tsl = slice(tc * 512, (tc + 1) * 512)
        k.mm(bank[6], k.orow(64), featsC[64:65, tsl], True, True,
             r=("bc_rs",), w=("bank6",), tag="msgmbm")
        k.op("vector", lambda e, tsl=tsl: e.tensor_copy(gmb[:, tsl], bank[6]),
             r=("bank6",), w=("msgmb",), tag="msgmbe")

    for c in range(16):
        csl = slice(c * 128, (c + 1) * 128)
        for (p0, dst, dstres, f32out) in ((32, ktok, "msk", False),
                                          (64, vtok, "msv", False),
                                          (0, qtok, "msq", True)):
            k.op("tensor", lambda e, p0=p0, csl=csl: e.transpose(
                bank[4][:, 0:32], featsA[p0:p0 + 32, csl],
                identf[p0:p0 + 32, p0:p0 + 32]),
                r=("bc_mu", "ident_f"), w=("bank4",), tag="mskvT")
            if f32out:
                k.op("scalar", lambda e, dst=dst: e.activation(
                    dst, bank[4][:, 0:32], AF.Copy, scale=1.0 / np.sqrt(32.0)),
                    r=("bank4",), w=(dstres,), tag="msqE")
            else:
                k.op("vector", lambda e, dst=dst: e.tensor_copy(dst, bank[4][:, 0:32]),
                     r=("bank4",), w=(dstres,), tag="mskvE")
        k.op("tensor", lambda e, csl=csl: e.transpose(
            bank[4][:, 0:1], featsC[0:1, csl], identf[0:1, 0:1]),
            r=("bc_rs", "ident_f"), w=("bank4",), tag="msgT")
        k.op("vector", lambda e: e.tensor_copy(cols[:, CG:CG + 1], bank[4][:, 0:1]),
             r=("bank4",), w=("mscol",), tag="msgE")
        k.op("vector", lambda e: e.tensor_scalar(
            cols[:, CL:CL + 1], cols[:, CG:CG + 1], -1.0, 1.0, ALU.mult, ALU.add),
            r=("mscol",), w=("mscol",), tag="msL1")
        k.op("vector", lambda e: e.tensor_scalar_max(
            cols[:, CL:CL + 1], cols[:, CL:CL + 1], 1e-30),
            r=("mscol",), w=("mscol",), tag="msL2")
        k.op("scalar", lambda e: e.activation(
            cols[:, CL:CL + 1], cols[:, CL:CL + 1], AF.Ln),
            r=("mscol",), w=("mscol",), tag="msL3")

        for h in range(2):
            wcol = rwidx[:, c * 4 + 2 + h: c * 4 + 3 + h]
            rcol = rwidx[:, c * 4 + h: c * 4 + 1 + h]
            k.op("vector", lambda e, wcol=wcol: e.tensor_single_scalar(
                OHT, iota512, wcol, ALU.is_equal),
                r=("l2idx", "iota512"), w=("msOHT",), tag="msOHT")
            k.op("vector", lambda e, rcol=rcol: e.tensor_single_scalar(
                OHRT, iota512, rcol, ALU.is_equal),
                r=("l2idx", "iota512"), w=("msORT",), tag="msORT")
            for bt in range(4):
                k.op("tensor", lambda e, bt=bt: e.transpose(
                    bank_b[:, 0:128], OHT[:, bt * 128:(bt + 1) * 128], identb),
                    r=("msOHT", "ident_b"), w=("bank_b",), tag="msOHTt")
                k.op("vector", lambda e, bt=bt: e.tensor_copy(
                    OH[:, bt * 128:(bt + 1) * 128], bank_b[:, 0:128]),
                    r=("bank_b",), w=("msOH",), tag="msOHe")
                k.op("tensor", lambda e, bt=bt: e.transpose(
                    bank_b[:, 128:256], OHRT[:, bt * 128:(bt + 1) * 128], identb),
                    r=("msORT", "ident_b"), w=("bank_b",), tag="msORTt")
                k.op("vector", lambda e, bt=bt: e.tensor_copy(
                    OHR[:, bt * 128:(bt + 1) * 128], bank_b[:, 128:256]),
                    r=("bank_b",), w=("msOR",), tag="msORe")
            for bt in range(4):
                k.mm(bank[0][:, 0:128], OH[:, bt * 128:(bt + 1) * 128],
                     OH[:, bt * 128:(bt + 1) * 128], bt == 0, bt == 3,
                     r=("msOH",), w=("bank0",), tag="msM")
            k.op("vector", lambda e: e.tensor_mul(W2W, bank[0][:, 0:128], SLb),
                 r=("bank0", "SLb"), w=("msW2W",), tag="msW2W")
            k.op("vector", lambda e: e.tensor_mul(W2WU, bank[0][:, 0:128], SUb),
                 r=("bank0", "SUb"), w=("msW2U",), tag="msW2U")
            for bt in range(4):
                k.mm(bank[1][:, 0:128], OH[:, bt * 128:(bt + 1) * 128],
                     OHR[:, bt * 128:(bt + 1) * 128], bt == 0, bt == 3,
                     r=("msOH", "msOR"), w=("bank1",), tag="msR2m")
            k.op("vector", lambda e: e.tensor_mul(R2, bank[1][:, 0:128], SLb),
                 r=("bank1", "SLb"), w=("msR2",), tag="msR2")
            k.mm(bank[2][0:1, 0:128], ocb, W2W, True, True,
                 r=("msW2W",), w=("bank2",), tag="msn")
            k.op("vector", lambda e: e.tensor_copy(row[:, 0:128], bank[2][0:1, 0:128]),
                 r=("bank2",), w=("row",), tag="msnE")
            k.mm(bank[3][:, 0:128], k.orow(0), row[:, 0:128], True, True,
                 r=("row",), w=("bank3",), tag="msnB")
            k.op("vector", lambda e: e.tensor_single_scalar(
                OHN, bank[3][:, 0:128], iotac, ALU.is_equal),
                r=("bank3", "iotacol"), w=("msOHN",), tag="msOHN")
            k.mm(bank[2][0:1, 0:128], parcol, OHN, True, True,
                 r=("msOHN", "parcol"), w=("bank2",), tag="mspn")
            for bt in range(4):
                k.mm(bank[4][0:1, 0:128], lastm[:, h * 4 + bt: h * 4 + bt + 1],
                     OH[:, bt * 128:(bt + 1) * 128], bt == 0, bt == 3,
                     r=("msL", "msOH"), w=("bank4",), tag="msbl")
            k.op("vector", lambda e: e.tensor_single_scalar(
                row[:, 128:256], bank[4][0:1, 0:128], 0.0, ALU.is_equal),
                r=("bank4",), w=("row",), tag="mspb")
            k.op("vector", lambda e: e.tensor_tensor(
                row[:, 256:384], row[:, 128:256], bank[2][0:1, 0:128], ALU.mult),
                r=("row", "bank2"), w=("row",), tag="mspp")
            k.op("vector", lambda e: e.tensor_tensor(
                row[:, 384:512], row[:, 128:256], bank[2][0:1, 0:128], ALU.add),
                r=("row", "bank2"), w=("row",), tag="msps")
            k.op("vector", lambda e: e.scalar_tensor_tensor(
                row[:, 0:128], row[:, 256:384], -2.0, row[:, 384:512],
                ALU.mult, ALU.add),
                r=("row",), w=("row",), tag="msslot")
            k.op("tensor", lambda e: e.transpose(
                bank[4][:, 0:1], row[:, 0:128], identf[0:1, 0:1]),
                r=("row", "ident_f"), w=("bank4",), tag="msslT")
            k.op("vector", lambda e: e.tensor_copy(cols[:, CLS:CLS + 1],
                                                   bank[4][:, 0:1]),
                 r=("bank4",), w=("mscol",), tag="msslE")
            k.op("vector", lambda e: e.tensor_copy(colsb[:, BS1:BS1 + 1],
                                                   cols[:, CLS:CLS + 1]),
                 r=("mscol",), w=("mscbb",), tag="mss1")
            k.op("vector", lambda e: e.tensor_scalar(
                colsb[:, BS0:BS0 + 1], cols[:, CLS:CLS + 1], -1.0, 1.0,
                ALU.mult, ALU.add),
                r=("mscol",), w=("mscbb",), tag="mss0")

            for s in range(2):
                smask = colsb[:, (BS1 if s == 1 else BS0):(BS1 if s == 1 else BS0) + 1]
                off = h * 256 + s * 128
                k.op("vector", lambda e, smask=smask: e.tensor_tensor(
                    cols[:, CLMS:CLMS + 1], cols[:, CL:CL + 1], smask, ALU.mult),
                    r=("mscol", "mscbb"), w=("mscol",), tag="mslms")
                k.op("vector", lambda e: e.tensor_copy(
                    colsb[:, BLMS:BLMS + 1], cols[:, CLMS:CLMS + 1]),
                    r=("mscol",), w=("mscbb",), tag="mslmsb")
                k.op("vector", lambda e, smask=smask: e.tensor_tensor(
                    cols[:, CGMS:CGMS + 1], cols[:, CG:CG + 1], smask, ALU.mult),
                    r=("mscol", "mscbb"), w=("mscol",), tag="msgms")
                k.op("vector", lambda e: e.tensor_single_scalar(
                    R2L, R2, cols[:, CLMS:CLMS + 1], ALU.mult),
                    r=("msR2", "mscol"), w=("msR2L",), tag="msR2L")
                k.mm(bank[5][:, 0:1], R2, colsb[:, BLMS:BLMS + 1], True, True,
                     r=("msR2", "mscbb"), w=("bank5",), tag="msc0")
                k.op("scalar", lambda e: e.activation(
                    cols[:, CC0:CC0 + 1], bank[5][:, 0:1], AF.Exp),
                    r=("bank5",), w=("mscol",), tag="msc0e")
                k.mm(bank[3][:, 0:128], SUb, R2L, True, True,
                     r=("msR2L", "SUb"), w=("bank3",), tag="msSsuf")
                k.op("scalar", lambda e: e.activation(EX, bank[3][:, 0:128], AF.Exp),
                     r=("bank3",), w=("msEX",), tag="msEXe")
                k.op("vector", lambda e: e.tensor_single_scalar(
                    CF, R2, cols[:, CGMS:CGMS + 1], ALU.mult),
                    r=("msR2", "mscol"), w=("msCF",), tag="msCF1")
                k.op("vector", lambda e: e.tensor_mul(CF, CF, EX),
                     r=("msCF", "msEX"), w=("msCF",), tag="msCF2")
                for (tokb, mem, memres, outdst, outres) in (
                        (ktok, Km, "msK", rk, "msrk"),
                        (vtok, Vm, "msV", None, None)):
                    k.mm(bank[2][:, 0:32], CF, tokb, True, True,
                         r=("msCF", "msk" if tokb is ktok else "msv"),
                         w=("bank2",), tag="msrx")
                    for bt in range(4):
                        k.mm(bank[4][:, 0:32], OHR[:, bt * 128:(bt + 1) * 128],
                             mem[:, off + bt * 32: off + (bt + 1) * 32],
                             bt == 0, bt == 3, r=("msOR", memres),
                             w=("bank4",), tag="msbx")
                    k.op("vector", lambda e: e.tensor_single_scalar(
                        tt, bank[4][:, 0:32], cols[:, CC0:CC0 + 1], ALU.mult),
                        r=("bank4", "mscol"), w=("mstt",), tag="msbc")
                    if outdst is not None:
                        k.op("vector", lambda e, outdst=outdst: e.tensor_tensor(
                            outdst, tt, bank[2][:, 0:32], ALU.add),
                            r=("mstt", "bank2"), w=(outres,), tag="msrkE")
                    else:
                        sl_rv = slice((h * 2 + s) * 32, (h * 2 + s + 1) * 32)
                        k.op("vector", lambda e, sl_rv=sl_rv: e.tensor_tensor(
                            rv[:, sl_rv], tt, bank[2][:, 0:32], ALU.add),
                            r=("mstt", "bank2"), w=("msrv",), tag="msrvE")
                k.op("vector", lambda e: e.tensor_mul(tt, rk, qtok),
                     r=("msrk", "msq"), w=("mstt",), tag="mssc1")
                k.op("vector", lambda e, h=h, s=s: e.reduce_sum(
                    sc[:, h * 2 + s: h * 2 + s + 1], tt, axis=AX.X),
                    r=("mstt",), w=("mssc",), tag="mssc2")
                for bt in range(4):
                    k.mm(bank[6][:, bt:bt + 1], OHT[:, bt * 128:(bt + 1) * 128],
                         colsb[:, BLMS:BLMS + 1], True, True,
                         r=("msOHT", "mscbb"), w=("bank6",), tag="msPi")
                k.op("scalar", lambda e: e.activation(
                    cols[:, CPI:CPI + 4], bank[6][:, 0:4], AF.Exp),
                    r=("bank6",), w=("mscol",), tag="msPie")
                k.mm(bank[5][:, 0:1], W2WU, colsb[:, BLMS:BLMS + 1], True, True,
                     r=("msW2U", "mscbb"), w=("bank5",), tag="msSa")
                k.op("scalar", lambda e: e.activation(
                    cols[:, CSU:CSU + 1], bank[5][:, 0:1], AF.Exp),
                    r=("bank5",), w=("mscol",), tag="msSae")
                k.op("vector", lambda e: e.tensor_tensor(
                    cols[:, CWC:CWC + 1], cols[:, CSU:CSU + 1],
                    cols[:, CGMS:CGMS + 1], ALU.mult),
                    r=("mscol",), w=("mscol",), tag="mswc")
                k.op("vector", lambda e: e.tensor_single_scalar(
                    OHTw, OHT, cols[:, CWC:CWC + 1], ALU.mult),
                    r=("msOHT", "mscol"), w=("msOTW",), tag="msOTW")
                for (tokb, tokres, mem, memres) in ((ktok, "msk", Km, "msK"),
                                                    (vtok, "msv", Vm, "msV")):
                    for bt in range(4):
                        k.mm(bank[2][:, 0:32], OHTw[:, bt * 128:(bt + 1) * 128],
                             tokb, True, True, r=("msOTW", tokres),
                             w=("bank2",), tag="msDx")
                        k.op("vector", lambda e, mem=mem, bt=bt, off=off: e.scalar_tensor_tensor(
                            mem[:, off + bt * 32: off + (bt + 1) * 32],
                            mem[:, off + bt * 32: off + (bt + 1) * 32],
                            cols[:, CPI + bt: CPI + bt + 1], bank[2][:, 0:32],
                            ALU.mult, ALU.add),
                            r=(memres, "mscol", "bank2"), w=(memres,), tag="msKu")
            k.mm(bank[5][:, 0:1], W2WU, ocb, True, True,
                 r=("msW2U",), w=("bank5",), tag="mssu")
            k.op("vector", lambda e: e.tensor_single_scalar(
                tt4[:, 0:1], bank[5][:, 0:1], 0.5, ALU.is_lt),
                r=("bank5",), w=("mst4",), tag="msil")
            k.op("vector", lambda e: e.tensor_tensor(
                colsb[:, BIL:BIL + 1], tt4[:, 0:1], cols[:, CLS:CLS + 1], ALU.mult),
                r=("mst4", "mscol"), w=("mscbb",), tag="msilb")
            for bt in range(4):
                k.mm(bank[6][:, bt:bt + 1], OHT[:, bt * 128:(bt + 1) * 128],
                     ocb, True, True, r=("msOHT",), w=("bank6",), tag="mscnt")
            k.op("vector", lambda e: e.tensor_scalar_min(
                tt4[:, 4:8], bank[6][:, 0:4], 1.0),
                r=("bank6",), w=("mst4",), tag="msanyw")
            for bt in range(4):
                k.mm(bank[3][:, bt:bt + 1], OHT[:, bt * 128:(bt + 1) * 128],
                     colsb[:, BIL:BIL + 1], True, True,
                     r=("msOHT", "mscbb"), w=("bank3",), tag="mslv")
            k.op("vector", lambda e, h=h: e.tensor_tensor(
                tt4[:, 0:4], bank[3][:, 0:4], lastm[:, h * 4:h * 4 + 4],
                ALU.subtract),
                r=("bank3", "msL"), w=("mst4",), tag="msld")
            k.op("vector", lambda e: e.tensor_mul(tt4[:, 0:4], tt4[:, 0:4],
                                                  tt4[:, 4:8]),
                 r=("mst4",), w=("mst4",), tag="msla")
            k.op("vector", lambda e, h=h: e.tensor_tensor(
                lastm[:, h * 4:h * 4 + 4], lastm[:, h * 4:h * 4 + 4],
                tt4[:, 0:4], ALU.add),
                r=("msL", "mst4"), w=("msL",), tag="mslu")
        k.op("vector", lambda e: e.reduce_max(tt4[:, 0:1], sc[:, 0:4], axis=AX.X),
             r=("mssc",), w=("mst4",), tag="msmx")
        k.op("vector", lambda e: e.tensor_single_scalar(
            sc[:, 0:4], sc[:, 0:4], tt4[:, 0:1], ALU.subtract),
            r=("mssc", "mst4"), w=("mssc",), tag="msms")
        k.op("scalar", lambda e: e.activation(sc[:, 0:4], sc[:, 0:4], AF.Exp),
             r=("mssc",), w=("mssc",), tag="msme")
        k.op("vector", lambda e: e.reduce_sum(tt4[:, 0:1], sc[:, 0:4], axis=AX.X),
             r=("mssc",), w=("mst4",), tag="msmsum")
        k.op("vector", lambda e: e.reciprocal(tt4[:, 0:1], tt4[:, 0:1]),
             r=("mst4",), w=("mst4",), tag="msmz")
        k.op("vector", lambda e: e.tensor_single_scalar(
            sc[:, 0:4], sc[:, 0:4], tt4[:, 0:1], ALU.mult),
            r=("mssc", "mst4"), w=("mssc",), tag="msmn")
        k.op("vector", lambda e: e.tensor_single_scalar(
            rk, rv[:, 0:32], sc[:, 0:1], ALU.mult),
            r=("msrv", "mssc"), w=("msrk",), tag="msrd0")
        for a in range(1, 4):
            k.op("vector", lambda e, a=a: e.scalar_tensor_tensor(
                rk, rv[:, a * 32:(a + 1) * 32], sc[:, a:a + 1], rk,
                ALU.mult, ALU.add),
                r=("msrv", "mssc", "msrk"), w=("msrk",), tag="msrdA")
        k.op("tensor", lambda e, csl=csl: e.transpose(
            bank[4][0:32, 0:128], rk, identf),
            r=("msrk", "ident_f"), w=("bank4",), tag="msrdT")
        k.op("vector", lambda e, csl=csl: e.tensor_copy(
            readsT[:, csl], bank[4][0:32, 0:128]),
            r=("bank4",), w=("msRd",), tag="msrdE")

    for fo in range(8):
        for tc in range(4):
            tsl = slice(tc * 512, (tc + 1) * 512)
            k.mm(bank[2], mow[:, fo * 128:(fo + 1) * 128], readsT[:, tsl],
                 True, True, r=("msmw", "msRd"), w=("bank2",), tag="msom")
            k.op("vector", lambda e, tsl=tsl: e.tensor_mul(
                scr0[:, 0:512], bank[2], gmb[:, tsl]),
                r=("bank2", "msgmb"), w=("scr0_0",), tag="msogm")
            k.op("vector", lambda e, fo=fo, tsl=tsl: e.tensor_add(
                xs(xn, fo, tsl), xs(xn, fo, tsl), scr0[:, 0:512]),
                r=("scr0_0", "xn"), w=("xn",), tag="msoadd")
    k.pop("l2ms_done")


L3_PARTS = "af"


def emit_layer3(k):
    emit_ln(k, "l3ln_")
    if "f" in L3_PARTS:
        emit_l3_ffn(k)
    if "a" in L3_PARTS:
        emit_l3_attn(k)


def emit_l3_ffn(k):
    x, xn = k.t["x"], k.t["xn"]
    bank = k.t["banks"]

    k.push()
    bd = k.carve("r_bd", [128, 32 * 512], BF16)
    sel = k.t["c_sel"]

    k.dma("gpsimd", k.dram["wc_bf"], k.dram["l3_wct"], w=("wc_bf",), tag="r_pre")

    k.push()
    rmts = k.carve("r_rm", [16, 1024], BF16)
    rupts = k.carve("r_ru", [16, 4096], BF16)
    stg = k.carve("r_st", [16, 512], BF16)
    for h in range(4):
        k.dma("gpsimd", rmts, k.dram["l3_rmt"][h], r=("r_rm",), w=("r_rm",),
              tag="r_rml")
        k.dma("gpsimd", rupts, k.dram["l3_rut"][h], r=("r_ru",), w=("r_ru",),
              tag="r_rul")
        for dt in range(8):
            for r8 in range(8):
                r = dt * 8 + r8
                k.mm(bank[0][0:16, r8 * 64:(r8 + 1) * 64],
                     rmts[:, r * 16:(r + 1) * 16],
                     rupts[:, r * 64:(r + 1) * 64], True, True,
                     r=("r_rm", "r_ru"), w=("bank0",), tag="r_cmb")
            k.op("vector", lambda e: e.tensor_copy(stg, bank[0][0:16, 0:512]),
                 r=("bank0",), w=("r_st",), tag="r_stge")
            for r8 in range(8):
                k.mm(bank[1][:, r8 * 64:(r8 + 1) * 64],
                     sel[:, r8 * 128:(r8 + 1) * 128],
                     stg[:, r8 * 64:(r8 + 1) * 64], True, True,
                     r=("r_st", "c_sel"), w=("bank1",), tag="r_sct")
            k.op("vector", lambda e, h=h, dt=dt: e.tensor_copy(
                bd[:, (h * 8 + dt) * 512:(h * 8 + dt + 1) * 512], bank[1]),
                r=("bank1",), w=("r_bd",), tag="r_bde")

    k.pop("r_bd_done")
    hz = k.carve("r_hz", [128, 32 * 512], BF16)
    wcb = [k.carve("r_w0", [128, 1024], BF16), k.carve("r_w1", [128, 1024], BF16)]
    wc = k.dram["wc_bf"]
    for tc in range(4):
        tsl = slice(tc * 512, (tc + 1) * 512)
        for h in range(4):
            for dt in range(8):
                for mc in range(4):
                    ktl = dt * 4 + mc
                    zbk = bank[mc % 2]
                    zbr = f"bank{mc % 2}"
                    k.mm(zbk,
                         bd[:, (h * 8 + dt) * 512 + mc * 128:
                            (h * 8 + dt) * 512 + (mc + 1) * 128],
                         xs(xn, dt, tsl), True, True,
                         r=("r_bd", "xn"), w=(zbr,), tag="r_zmm")
                    k.op("scalar", lambda e, ktl=ktl, zbk=zbk: e.activation(
                        hz[:, ktl * 512:(ktl + 1) * 512], zbk, AF.Gelu),
                        r=(zbr,), w=("r_hz",), tag="r_zg")
            for kt in range(32):
                wbuf = wcb[kt % 2]
                wbr = f"r_w{kt % 2}"
                gkt = h * 32 + kt
                k.dma("sync", wbuf, wc[gkt * 128:(gkt + 1) * 128, :],
                      r=(wbr,), w=(wbr,), tag="r_wld")
                for fo in range(8):
                    if fo < 7:
                        bko, bkor = bank[fo], f"bank{fo}"
                    else:
                        bko, bkor = k.t["bank_b"].bitcast(F32), "bank_b"
                    k.mm(bko[:, 0:512], wbuf[:, fo * 128:(fo + 1) * 128],
                         hz[:, kt * 512:(kt + 1) * 512],
                         kt == 0, kt == 31, r=(wbr, "r_hz"),
                         w=(bkor,), tag="r_cmm")
            for fo in range(8):
                if fo < 7:
                    bko, bkor = bank[fo], f"bank{fo}"
                else:
                    bko, bkor = k.t["bank_b"].bitcast(F32), "bank_b"
                k.op("vector", lambda e, fo=fo, tsl=tsl, bko=bko: e.tensor_add(
                    xs(x, fo, tsl), xs(x, fo, tsl), bko[:, 0:512]),
                    r=(bkor, "x"), w=("x",), tag="r_cadd")
    k.pop("r_done")


def emit_l3_attn(k):
    x, xn = k.t["x"], k.t["xn"]
    bank = k.t["banks"]
    bank_b = k.t["bank_b"]
    scr0 = k.t["scr0"]
    identb = k.t["ident_b"]
    caus01 = k.t["causal01"]
    oc = k.t["ones_col"]
    bc_mu = k.t["bc_mu"]
    row = k.t["row"]

    k.push()
    oa = k.carve("g_oa", [128, NDT * T], BF16)
    qh = k.carve("g_qh", [128, T], BF16)
    kh = k.carve("g_kh2", [128, T], BF16)
    vh = k.carve("g_vh", [128, 2 * T], BF16)
    gh = k.carve("g_gh", [128, T], BF16)
    gth = k.carve("g_gth", [128, 2 * T], BF16)
    t1b = k.carve("g_t1", [16, T], BF16)
    S = k.carve("g_S", [128, 256], F32)
    Sb = k.carve("g_Sb", [128, 256], BF16)
    qtb = k.carve("g_qt", [128, 64], BF16)
    ktb = k.carve("g_ktl", [128, 64], BF16)
    amb = k.carve("g_am", [64, 64], BF16)
    vtk = k.carve("g_vt", [64, 256], BF16)
    kht = k.carve("g_khx", [128, 64], BF16)
    khtk = k.carve("g_khT", [64, 128], BF16)
    eecol = k.carve("g_ee", [128, 1], F32)
    zs64 = k.carve("g_zs", [128, 64], F32)
    wvb = [k.carve("g_wv0", [128, 1024], BF16), k.carve("g_wv1", [128, 1024], BF16)]
    gk1 = k.carve("g_k1", [128, 128], BF16)
    gk2 = k.carve("g_k2", [16, 512], BF16)
    bg2 = k.carve("g_b2", [128, 4], F32)
    rmw = k.carve("g_rw", [128, 2], F32)

    k.dma("gpsimd", sb3(gk1, 8), fm3(k.dram["l3_wgk1"]), w=("g_k1",), tag="g_k1l")
    k.dma("gpsimd", gk2, k.dram["l3_wgk2"], w=("g_k2",), tag="g_k2l")
    k.dma("sync", bg2, k.dram["l3_bgk2"], w=("g_b2",), tag="g_b2l")
    k.dma("sync", rmw, k.dram["l3_rmsw"], w=("g_rw",), tag="g_rwl")
    k.op("gpsimd", lambda e: e.memset(zs64, 0.0), w=("g_zs",), tag="g_z0")

    for tc2 in range(4):
        tsl = slice(tc2 * 512, (tc2 + 1) * 512)
        for d in range(NDT):
            k.mm(bank[3][0:16, :], gk1[:, d * 16:(d + 1) * 16],
                 xs(xn, d, tsl), d == 0, d == NDT - 1,
                 r=("g_k1", "xn"), w=("bank3",), tag="g_t1m")
        k.op("vector", lambda e, tsl=tsl: e.tensor_copy(
            t1b[:, tsl], bank[3][0:16, :]),
            r=("bank3",), w=("g_t1",), tag="g_t1e")

    for h in range(4):
        projs = ((k.dram["l3_wq"][:, h * 128:(h + 1) * 128], qh, "g_qh", 1, None),
                 (k.dram["l3_wk"][:, h * 128:(h + 1) * 128], kh, "g_kh2", 1, None),
                 (k.dram["l3_wv"][:, h * 256:(h + 1) * 256], vh, "g_vh", 2, None),
                 (k.dram["l3_wg"][:, h * 256:(h + 1) * 256], gth, "g_gth", 2, AF.Silu))
        for (src, dst, dres, nfc, act) in projs:
            for fc in range(nfc):
                wbuf = wvb[fc % 2]
                wbr = f"g_wv{fc % 2}"
                k.dma("gpsimd", sb3(wbuf, 8),
                      fm3(src[:, fc * 128:(fc + 1) * 128]),
                      r=(wbr,), w=(wbr,), tag="g_wpl")
                for tc2 in range(4):
                    tsl = slice(tc2 * 512, (tc2 + 1) * 512)
                    bk = bank[tc2 % 2]
                    bkr = f"bank{tc2 % 2}"
                    for d in range(NDT):
                        k.mm(bk, wbuf[:, d * 128:(d + 1) * 128],
                             xs(xn, d, tsl), d == 0, d == NDT - 1,
                             r=(wbr, "xn"), w=(bkr,), tag="g_pm")
                    if act is None:
                        k.op("scalar", lambda e, bk=bk, dst=dst, fc=fc, tsl=tsl: e.copy(
                            dst[:, fc * T + tsl.start: fc * T + tsl.stop], bk),
                            r=(bkr,), w=(dres,), tag="g_pe")
                    else:
                        k.op("scalar", lambda e, bk=bk, dst=dst, fc=fc, tsl=tsl,
                             act=act: e.activation(
                            dst[:, fc * T + tsl.start: fc * T + tsl.stop], bk, act),
                            r=(bkr,), w=(dres,), tag="g_pea")
        for tc2 in range(4):
            tsl = slice(tc2 * 512, (tc2 + 1) * 512)
            k.mm(bank[4], gk2[:, h * 128:(h + 1) * 128], t1b[:, tsl],
                 True, True, r=("g_k2", "g_t1"), w=("bank4",), tag="g_t2m")
            sA = scr0[:, 0:512]
            sB = scr0[:, 512:1024]
            sC = scr0[:, 1024:1536]
            k.op("vector", lambda e, h=h: e.tensor_single_scalar(
                sA, bank[4], bg2[:, h:h + 1], ALU.add),
                r=("bank4", "g_b2"), w=("scr0_0",), tag="g_t2b")
            k.op("scalar", lambda e: e.activation(sB, sA, AF.Abs),
                 r=("scr0_0",), w=("scr0_1",), tag="g_abs")
            k.op("scalar", lambda e: e.activation(sB, sB, AF.Exp, scale=-1.0),
                 r=("scr0_1",), w=("scr0_1",), tag="g_expx")
            k.op("scalar", lambda e: e.activation(sB, sB, AF.Ln,
                                                  bias=k.kcol(1.0)),
                 r=("scr0_1",), w=("scr0_1",), tag="g_lnx")
            k.op("vector", lambda e: e.tensor_scalar_min(sC, sA, 0.0),
                 r=("scr0_0",), w=("scr0_2",), tag="g_minx")
            k.op("vector", lambda e: e.tensor_sub(sC, sC, sB),
                 r=("scr0_2", "scr0_1"), w=("scr0_2",), tag="g_subx")
            k.op("vector", lambda e, tsl=tsl: e.tensor_scalar(
                gh[:, tsl], sC, 1.0 / 16.0, -3.0, ALU.mult, ALU.max),
                r=("scr0_2",), w=("g_gh",), tag="g_gfin")
        k.op("gpsimd", lambda e: e.memset(S, 0.0), r=("g_S",), w=("g_S",),
             tag="g_S0")
        k.op("gpsimd", lambda e: e.memset(Sb, 0.0), r=("g_Sb",), w=("g_Sb",),
             tag="g_Sb0")
        for c in range(32):
            csl = slice(c * 64, (c + 1) * 64)
            gc = scr0[:, 0:64]
            egs = scr0[:, 64:128]
            ens = scr0[:, 128:192]
            k.op("vector", lambda e, csl=csl: e.tensor_tensor_scan(
                gc, gh[:, csl], zs64, 0.0, ALU.add, ALU.add),
                r=("g_gh", "g_zs"), w=("scr0_0",), tag="g_cum")
            k.op("scalar", lambda e: e.activation(egs, gc, AF.Exp),
                 r=("scr0_0",), w=("scr0_1",), tag="g_eg")
            k.op("vector", lambda e: e.tensor_scalar(
                ens, gc, -1.0, 60.0, ALU.mult, ALU.min),
                r=("scr0_0",), w=("scr0_2",), tag="g_en1")
            k.op("scalar", lambda e: e.activation(ens, ens, AF.Exp),
                 r=("scr0_2",), w=("scr0_2",), tag="g_en2")
            k.op("vector", lambda e, csl=csl: e.tensor_mul(qtb, qh[:, csl], egs),
                 r=("g_qh", "scr0_1"), w=("g_qt",), tag="g_qtm")
            k.op("vector", lambda e, csl=csl: e.tensor_mul(ktb, kh[:, csl], ens),
                 r=("g_kh2", "scr0_2"), w=("g_ktl",), tag="g_ktm")
            k.mm(bank[2][0:64, 0:64], ktb, qtb, True, True,
                 r=("g_qt", "g_ktl"), w=("bank2",), tag="g_amm")
            k.op("vector", lambda e: e.tensor_mul(
                amb, bank[2][0:64, 0:64], caus01[0:64, 0:64]),
                r=("bank2",), w=("g_am",), tag="g_amsk")
            for dvc in range(2):
                k.op("tensor", lambda e, dvc=dvc, csl=csl: e.transpose(
                    bank_b[0:64, 0:128],
                    vh[:, dvc * T + csl.start: dvc * T + csl.stop], identb),
                    r=("g_vh", "ident_b"), w=("bank_b",), tag="g_vT")
                k.op("vector", lambda e, dvc=dvc: e.tensor_copy(
                    vtk[0:64, dvc * 128:(dvc + 1) * 128], bank_b[0:64, 0:128]),
                    r=("bank_b",), w=("g_vt",), tag="g_vTe")
            for dvc in range(2):
                bko = bank[4 + dvc]
                bkor = f"bank{4 + dvc}"
                k.mm(bko[:, 0:64], Sb[:, dvc * 128:(dvc + 1) * 128], qtb,
                     True, False, r=("g_Sb", "g_qt"), w=(bkor,), tag="g_oin")
                k.mm(bko[:, 0:64], vtk[0:64, dvc * 128:(dvc + 1) * 128], amb,
                     False, True, r=("g_vt", "g_am"), w=(bkor,), tag="g_oia")
                k.op("scalar", lambda e, bko=bko, dvc=dvc, h=h, csl=csl: e.copy(
                    xs(oa, 2 * h + dvc, csl), bko[:, 0:64]),
                    r=(bkor,), w=("g_oa",), tag="g_oe")
            k.op("scalar", lambda e: e.activation(eecol, gc[:, 63:64], AF.Exp),
                 r=("scr0_0",), w=("g_ee",), tag="g_eec")
            k.op("vector", lambda e: e.tensor_single_scalar(
                kht, ktb, eecol, ALU.mult),
                r=("g_ktl", "g_ee"), w=("g_khx",), tag="g_khm")
            k.op("tensor", lambda e: e.transpose(bank_b[0:64, 128:256], kht,
                                                 identb),
                 r=("g_khx", "ident_b"), w=("bank_b",), tag="g_khT")
            k.op("vector", lambda e: e.tensor_copy(khtk, bank_b[0:64, 128:256]),
                 r=("bank_b",), w=("g_khT2",), tag="g_khTe")
            k.mm(bank[6][:, 0:256], khtk, vtk[0:64, :], True, True,
                 r=("g_khT2", "g_vt"), w=("bank6",), tag="g_dS")
            k.op("vector", lambda e: e.scalar_tensor_tensor(
                S, S, eecol, bank[6][:, 0:256], ALU.mult, ALU.add),
                r=("g_S", "g_ee", "bank6"), w=("g_S",), tag="g_Sup")
            k.op("vector", lambda e: e.tensor_copy(Sb, S),
                 r=("g_S",), w=("g_Sb",), tag="g_Sbc")
        for nch in range(4):
            tsl = slice(nch * 512, (nch + 1) * 512)
            for dvc in range(2):
                sq = scr0[:, (dvc % 2) * 512:(dvc % 2) * 512 + 512]
                k.op("scalar", lambda e, h=h, dvc=dvc, tsl=tsl, sq=sq: e.activation(
                    sq, xs(oa, 2 * h + dvc, tsl), AF.Square),
                    r=("g_oa",), w=(f"scr0_{dvc % 2}",), tag="g_rsq")
                k.mm(bank[5][0:1, :], oc, sq, dvc == 0, dvc == 1,
                     r=(f"scr0_{dvc % 2}",), w=("bank5",), tag="g_rssq")
            k.op("scalar", lambda e, tsl=tsl: e.activation(
                row[:, tsl], bank[5][0:1, :], AF.Copy, scale=1.0 / 256),
                r=("bank5",), w=("row",), tag="g_rmsc")
        k.op("scalar", lambda e: e.activation(row, row, AF.Ln,
                                              bias=k.kcol(1e-5, 1)),
             r=("row",), w=("row",), tag="g_rln")
        k.op("scalar", lambda e: e.activation(row, row, AF.Exp, scale=-0.5),
             r=("row",), w=("row",), tag="g_rex")
        for nch in range(4):
            tsl = slice(nch * 512, (nch + 1) * 512)
            k.mm(bank[5], k.orow(0), row[:, tsl], True, True,
                 r=("row",), w=("bank5",), tag="g_rbm")
            k.op("vector", lambda e, tsl=tsl: e.tensor_copy(bc_mu[:, tsl],
                                                            bank[5]),
                 r=("bank5",), w=("bc_mu",), tag="g_rbe")
        for dvc in range(2):
            dv = 2 * h + dvc
            k.op("vector", lambda e, dv=dv: e.tensor_mul(
                scr0[:, 0:T], xs(oa, dv), bc_mu),
                r=("g_oa", "bc_mu"),
                w=("scr0_0", "scr0_1", "scr0_2", "scr0_3"), tag="g_on1")
            k.op("vector", lambda e, dvc=dvc: e.tensor_single_scalar(
                scr0[:, 0:T], scr0[:, 0:T], rmw[:, dvc:dvc + 1], ALU.mult),
                r=("scr0_0", "scr0_1", "scr0_2", "scr0_3", "g_rw"),
                w=("scr0_0", "scr0_1", "scr0_2", "scr0_3"), tag="g_on2")
            k.op("vector", lambda e, dv=dv, dvc=dvc: e.tensor_mul(
                xs(oa, dv), scr0[:, 0:T], gth[:, dvc * T:(dvc + 1) * T]),
                r=("scr0_0", "scr0_1", "scr0_2", "scr0_3", "g_gth"),
                w=("g_oa",), tag="g_on3")

    for fo in range(8):
        wbuf = wvb[fo % 2]
        wbr = f"g_wv{fo % 2}"
        k.dma("gpsimd", sb3(wbuf, 8),
              fm3(k.dram["l3_wout"][:, fo * 128:(fo + 1) * 128]),
              r=(wbr,), w=(wbr,), tag="g_wol")
        for tc2 in range(4):
            tsl = slice(tc2 * 512, (tc2 + 1) * 512)
            bk = bank[2 + tc2 % 2]
            bkr = f"bank{2 + tc2 % 2}"
            for d in range(NDT):
                k.mm(bk, wbuf[:, d * 128:(d + 1) * 128], xs(oa, d, tsl),
                     d == 0, d == NDT - 1, r=(wbr, "g_oa"), w=(bkr,),
                     tag="g_wom")
            k.op("vector", lambda e, bk=bk, fo=fo, tsl=tsl: e.tensor_add(
                xs(x, fo, tsl), xs(x, fo, tsl), bk),
                r=(f"bank{2 + tc2 % 2}", "x"), w=("x",), tag="g_woa")
    k.pop("g_done")



LAST_K = [None]


def build_program(layers, load_x=True, store_x=True):
    nc = bass.Bass("TRN2", target_bir_lowering=False, debug=False)
    sched = Sched()
    k = K(nc, sched)
    LAST_K[0] = k

    xin = k.din("xin", [D, T], F32)
    xout = k.dout("xout", [D, T], F32)
    k.din("c_ones_mat", [128, 128], F32)
    k.din("c_ident_b", [128, 128], BF16)
    k.din("c_ident_f", [128, 128], F32)
    k.din("c_causal01", [128, 128], BF16)
    k.din("c_pow2", [128, 4], F32)
    k.din("c_SLb", [128, 128], BF16)
    k.din("c_SUb", [128, 128], BF16)
    k.din("c_iota512", [128, 512], F32)
    k.din("c_iotacol", [128, 1], F32)
    k.din("c_parcol", [128, 1], BF16)

    if 0 in layers:
        k.din("l0_wqkv_t", [D, 3 * D], F32)
        k.din("l0_wout_t", [D, D], F32)
        k.din("l0_wfc_t", [D, 4096], F32)
        k.din("l0_wproj_t", [4096, D], F32)
        k.din("l0_lnw", [128, 32], F32)
        k.din("l0_lnb", [128, 32], F32)
        k.dtmp("wq_fc", [D, 4096], BF16)
        k.dtmp("wq_proj", [4096, D], BF16)
    if 1 in layers:
        k.din("l1_wqkv", [D, 3 * D], F32)
        k.din("l1_wout", [D, D], F32)
        k.din("l1_bqkv", [128, 24], F32)
        k.din("l1_bout", [128, 8], F32)
        k.din("l1_rw", [D, 8], F32)
        k.din("l1_rb", [128, 1], F32)
        k.din("l1_w1", [8, D, 8192], F32)
        k.din("l1_w2", [8, 4096, D], F32)
    if 2 in layers:
        k.din("l2_gate_wt", [D, D], F32)
        k.din("l2_gate_b", [128, 8], F32)
        k.din("l2_up_wt", [D, 4096], F32)
        k.din("l2_up_b", [128, 32], F32)
        k.din("l2_down_wt", [4096, D], F32)
        k.din("l2_down_b", [128, 8], F32)
        k.din("l2_sin_wt", [D, 4096], F32)
        k.din("l2_sout_wt", [4096, D], F32)
        k.din("l2_decay", [128, 1], F32)
        k.din("l2_conv", [128, 32], F32)
        k.din("l2_smallw", [D, 320], F32)
        k.din("l2_fcb", [128, 1], F32)
        k.din("l2_mow", [32, D], F32)
        k.din("l2_fw1t", [D, 4096], F32)
        k.din("l2_fw1n", [4096, D], F32)
        k.din("l2_fw2t", [4096, D], F32)
        k.din("l2_fw2n", [D, 4096], F32)
        k.din("l2_fb1", [128, 32], F32)
        k.din("l2_fb2", [128, 8], F32)
        k.dtmp("sgn_w1", [D, 4096], BF16)
        k.dtmp("sgn_w2", [4096, D], BF16)
    if 3 in layers:
        k.din("l3_wq", [D, 512], F32)
        k.din("l3_wk", [D, 512], F32)
        k.din("l3_wv", [D, D], F32)
        k.din("l3_wg", [D, D], F32)
        k.din("l3_wgk1", [D, 16], F32)
        k.din("l3_wgk2", [16, 512], F32)
        k.din("l3_bgk2", [128, 4], F32)
        k.din("l3_wout", [D, D], F32)
        k.din("l3_rmsw", [128, 2], F32)
        k.din("l3_rmt", [4, 16, 1024], F32)
        k.din("l3_rut", [4, 16, 4096], F32)
        k.din("l3_wct", [16384, D], F32)
        k.din("c_sel", [16, 1024], BF16)
        k.dtmp("wc_bf", [16384, D], BF16)

    k.sb("x", [128, NDT * T], F32)
    k.sb("xn", [128, NDT * T], BF16)
    k.arena = k.sb("arena", [128, ARENA_F32], F32)
    k.sb("row", [1, T], F32)
    k.sb("scr0", [128, T], F32)
    k.sb("bc_mu", [128, T], F32)
    k.sb("bc_rs", [128, T], F32)
    k.sb("ones_mat", [128, 128], F32)
    k.t["ones_col"] = k.t["ones_mat"][:, 0:1]
    k.sb("ones_col_b", [128, 1], BF16)
    k.sb("ident_b", [128, 128], BF16)
    k.sb("ident_f", [128, 128], F32)
    k.sb("causal01", [128, 128], BF16)
    k.sb("pow2", [128, 4], F32)
    k.sb("SLb", [128, 128], BF16)
    k.sb("SUb", [128, 128], BF16)
    k.sb("iota512", [128, 512], F32)
    k.sb("iotacol", [128, 1], F32)
    k.sb("parcol", [128, 1], BF16)
    k.sb("col0", [128, 1], F32)
    for i in range(3):
        k.sb(f"one{i}", [1, 1], F32)

    k.t["banks"] = [k.ps(f"bank{i}", [128, 512], F32) for i in range(7)]
    k.t["bank_b"] = k.ps("bank_b", [128, 1024], BF16)

    for nm, dst in (("c_ones_mat", "ones_mat"), ("c_ident_f", "ident_f"),
                    ("c_ident_b", "ident_b"), ("c_causal01", "causal01"),
                    ("c_pow2", "pow2"), ("c_SLb", "SLb"), ("c_SUb", "SUb"),
                    ("c_iota512", "iota512"), ("c_iotacol", "iotacol"),
                    ("c_parcol", "parcol")):
        k.dma("sync", k.t[dst], k.dram[nm], w=(dst,), tag=f"ld{nm}")
    k.op("vector", lambda e: e.tensor_copy(k.t["ones_col_b"], k.t["ones_col"]),
         r=("ones_mat",), w=("ones_col_b",))

    k.kcol(1e-5)
    k.kcol(1e-6)
    k.kcol(1.0)
    if load_x:
        k.dma("sync", sb3(k.t["x"], NDT), fm3(xin), w=("x",), tag="ld_x")
    k.s.barrier("init")
    k.init_done = True

    if 3 in layers:
        k.sb("c_sel_s", [16, 1024], BF16)
        k.dma("sync", k.t["c_sel_s"], k.dram["c_sel"], w=("c_sel",), tag="selld")
        k.t["c_sel"] = k.t["c_sel_s"]
        k.s.barrier("init3")
    emitters = {0: emit_layer0, 1: emit_layer1, 2: emit_layer2, 3: emit_layer3}
    for li in layers:
        emitters[li](k)

    if store_x:
        k.dma("sync", fm3(xout), sb3(k.t["x"], NDT),
              r=("x",), w=("xout",), tag="st_x")

    with nc.Block() as block:
        sched.finalize(nc, block)
    return nc



def _f32c(a):
    return np.ascontiguousarray(a, dtype=np.float32)


def make_consts():
    c = {}
    c["c_ones_mat"] = np.ones((128, 128), np.float32)
    c["c_ident_b"] = np.eye(128, dtype=ml_dtypes.bfloat16)
    c["c_ident_f"] = np.eye(128, dtype=np.float32)
    kk = np.arange(128)[:, None]
    qq = np.arange(128)[None, :]
    c["c_causal01"] = (kk <= qq).astype(ml_dtypes.bfloat16)
    p2 = np.zeros((128, 4), np.float32)
    for h in range(2):
        for i in range(9):
            p2[9 * h + i, h] = float(2 ** i)
            p2[32 + 9 * h + i, 2 + h] = float(2 ** i)
    c["c_pow2"] = p2
    c["c_SLb"] = (kk < qq).astype(ml_dtypes.bfloat16)
    c["c_SUb"] = (kk > qq).astype(ml_dtypes.bfloat16)
    c["c_iota512"] = np.broadcast_to(np.arange(512, dtype=np.float32),
                                     (128, 512)).copy()
    c["c_iotacol"] = np.arange(128, dtype=np.float32).reshape(128, 1)
    c["c_parcol"] = (np.arange(128) % 2).astype(ml_dtypes.bfloat16).reshape(128, 1)
    return c


def make_weight_arrays(inputs, layers):
    w = {}
    if 0 in layers:
        w["l0_wqkv_t"] = _f32c(inputs["L0_attn_Wqkv"].T)
        w["l0_wout_t"] = _f32c(inputs["L0_attn_Wout"].T)
        w["l0_wfc_t"] = _f32c(inputs["L0_ffn_W_fc"].T)
        w["l0_wproj_t"] = _f32c(inputs["L0_ffn_W_proj"].T)
        w["l0_lnw"] = _f32c(np.asarray(inputs["L0_ffn_ln_w"]).reshape(32, 128).T)
        w["l0_lnb"] = _f32c(np.asarray(inputs["L0_ffn_ln_b"]).reshape(32, 128).T)
    if 1 in layers:
        w["l1_wqkv"] = _f32c(inputs["L1_attn_w_qkv"])
        w["l1_wout"] = _f32c(inputs["L1_attn_w_out"])
        w["l1_bqkv"] = _f32c(np.asarray(inputs["L1_attn_b_qkv"]).reshape(3, 8, 128)
                             .transpose(2, 0, 1).reshape(128, 24))
        w["l1_bout"] = _f32c(np.asarray(inputs["L1_attn_b_out"]).reshape(8, 128).T)
        w["l1_rw"] = _f32c(inputs["L1_ffn_router_w"])
        rb = np.zeros((128, 1), np.float32)
        rb[0:8, 0] = np.asarray(inputs["L1_ffn_router_b"], np.float32)
        w["l1_rb"] = rb
        w["l1_w1"] = _f32c(inputs["L1_ffn_w1"])
        w["l1_w2"] = _f32c(inputs["L1_ffn_w2"])
    if 2 in layers:
        w["l2_gate_wt"] = _f32c(np.asarray(inputs["L2_attn_gate_w"]).T)
        w["l2_gate_b"] = _f32c(np.asarray(inputs["L2_attn_gate_b"]).reshape(8, 128).T)
        w["l2_up_wt"] = _f32c(np.asarray(inputs["L2_attn_mlp_up_w"]).T)
        w["l2_up_b"] = _f32c(np.asarray(inputs["L2_attn_mlp_up_b"]).reshape(32, 128).T)
        w["l2_down_wt"] = _f32c(np.asarray(inputs["L2_attn_mlp_down_w"]).T)
        w["l2_down_b"] = _f32c(np.asarray(inputs["L2_attn_mlp_down_b"]).reshape(8, 128).T)
        w["l2_sin_wt"] = _f32c(np.asarray(inputs["L2_attn_state_in_w"]).T)
        w["l2_sout_wt"] = _f32c(np.asarray(inputs["L2_attn_state_out_w"]).T)
        dc = np.zeros((128, 1), np.float32)
        dc[0:4, 0] = np.asarray(inputs["L2_attn_state_decay_logit"], np.float32)
        w["l2_decay"] = dc
        w["l2_conv"] = _f32c(np.asarray(inputs["L2_attn_conv_w"])
                             .reshape(8, 128, 4).transpose(1, 0, 2).reshape(128, 32))
        sw = np.zeros((1024, 320), np.float32)
        sw[:, 0:32] = np.asarray(inputs["L2_attn_qkey_w"]).T
        sw[:, 32:64] = np.asarray(inputs["L2_attn_wkey_w"]).T
        sw[:, 64:96] = np.asarray(inputs["L2_attn_value_w"]).T
        sw[:, 128:146] = np.asarray(inputs["L2_attn_read_bits_w"]).T
        sw[:, 160:178] = np.asarray(inputs["L2_attn_write_bits_w"]).T
        sw[:, 192:193] = np.asarray(inputs["L2_attn_write_gate_w"]).T
        sw[:, 224:225] = np.asarray(inputs["L2_attn_gate_long_w"]).T
        sw[:, 256:257] = np.asarray(inputs["L2_attn_gate_mem_w"]).T
        w["l2_smallw"] = sw
        fcb = np.zeros((128, 1), np.float32)
        fcb[0, 0] = np.asarray(inputs["L2_attn_write_gate_b"]).reshape(-1)[0]
        fcb[32, 0] = np.asarray(inputs["L2_attn_gate_long_b"]).reshape(-1)[0]
        fcb[64, 0] = np.asarray(inputs["L2_attn_gate_mem_b"]).reshape(-1)[0]
        w["l2_fcb"] = fcb
        w["l2_mow"] = _f32c(np.asarray(inputs["L2_attn_mem_out_w"]).T)
        w["l2_fw1t"] = _f32c(np.asarray(inputs["L2_ffn_w1"]).T)
        w["l2_fw1n"] = _f32c(inputs["L2_ffn_w1"])
        w["l2_fw2t"] = _f32c(np.asarray(inputs["L2_ffn_w2"]).T)
        w["l2_fw2n"] = _f32c(inputs["L2_ffn_w2"])
        w["l2_fb1"] = _f32c(np.asarray(inputs["L2_ffn_b1"]).reshape(32, 128).T)
        w["l2_fb2"] = _f32c(np.asarray(inputs["L2_ffn_b2"]).reshape(8, 128).T)
    if 3 in layers:
        w["l3_wq"] = _f32c(inputs["L3_attn_Wq"])
        w["l3_wk"] = _f32c(inputs["L3_attn_Wk"])
        w["l3_wv"] = _f32c(inputs["L3_attn_Wv"])
        w["l3_wg"] = _f32c(inputs["L3_attn_Wg"])
        w["l3_wgk1"] = _f32c(inputs["L3_attn_Wgk1"])
        w["l3_wgk2"] = _f32c(inputs["L3_attn_Wgk2"])
        w["l3_bgk2"] = _f32c(np.asarray(inputs["L3_attn_bgk2"]).reshape(4, 128).T)
        w["l3_wout"] = _f32c(inputs["L3_attn_Wout"])
        w["l3_rmsw"] = _f32c(np.asarray(inputs["L3_attn_rms_w"]).reshape(2, 128).T)
        w["l3_rmt"] = _f32c(np.transpose(np.asarray(inputs["L3_ffn_rail_maker"]),
                                         (0, 3, 1, 2)).reshape(4, 16, 1024))
        w["l3_rut"] = _f32c(np.transpose(np.asarray(inputs["L3_ffn_rail_up_proj"]),
                                         (0, 3, 1, 2)).reshape(4, 16, 4096))
        w["l3_wct"] = _f32c(np.asarray(inputs["L3_ffn_w_cproj"]).T)
        sel = np.zeros((16, 1024), ml_dtypes.bfloat16)
        for r8 in range(8):
            for i in range(16):
                sel[i, r8 * 128 + r8 * 16 + i] = 1
        w["c_sel"] = sel
    return w


def run_layers(inputs, x_in, layers):
    nc = build_program(layers)
    consts = make_consts()
    warr = make_weight_arrays(inputs, layers)
    in_maps = []
    for b in range(NCORES):
        m = {"xin": _f32c(x_in[b])}
        m.update(consts)
        m.update(warr)
        in_maps.append(m)
    res = run_bass_kernel_spmd(nc, in_maps, core_ids=list(range(NCORES)))
    return np.stack([res.results[b]["xout"] for b in range(NCORES)], axis=0)


def kernel(**inputs):
    x = np.asarray(inputs["L0_attn_x"])
    x_fm = np.ascontiguousarray(np.transpose(x, (0, 2, 1)))
    out_fm = run_layers(inputs, x_fm, [0, 1, 2, 3])
    return np.ascontiguousarray(np.transpose(out_fm, (0, 2, 1)).astype(np.float32))


# version 04
# speedup vs baseline: 1.0828x; 1.0111x over previous
import numpy as np
import ml_dtypes
import concourse.bass as bass
import concourse.mybir as mybir
from concourse.bass_utils import run_bass_kernel_spmd

F32 = mybir.dt.float32
BF16 = mybir.dt.bfloat16
AF = mybir.ActivationFunctionType
ALU = mybir.AluOpType
AX = mybir.AxisListType

NCORES = 8
T = 2048
D = 1024
NDT = D // 128
DEBUG_CKPT = False
ARENA_F32 = 18560

ENGINES = ("tensor", "vector", "scalar", "gpsimd", "sync")
TRACKS = ("tensor", "vector", "scalar", "gpsimd", "sync", "gq", "sq")


NSLOT = 8
DMA_TRACKS = ("sq", "gq")


def _slot_key(track, pos):
    return (track, pos % NSLOT)


class Op:
    __slots__ = ("issue", "track", "fn", "dma", "pos", "know", "inc", "waits",
                 "tag", "barrier")

    def __init__(self, issue, track, fn, dma, pos, tag, barrier=False):
        self.issue = issue
        self.track = track
        self.fn = fn
        self.dma = dma
        self.pos = pos
        self.know = None
        self.inc = dma
        self.waits = []
        self.tag = tag
        self.barrier = barrier


class Sched:

    def __init__(self):
        self.ops = []
        self.track_ops = {t: [] for t in TRACKS}
        self.last_writer = {}
        self.readers = {}
        self.know = {e: {} for e in ENGINES}

    def _dep_key_val(self, dt, dp):
        if dt in DMA_TRACKS:
            return _slot_key(dt, dp), dp // NSLOT + 1
        return dt, dp + 1

    def emit(self, issue, fn, reads=(), writes=(), dma=False, tag=""):
        if dma:
            track = {"sync": "sq", "gpsimd": "gq", "scalar": "sq"}[issue]
        else:
            track = issue
        pos = len(self.track_ops[track])
        op = Op(issue, track, fn, dma, pos, tag)
        if track in ("vector", "scalar", "gpsimd"):
            op.inc = True
        deps = {}

        def add(t, p):
            key = (t, p % NSLOT) if t in DMA_TRACKS else t
            cur = deps.get(key)
            if cur is None or cur[1] < p:
                deps[key] = (t, p)

        for r in reads:
            lw = self.last_writer.get(r)
            if lw is not None:
                add(*lw)
        for w in writes:
            lw = self.last_writer.get(w)
            if lw is not None:
                add(*lw)
            for (rt, rp) in self.readers.get(w, ()):
                add(rt, rp)
        know = self.know[issue]
        if dma and pos >= NSLOT:
            add(track, pos - NSLOT)
        for dkey in sorted(deps, key=str):
            dt, dp = deps[dkey]
            if dt == "tensor" and (dt == track or dt == issue):
                continue
            key, val = self._dep_key_val(dt, dp)
            if know.get(key, 0) >= val:
                continue
            dep_op = self.track_ops[dt][dp]
            dep_op.inc = True
            op.waits.append((key, val, dt, dp))
            for u, c in dep_op.know.items():
                if c > know.get(u, 0):
                    know[u] = c
            know[key] = max(know.get(key, 0), val)
        self.ops.append(op)
        self.track_ops[track].append(op)
        op.know = dict(know)
        okey, oval = self._dep_key_val(track, pos)
        op.know[okey] = max(op.know.get(okey, 0), oval)
        if not dma and track == "tensor":
            self.know[issue][track] = pos + 1
        for r in reads:
            self.readers.setdefault(r, []).append((track, pos))
        for w in writes:
            self.last_writer[w] = (track, pos)
            self.readers[w] = []
        return op

    def _all_dep_targets(self):
        out = []
        for t in TRACKS:
            n = len(self.track_ops[t])
            if t in DMA_TRACKS:
                for s in range(min(n, NSLOT)):
                    p = (n - 1 - s)
                    out.append((t, p))
            else:
                p = n - 1
                while p >= 0 and self.track_ops[t][p].fn is None:
                    p -= 1
                if p >= 0:
                    out.append((t, p))
        return out

    def barrier(self, tag="bar"):
        targets = self._all_dep_targets()
        for e in ENGINES:
            know = self.know[e]
            waits = []
            for (t, dp) in targets:
                key, val = self._dep_key_val(t, dp)
                if know.get(key, 0) >= val:
                    continue
                dep_op = self.track_ops[t][dp]
                dep_op.inc = True
                waits.append((key, val, t, dp))
                for u, c in dep_op.know.items():
                    if c > know.get(u, 0):
                        know[u] = c
                know[key] = max(know.get(key, 0), val)
            if waits:
                pos = len(self.track_ops[e])
                op = Op(e, e, None, False, pos, tag, barrier=True)
                op.waits = waits
                op.know = dict(know)
                self.ops.append(op)
                self.track_ops[e].append(op)
                self.know[e][e] = pos + 1

    def finalize(self, nc, block, total_wait_engine="sync"):
        semkeys = list(ENGINES)
        for q in DMA_TRACKS:
            for s in range(NSLOT):
                semkeys.append((q, s))
        sems = {kk: nc.ctx.enter_context(
            nc.semaphore(f"sem_{kk if isinstance(kk, str) else kk[0] + str(kk[1])}"))
            for kk in semkeys}
        inc_no = {}
        for t in TRACKS:
            if t in DMA_TRACKS:
                continue
            c = 0
            for op in self.track_ops[t]:
                if op.inc:
                    c += 1
                inc_no[(t, op.pos)] = c

        def wait_val(key, val, dt, dp):
            if dt in DMA_TRACKS:
                return 16 * val
            return inc_no[(dt, dp)]

        tagmap = {}

        def run_engine(eng, name):
            for op in self.ops:
                if op.issue != name:
                    continue
                for (key, val, dt, dp) in op.waits:
                    eng.wait_ge(sems[key], wait_val(key, val, dt, dp))
                if op.fn is None:
                    continue
                try:
                    inst = op.fn(eng)
                except Exception:
                    print(f"EMIT FAIL at op tag={op.tag} track={op.track} "
                          f"pos={op.pos}")
                    raise
                try:
                    tagmap[inst.ins.name] = op.tag
                except Exception:
                    pass
                if op.inc:
                    if op.dma:
                        inst.then_inc(sems[_slot_key(op.track, op.pos)], 16)
                    else:
                        inst.then_inc(sems[op.track], 1)
            if name == total_wait_engine:
                for (t, dp) in self._all_dep_targets():
                    key, val = self._dep_key_val(t, dp)
                    eng.wait_ge(sems[key], wait_val(key, val, t, dp))

        @block.sync
        def _(eng):
            run_engine(eng, "sync")

        @block.tensor
        def _(eng):
            run_engine(eng, "tensor")

        @block.vector
        def _(eng):
            run_engine(eng, "vector")

        @block.scalar
        def _(eng):
            run_engine(eng, "scalar")

        @block.gpsimd
        def _(eng):
            run_engine(eng, "gpsimd")

        try:
            import json as _json
            with open("/tmp/ktagmap.json", "w") as _f:
                _json.dump(tagmap, _f)
        except Exception:
            pass


class K:
    def __init__(self, nc, sched):
        self.nc = nc
        self.s = sched
        self.t = {}
        self.dram = {}
        self.arena = None
        self.aoff = 0
        self.scopes = []
        self.carve_log = {}

    def sb(self, name, shape, dtype):
        h = self.nc.ctx.enter_context(self.nc.sbuf_tensor(name, shape, dtype))
        self.t[name] = h[:]
        return self.t[name]

    def ps(self, name, shape, dtype=F32):
        h = self.nc.ctx.enter_context(self.nc.psum_tensor(name, shape, dtype))
        self.t[name] = h[:]
        return self.t[name]

    def carve(self, name, shape, dtype):
        p, cols = shape
        f32cols = (cols * mybir.dt.size(dtype) + 3) // 4
        f32cols = (f32cols + 7) & ~7
        assert self.aoff + f32cols <= ARENA_F32, \
            f"arena overflow at {name}: {self.aoff}+{f32cols}"
        view = self.arena[:, self.aoff:self.aoff + f32cols]
        if dtype != F32:
            view = view.bitcast(dtype)
        view = view[0:p, 0:cols]
        self.aoff += f32cols
        self.t[name] = view
        self.carve_log[name] = (self.aoff - f32cols, p, cols, dtype)
        return view

    def push(self):
        self.scopes.append(self.aoff)

    def pop(self, tag="pop"):
        self.aoff = self.scopes.pop()
        self.s.barrier(tag)

    def din(self, name, shape, dtype):
        ap = self.nc.dram_tensor(name, shape, dtype, kind="ExternalInput").ap()
        self.dram[name] = ap
        return ap

    def dout(self, name, shape, dtype):
        ap = self.nc.dram_tensor(name, shape, dtype, kind="ExternalOutput").ap()
        self.dram[name] = ap
        return ap

    def dtmp(self, name, shape, dtype):
        ap = self.nc.dram_tensor(name, shape, dtype, kind="Internal").ap()
        self.dram[name] = ap
        return ap

    def op(self, issue, fn, r=(), w=(), dma=False, tag=""):
        self.s.emit(issue, fn, r, w, dma, tag)

    def dbg_ckpt(self, name):
        if not DEBUG_CKPT:
            return
        ap = self.dtmp(f"ckpt_{name}", [128, NDT * T], F32)
        self.dram[f"ckpt_{name}"] = ap
        xnv = self.t["xn"]
        scr = self.t["scr0"]
        for d2 in range(NDT):
            for nch2 in range(4):
                tsl2 = slice(nch2 * 512, (nch2 + 1) * 512)
                self.op("vector", lambda e, d2=d2, tsl2=tsl2: e.tensor_copy(
                    scr[:, tsl2], xs(self.t["xn"], d2, tsl2)),
                    r=("xn",), w=(f"scr0_{nch2}",), tag=f"ck{name}")
            self.dma("sync", ap[:, d2 * T:(d2 + 1) * T], scr[:, 0:T],
                     r=("scr0_0", "scr0_1", "scr0_2", "scr0_3"),
                     w=(f"ckpt_{name}",), tag=f"ckd{name}")

    def dma(self, queue, out, in_, r=(), w=(), tag=""):
        self.op(queue, lambda e: e.dma_start(out=out, in_=in_), r, w, dma=True,
                tag=tag)

    def orow(self, base=0):
        return self.t["ones_mat"][base:base + 1, 0:128]

    def kcol(self, value, p=128):
        key = f"kc_{value!r}"
        if key not in self.t:
            assert not getattr(self, "init_done", False), \
                f"kcol({value}) created after init barrier"
            col = self.sb(key, [128, 1], F32)
            self.op("gpsimd", lambda e: e.memset(col, float(value)),
                    r=(), w=(key,), tag=key)
        return self.t[key][0:p, :]

    def mm(self, out, lhsT, rhs, start, stop, r=(), w=(), tag=""):
        self.op("tensor",
                lambda e: e.matmul(out, lhsT, rhs, start=start, stop=stop),
                r, w, tag=tag)


def xs(x, d, tsl=None):
    if tsl is None:
        return x[:, d * T:(d + 1) * T]
    return x[:, d * T + tsl.start: d * T + tsl.stop]


def fm3(dram2d, p=128):
    return dram2d.rearrange("(n p) f -> p n f", p=p)


def sb3(view, n):
    return view.rearrange("p (n f) -> p n f", n=n)



def emit_ln(k, tag):
    x, xn = k.t["x"], k.t["xn"]
    bc_mu, bc_rs = k.t["bc_mu"], k.t["bc_rs"]
    scr0 = k.t["scr0"]
    oc = k.t["ones_col"]
    bank = k.t["banks"]
    row = k.t["row"]
    mu_r = bc_mu[0:1, :]
    tmp_r = bc_rs[0:1, :]
    for nch in range(4):
        tsl = slice(nch * 512, (nch + 1) * 512)
        for d in range(NDT):
            sq = scr0[:, (d % 2) * 512:(d % 2) * 512 + 512]
            k.op("scalar", lambda e, d=d, tsl=tsl, sq=sq: e.activation(
                sq, xs(x, d, tsl), AF.Square), r=("x",), w=(f"scr0_{d % 2}",),
                tag=f"{tag}sq")
            k.mm(bank[4][0:1, :], oc, xs(x, d, tsl), d == 0, d == NDT - 1,
                 r=("x",), w=("bank4",), tag=f"{tag}sum")
            k.mm(bank[5][0:1, :], oc, sq, d == 0, d == NDT - 1,
                 r=(f"scr0_{d % 2}",), w=("bank5",), tag=f"{tag}ssq")
        k.op("scalar", lambda e, tsl=tsl: e.activation(
            mu_r[:, tsl], bank[4][0:1, :], AF.Copy, scale=1.0 / D),
            r=("bank4",), w=("bc_mu",), tag=f"{tag}mu")
        k.op("scalar", lambda e, tsl=tsl: e.activation(
            row[:, tsl], bank[5][0:1, :], AF.Copy, scale=1.0 / D),
            r=("bank5",), w=("row",), tag=f"{tag}msq")
    k.op("vector", lambda e: e.tensor_tensor(tmp_r, mu_r, mu_r, ALU.mult),
         r=("bc_mu",), w=("bc_rs",), tag=f"{tag}mu2")
    k.op("vector", lambda e: e.tensor_sub(row, row, tmp_r),
         r=("row", "bc_rs"), w=("row",), tag=f"{tag}var")
    k.op("scalar", lambda e: e.activation(row, row, AF.Ln, bias=k.kcol(1e-5, 1)),
         r=("row",), w=("row",), tag=f"{tag}ln")
    k.op("scalar", lambda e: e.activation(row, row, AF.Exp, scale=-0.5),
         r=("row",), w=("row",), tag=f"{tag}rstd")
    for nch in range(4):
        tsl = slice(nch * 512, (nch + 1) * 512)
        k.mm(bank[5], k.orow(0), row[:, tsl], True, True,
             r=("row",), w=("bank5",), tag=f"{tag}brs")
        k.op("vector", lambda e, tsl=tsl: e.tensor_copy(bc_rs[:, tsl], bank[5]),
             r=("bank5",), w=("bc_rs",), tag=f"{tag}brse")
        k.mm(bank[4], k.orow(0), mu_r[:, tsl], True, True,
             r=("bc_mu",), w=("bank4",), tag=f"{tag}bmu")
        k.op("vector", lambda e, tsl=tsl: e.tensor_copy(bc_mu[:, tsl], bank[4]),
             r=("bank4",), w=("bc_mu",), tag=f"{tag}bmue")
    for d in range(NDT):
        scr = scr0[:, 0:T]
        k.op("vector", lambda e, d=d, scr=scr: e.tensor_sub(scr, xs(x, d), bc_mu),
             r=("x", "bc_mu"), w=("scr0_0", "scr0_1", "scr0_2", "scr0_3"),
             tag=f"{tag}xsub")
        k.op("vector", lambda e, d=d, scr=scr: e.tensor_mul(xs(xn, d), scr, bc_rs),
             r=("scr0_0", "scr0_1", "scr0_2", "scr0_3", "bc_rs"), w=("xn",),
             tag=f"{tag}xmul")



def emit_attn(k, wqkv_name, wout_name, causal, eps, bias_qkv, bias_out, tag):
    x, xn = k.t["x"], k.t["xn"]
    oc = k.t["ones_col"]
    ocb = k.t["ones_col_b"]
    identb, caus01 = k.t["ident_b"], k.t["causal01"]
    bank = k.t["banks"]
    bank_b = k.t["bank_b"]
    scr0 = k.t["scr0"]
    rows2 = k.t["row"]
    Dh = 64
    scale = 1.0 / np.sqrt(Dh)
    wq = k.dram[wqkv_name]
    wo = k.dram[wout_name]

    k.push()
    kpair = [k.carve(f"{tag}kp{i}", [128, T], BF16) for i in range(2)]
    vpair = k.carve(f"{tag}vp", [128, T], BF16)
    qpair = [k.carve(f"{tag}qp{i}", [128, 1024], BF16) for i in range(2)]
    wpair = [k.carve(f"{tag}w0", [128, 3 * 1024], BF16),
             k.carve(f"{tag}w1", [128, 3 * 1024], BF16)]
    vtok = [k.carve(f"{tag}vt{i}", [128, 16 * 65], BF16) for i in range(2)]
    probs = [k.carve(f"{tag}pA", [128, 512], BF16),
             k.carve(f"{tag}pB", [128, 512], BF16)]
    zbuf = [k.carve(f"{tag}zb{i}", [128, 512], F32) for i in range(2)]
    ctx = k.carve(f"{tag}ctx", [128, NDT * 1024], BF16)
    wob = [k.carve(f"{tag}wo0", [128, 1024], BF16),
           k.carve(f"{tag}wo1", [128, 1024], BF16)]
    sqt = [k.carve(f"{tag}sq{i}", [128, 1024 + T], BF16) for i in range(2)]
    pcol = [k.carve(f"{tag}pc{i}", [128, 1], F32) for i in range(2)]
    cone = [k.carve(f"{tag}co{i}", [1, 8], F32) for i in range(2)]

    if bias_qkv is not None:
        bq = k.carve(f"{tag}bq", [128, 24], F32)
        bo = k.carve(f"{tag}bo", [128, 8], F32)
        k.dma("sync", bq, k.dram[bias_qkv], w=(f"{tag}bq",))
        k.dma("sync", bo, k.dram[bias_out], w=(f"{tag}bo",))
        bqs = k.carve(f"{tag}bqs", [128, 8], F32)
        k.op("vector", lambda e: e.tensor_scalar_mul(bqs, bq[:, 0:8], scale),
             r=(f"{tag}bq",), w=(f"{tag}bqs",), tag=f"{tag}bsc")

    for qh in range(2):
        q0 = qh * 1024
        kvT = T if (not causal or qh == 1) else 1024
        nkt = kvT // 128
        for hp in range(8):
            hpp = hp % 2
            wb = wpair[hpp]
            wbr = f"{tag}w{hpp}"
            kpb = kpair[hpp]
            kpr = f"{tag}kp{hpp}"
            qpb = qpair[hpp]
            qpr = f"{tag}qp{hpp}"
            sqb = sqt[hpp]
            sqr = f"{tag}sq{hpp}"
            for which in range(3):
                src = wq[:, which * 1024 + hp * 128: which * 1024 + (hp + 1) * 128]
                k.dma("gpsimd", sb3(wb[:, which * 1024:(which + 1) * 1024], 8),
                      fm3(src),
                      r=(wbr,), w=(wbr,), tag=f"{tag}wld")
            for which, dst, dstn, toks in ((0, qpb, qpr, (q0, q0 + 1024)),
                                           (1, kpb, kpr, (0, kvT)),
                                           (2, vpair, f"{tag}vp", (0, kvT))):
                for tc in range((toks[1] - toks[0]) // 512):
                    gsl = slice(toks[0] + tc * 512, toks[0] + (tc + 1) * 512)
                    lsl = slice(gsl.start - toks[0],
                                gsl.stop - toks[0]) if which == 0 else gsl
                    bk = bank[tc % 2]
                    bkr = f"bank{tc % 2}"
                    for d in range(NDT):
                        k.mm(bk, wb[:, which * 1024 + d * 128: which * 1024 + (d + 1) * 128],
                             xs(xn, d, gsl), d == 0, d == NDT - 1,
                             r=(wbr, "xn"), w=(bkr,), tag=f"{tag}qkvm")
                    if which == 0:
                        if bias_qkv is None:
                            k.op("scalar", lambda e, bk=bk, dst=dst, lsl=lsl: e.activation(
                                dst[:, lsl], bk, AF.Copy, scale=scale),
                                r=(bkr,), w=(dstn,), tag=f"{tag}qe")
                        else:
                            k.op("vector", lambda e, bk=bk, dst=dst, lsl=lsl, hp=hp: e.tensor_scalar(
                                dst[:, lsl], bk, scale, bqs[:, hp:hp + 1],
                                ALU.mult, ALU.add),
                                r=(bkr, f"{tag}bqs"), w=(dstn,), tag=f"{tag}qe")
                    else:
                        if bias_qkv is None:
                            k.op("scalar", lambda e, bk=bk, dst=dst, lsl=lsl: e.copy(
                                dst[:, lsl], bk), r=(bkr,), w=(dstn,), tag=f"{tag}kve")
                        else:
                            k.op("vector", lambda e, bk=bk, dst=dst, lsl=lsl, hp=hp,
                                 which=which: e.tensor_single_scalar(
                                dst[:, lsl], bk,
                                bq[:, which * 8 + hp:which * 8 + hp + 1], ALU.add),
                                r=(bkr, f"{tag}bq"), w=(dstn,), tag=f"{tag}kve")
            k.op("scalar", lambda e, qpb=qpb, sqb=sqb: e.activation(
                sqb[:, 0:1024], qpb, AF.Square),
                r=(qpr,), w=(f"{tag}sqq{hpp}",), tag=f"{tag}sqq")
            k.op("scalar", lambda e, kvT=kvT, kpb=kpb, sqb=sqb: e.activation(
                sqb[:, 1024:1024 + kvT], kpb[:, 0:kvT], AF.Square),
                r=(kpr,), w=(f"{tag}sqk{hpp}",), tag=f"{tag}sqk")
            for h2 in range(2):
                head = hp * 2 + h2
                po = h2 * 64
                co = cone[h2]
                cor = f"{tag}co{h2}"
                pc = pcol[h2]
                pcr = f"{tag}pc{h2}"
                for (ssl, srcres, oi) in ((slice(0, 1024), f"{tag}sqq{hpp}", 0),
                                          (slice(1024, 1024 + kvT), f"{tag}sqk{hpp}", 1)):
                    nrch = (ssl.stop - ssl.start) // 512
                    for nch in range(nrch):
                        tsl = slice(ssl.start + nch * 512, ssl.start + (nch + 1) * 512)
                        k.mm(bank[4][0:1, :], ocb[po:po + 64, :], sqb[po:po + 64, tsl],
                             True, True, r=(srcres,), w=("bank4",), tag=f"{tag}n2")
                        k.op("vector", lambda e, nch=nch: e.tensor_copy(
                            rows2[:, nch * 512:(nch + 1) * 512], bank[4][0:1, :]),
                            r=("bank4",), w=("row",), tag=f"{tag}n2e")
                    k.op("vector", lambda e, oi=oi, nrch=nrch, co=co: e.reduce_max(
                        co[:, oi:oi + 1], rows2[:, 0:nrch * 512], axis=AX.X),
                        r=("row",), w=(cor,), tag=f"{tag}nmax")
                k.op("scalar", lambda e, co=co: e.activation(
                    co[:, 0:1], co[:, 0:1], AF.Sqrt),
                    r=(cor,), w=(cor,), tag=f"{tag}sqr0")
                k.op("scalar", lambda e, co=co: e.activation(
                    co[:, 1:2], co[:, 1:2], AF.Sqrt),
                    r=(cor,), w=(cor,), tag=f"{tag}sqr1")
                k.op("vector", lambda e, co=co: e.tensor_tensor(
                    co[:, 2:3], co[:, 0:1], co[:, 1:2], ALU.mult),
                    r=(cor,), w=(cor,), tag=f"{tag}cmul")
                k.op("vector", lambda e, co=co: e.tensor_scalar_mul(
                    co[:, 2:3], co[:, 2:3], -1.0),
                    r=(cor,), w=(cor,), tag=f"{tag}cneg")
                k.mm(bank[3][:, 0:1], k.orow(0), co[0:1, 2:3], True, True,
                     r=(cor,), w=("bank3",), tag=f"{tag}cb")
                k.op("vector", lambda e, pc=pc: e.tensor_copy(pc, bank[3][:, 0:1]),
                     r=("bank3",), w=(pcr,), tag=f"{tag}cbe")
                vtd = vtok[h2]
                vtr = f"{tag}vt{h2}"
                k.op("gpsimd", lambda e, vtd=vtd: e.memset(vtd, 1.0), r=(),
                     w=(vtr,), tag=f"{tag}vms")
                for kt in range(nkt):
                    k.op("tensor", lambda e, kt=kt, po=po: e.transpose(
                        bank_b[:, 0:64], vpair[po:po + 64, kt * 128:(kt + 1) * 128],
                        identb[po:po + 64, po:po + 64]),
                        r=(f"{tag}vp",), w=("bank_b",), tag=f"{tag}vT")
                    k.op("vector", lambda e, kt=kt, vtd=vtd: e.tensor_copy(
                        vtd[:, kt * 65:kt * 65 + 64], bank_b[:, 0:64]),
                        r=("bank_b",), w=(vtr,), tag=f"{tag}vTe")
                for qc in range(2):
                    gq0 = q0 + qc * 512
                    ktm = nkt if not causal else min(nkt, (gq0 + 512) // 128)
                    for kt in range(ktm):
                        sb = bank[kt % 2]
                        sbr = f"bank{kt % 2}"
                        k.mm(sb, kpb[po:po + 64, kt * 128:(kt + 1) * 128],
                             qpb[po:po + 64, qc * 512:(qc + 1) * 512],
                             True, True, r=(kpr, qpr), w=(sbr,),
                             tag=f"{tag}sc")
                        pb = probs[kt % 2]
                        pbr = f"{tag}p{'A' if kt % 2 == 0 else 'B'}"
                        k.op("scalar", lambda e, sb=sb, pb=pb, pc=pc: e.activation(
                            pb, sb, AF.Exp, bias=pc),
                            r=(sbr, pcr), w=(pbr,), tag=f"{tag}exp")
                        if causal:
                            for j in range(4):
                                Qj = (gq0 + j * 128) // 128
                                if kt == Qj:
                                    k.op("vector", lambda e, pb=pb, j=j: e.tensor_mul(
                                        pb[:, j * 128:(j + 1) * 128],
                                        pb[:, j * 128:(j + 1) * 128], caus01),
                                        r=(pbr,), w=(pbr,), tag=f"{tag}diag")
                                elif kt > Qj:
                                    k.op("gpsimd", lambda e, pb=pb, j=j: e.memset(
                                        pb[:, j * 128:(j + 1) * 128], 0.0),
                                        r=(), w=(pbr,), tag=f"{tag}mz")
                        pvb = bank[2] if qc % 2 == 0 else bank[6]
                        pvbr = "bank2" if qc % 2 == 0 else "bank6"
                        k.mm(pvb[0:65, :], vtd[:, kt * 65:(kt + 1) * 65], pb,
                             kt == 0, kt == ktm - 1,
                             r=(vtr, pbr), w=(pvbr,), tag=f"{tag}pv")
                    pvb = bank[2] if qc % 2 == 0 else bank[6]
                    pvbr = "bank2" if qc % 2 == 0 else "bank6"
                    zbv = zbuf[qc % 2]
                    zbr = f"{tag}zb{qc % 2}"
                    k.op("vector", lambda e, pvb=pvb: e.tensor_copy(
                        rows2[:, 0:512], pvb[64:65, :]),
                        r=(pvbr,), w=("row",), tag=f"{tag}z")
                    if eps:
                        k.op("vector", lambda e: e.tensor_scalar_add(
                            rows2[:, 0:512], rows2[:, 0:512], eps),
                            r=("row",), w=("row",), tag=f"{tag}zeps")
                    k.op("vector", lambda e: e.reciprocal(rows2[:, 0:512],
                                                          rows2[:, 0:512]),
                         r=("row",), w=("row",), tag=f"{tag}zi")
                    k.mm(bank[3], k.orow(0), rows2[:, 0:512], True, True,
                         r=("row",), w=("bank3",), tag=f"{tag}zbm")
                    k.op("vector", lambda e, zbv=zbv: e.tensor_copy(zbv, bank[3]),
                         r=("bank3",), w=(zbr,), tag=f"{tag}zbe")
                    dd, rr = divmod(head * 64, 128)
                    k.op("vector", lambda e, dd=dd, rr=rr, qc=qc, pvb=pvb,
                         zbv=zbv: e.tensor_tensor(
                        ctx[rr:rr + 64, dd * 1024 + qc * 512: dd * 1024 + (qc + 1) * 512],
                        pvb[0:64, :], zbv[0:64, :], ALU.mult),
                        r=(pvbr, zbr), w=(f"{tag}ctx",), tag=f"{tag}ctxw")
        for fc in range(8):
            wbuf = wob[fc % 2]
            wbr = f"{tag}wo{fc % 2}"
            k.dma("gpsimd", sb3(wbuf, 8),
                  fm3(wo[:, fc * 128:(fc + 1) * 128]),
                  r=(wbr,), w=(wbr,), tag=f"{tag}wold")
            for tc in range(2):
                gsl = slice(q0 + tc * 512, q0 + (tc + 1) * 512)
                bk = bank[4 + tc % 2]
                bkr = f"bank{4 + tc % 2}"
                for d in range(NDT):
                    k.mm(bk, wbuf[:, d * 128:(d + 1) * 128],
                         ctx[:, d * 1024 + tc * 512: d * 1024 + (tc + 1) * 512],
                         d == 0, d == NDT - 1, r=(wbr, f"{tag}ctx"), w=(bkr,),
                         tag=f"{tag}opm")
                if bias_out is None:
                    k.op("vector", lambda e, bk=bk, fc=fc, gsl=gsl: e.tensor_add(
                        xs(x, fc, gsl), xs(x, fc, gsl), bk),
                        r=(bkr, "x"), w=("x",), tag=f"{tag}oadd")
                else:
                    k.op("vector", lambda e, bk=bk, fc=fc, gsl=gsl: e.scalar_tensor_tensor(
                        xs(x, fc, gsl), bk, bo[:, fc:fc + 1], xs(x, fc, gsl),
                        ALU.add, ALU.add),
                        r=(bkr, "x", f"{tag}bo"), w=("x",), tag=f"{tag}oaddb")
    k.pop(f"{tag}done")



def emit_l0_ffn(k):
    x, xn = k.t["x"], k.t["xn"]
    oc = k.t["ones_col"]
    bank = k.t["banks"]
    scr0 = k.t["scr0"]
    col0 = k.t["col0"]
    one1 = [k.t[f"one{i}"] for i in range(3)]
    row = k.t["row"]
    bc_mu, bc_rs = k.t["bc_mu"], k.t["bc_rs"]
    ocb = k.t["ones_col_b"]

    k.push()
    fw = [k.carve("fw0", [128, 4096], F32), k.carve("fw1", [128, 4096], F32)]
    fq = k.carve("fqz", [128, 4096], BF16)
    facc = k.carve("facc", [128, 1], F32)
    sgn = k.carve("sgn", [128, 4096], F32)
    for wnm, qnm, nrow, ncol in (("l0_wfc_t", "wq_fc", D, 4096),
                                 ("l0_wproj_t", "wq_proj", 4096, D)):
        wd = k.dram[wnm]
        qd = k.dram[qnm]
        ntile = nrow // 128
        for i in range(ntile):
            buf = fw[i % 2]
            bnm = f"fw{i % 2}"
            k.dma("sync", buf[:, 0:ncol], wd[i * 128:(i + 1) * 128, :],
                  r=(bnm,), w=(bnm,), tag="q1ld")
            k.op("vector", lambda e, buf=buf, ncol=ncol: e.reduce_sum(
                col0, buf[:, 0:ncol], axis=AX.X, apply_absolute_value=True),
                r=(bnm,), w=("col0",), tag="q1abs")
            if i == 0:
                k.op("vector", lambda e: e.tensor_copy(facc, col0),
                     r=("col0",), w=("facc",), tag="q1cp")
            else:
                k.op("vector", lambda e: e.tensor_add(facc, facc, col0),
                     r=("col0", "facc"), w=("facc",), tag="q1acc")
        k.mm(bank[6][0:1, 0:1], oc, facc, True, True,
             r=("facc",), w=("bank6",), tag="q1red")
        nel = float(nrow * ncol)
        k.op("scalar", lambda e, nel=nel: e.activation(
            one1[0], bank[6][0:1, 0:1], AF.Copy, scale=0.5 / nel, bias=0.5e-6),
            r=("bank6",), w=("one0",), tag="q1th")
        k.mm(bank[6][:, 0:1], k.orow(0), one1[0], True, True,
             r=("one0",), w=("bank6",), tag="q1thb")
        k.op("vector", lambda e: e.tensor_copy(col0, bank[6][:, 0:1]),
             r=("bank6",), w=("col0",), tag="q1the")
        for i in range(ntile):
            buf = fw[i % 2]
            bnm = f"fw{i % 2}"
            k.dma("sync", buf[:, 0:ncol], wd[i * 128:(i + 1) * 128, :],
                  r=(bnm,), w=(bnm,), tag="q2ld")
            k.op("scalar", lambda e, buf=buf, ncol=ncol: e.activation(
                sgn[:, 0:ncol], buf[:, 0:ncol], AF.Sign),
                r=(bnm,), w=("sgn",), tag="q2sgn")
            k.op("scalar", lambda e, buf=buf, ncol=ncol: e.activation(
                buf[:, 0:ncol], buf[:, 0:ncol], AF.Abs),
                r=(bnm,), w=(bnm,), tag="q2abs")
            k.op("vector", lambda e, buf=buf, ncol=ncol: e.tensor_single_scalar(
                buf[:, 0:ncol], buf[:, 0:ncol], col0, ALU.is_gt),
                r=(bnm, "col0"), w=(bnm,), tag="q2cmp")
            k.op("vector", lambda e, buf=buf, ncol=ncol: e.tensor_mul(
                fq[:, 0:ncol], sgn[:, 0:ncol], buf[:, 0:ncol]),
                r=(bnm, "sgn"), w=("fqz",), tag="q2mul")
            k.dma("sync", qd[i * 128:(i + 1) * 128, :], fq[:, 0:ncol],
                  r=("fqz",), w=(qnm,), tag="q2st")
    k.pop("q_done")

    qfc = k.dram["wq_fc"]
    qpr = k.dram["wq_proj"]
    lnw_d, lnb_d = k.dram["l0_lnw"], k.dram["l0_lnb"]
    k.push()
    g32 = k.carve("g32", [128, 32 * 512], BF16)
    pw = [k.carve("pw0", [128, 1024], BF16), k.carve("pw1", [128, 1024], BF16)]
    hw = [k.carve("hw0", [128, 1024], BF16), k.carve("hw1", [128, 1024], BF16)]
    lnw = k.carve("lnw", [128, 32], F32)
    lnb = k.carve("lnb", [128, 32], F32)
    k.dma("sync", lnw, lnw_d, w=("lnw",))
    k.dma("sync", lnb, lnb_d, w=("lnb",))
    for tq in range(4):
        tsl = slice(tq * 512, (tq + 1) * 512)
        for fc in range(32):
            wbuf = hw[fc % 2]
            wbr = f"hw{fc % 2}"
            k.dma("sync", sb3(wbuf, 8),
                  fm3(qfc[:, fc * 128:(fc + 1) * 128]),
                  r=(wbr,), w=(wbr,), tag="hld")
            bk = bank[fc % 2]
            bkr = f"bank{fc % 2}"
            for d in range(NDT):
                k.mm(bk, wbuf[:, d * 128:(d + 1) * 128], xs(xn, d, tsl),
                     d == 0, d == NDT - 1, r=(wbr, "xn"), w=(bkr,), tag="hmm")
            k.op("scalar", lambda e, bk=bk, fc=fc: e.activation(
                g32[:, fc * 512:(fc + 1) * 512], bk, AF.Gelu),
                r=(bkr,), w=("g32",), tag="hgelu")
        for fc in range(32):
            gv = g32[:, fc * 512:(fc + 1) * 512]
            k.mm(bank[4][0:1, :], ocb, gv, fc == 0, fc == 31,
                 r=("g32",), w=("bank4",), tag="lns")
            sq = scr0[:, (fc % 2) * 512:(fc % 2) * 512 + 512]
            k.op("scalar", lambda e, gv=gv, sq=sq: e.activation(
                sq, gv, AF.Square), r=("g32",), w=(f"scr0_{fc % 2}",), tag="lnsq")
            k.mm(bank[5][0:1, :], oc, sq, fc == 0, fc == 31,
                 r=(f"scr0_{fc % 2}",), w=("bank5",), tag="lnss")
        rsl = slice(0, 512)
        mu_r = bc_mu[0:1, 0:512]
        tmp_r = bc_rs[0:1, 0:512]
        rr = row[:, rsl]
        k.op("scalar", lambda e: e.activation(mu_r, bank[4][0:1, :],
                                              AF.Copy, scale=1.0 / 4096),
             r=("bank4",), w=("bc_mu",), tag="lnmu")
        k.op("scalar", lambda e: e.activation(rr, bank[5][0:1, :],
                                              AF.Copy, scale=1.0 / 4096),
             r=("bank5",), w=("row",), tag="lnms")
        k.op("vector", lambda e: e.tensor_tensor(tmp_r, mu_r, mu_r, ALU.mult),
             r=("bc_mu",), w=("bc_rs",), tag="lnm2")
        k.op("vector", lambda e: e.tensor_sub(rr, rr, tmp_r),
             r=("row", "bc_rs"), w=("row",), tag="lnvr")
        k.op("scalar", lambda e: e.activation(rr, rr, AF.Ln, bias=k.kcol(1e-5, 1)),
             r=("row",), w=("row",), tag="lnl")
        k.op("scalar", lambda e: e.activation(rr, rr, AF.Exp, scale=-0.5),
             r=("row",), w=("row",), tag="lnr")
        k.mm(bank[5], k.orow(0), rr, True, True, r=("row",),
             w=("bank5",), tag="lbr")
        k.op("vector", lambda e: e.tensor_copy(bc_rs[:, 0:512], bank[5]),
             r=("bank5",), w=("bc_rs",), tag="lbre")
        k.mm(bank[4], k.orow(0), mu_r, True, True, r=("bc_mu",),
             w=("bank4",), tag="lbm")
        k.op("vector", lambda e: e.tensor_copy(bc_mu[:, 0:512], bank[4]),
             r=("bank4",), w=("bc_mu",), tag="lbme")
        for fc in range(32):
            gv = g32[:, fc * 512:(fc + 1) * 512]
            sq = scr0[:, (fc % 2) * 512:(fc % 2) * 512 + 512]
            k.op("vector", lambda e, gv=gv, sq=sq: e.tensor_sub(sq, gv, bc_mu[:, 0:512]),
                 r=("g32", "bc_mu"), w=(f"scr0_{fc % 2}",), tag="lhs")
            k.op("vector", lambda e, sq=sq: e.tensor_mul(sq, sq, bc_rs[:, 0:512]),
                 r=(f"scr0_{fc % 2}", "bc_rs"), w=(f"scr0_{fc % 2}",), tag="lhm")
            k.op("vector", lambda e, gv=gv, sq=sq, fc=fc: e.tensor_scalar(
                gv, sq, lnw[:, fc:fc + 1], lnb[:, fc:fc + 1], ALU.mult, ALU.add),
                r=(f"scr0_{fc % 2}", "lnw", "lnb"), w=("g32",), tag="lhwb")
        for fo in range(8):
            wbuf = pw[fo % 2]
            wbr = f"pw{fo % 2}"
            for hh in range(4):
                k.dma("sync", sb3(wbuf, 8),
                      fm3(qpr[hh * 1024:(hh + 1) * 1024, fo * 128:(fo + 1) * 128]),
                      r=(wbr,), w=(wbr,), tag="pld")
                bk = bank[2 + fo % 2]
                bkr = f"bank{2 + fo % 2}"
                for d8 in range(8):
                    hc = hh * 8 + d8
                    k.mm(bk, wbuf[:, d8 * 128:(d8 + 1) * 128],
                         g32[:, hc * 512:(hc + 1) * 512],
                         hc == 0, hc == 31, r=(wbr, "g32"), w=(bkr,), tag="pmm")
            bk = bank[2 + fo % 2]
            bkr = f"bank{2 + fo % 2}"
            k.op("vector", lambda e, bk=bk, fo=fo, tsl=tsl: e.tensor_add(
                xs(x, fo, tsl), xs(x, fo, tsl), bk),
                r=(bkr, "x"), w=("x",), tag="padd")
    k.pop("ffn_done")


def emit_layer0(k):
    emit_ln(k, "l0ln_")
    emit_attn(k, "l0_wqkv_t", "l0_wout_t", causal=False, eps=1e-6,
              bias_qkv=None, bias_out=None, tag="l0a_")
    emit_l0_ffn(k)


L1_PARTS = "am"


def emit_layer1(k):
    emit_ln(k, "l1ln_")
    if "a" in L1_PARTS:
        emit_attn(k, "l1_wqkv", "l1_wout", causal=True, eps=0.0,
                  bias_qkv="l1_bqkv", bias_out="l1_bout", tag="l1a_")
    if "m" in L1_PARTS:
        emit_l1_moe(k)


def emit_l1_moe(k):
    x, xn = k.t["x"], k.t["xn"]
    bank = k.t["banks"]
    bank_b = k.t["bank_b"]
    identb = k.t["ident_b"]
    col0 = k.t["col0"]
    w1d = k.dram["l1_w1"]
    w2d = k.dram["l1_w2"]

    k.push()
    act = k.carve("m_act", [128, 8 * T], BF16)
    w1b = [k.carve(f"m_w1{i}", [128, 2 * 2048], BF16) for i in range(2)]
    w2b = [k.carve(f"m_w2{i}", [128, 1024], BF16) for i in range(2)]
    bcm = k.carve("m_bcm", [128, T], F32)
    lgt = k.carve("m_lgt", [128, 16 * 8], F32)
    mrow = k.carve("m_mrow", [8, T], F32)
    mv1 = k.t["row"]
    rw = k.carve("m_rw", [128, 64], BF16)
    rb = k.carve("m_rb", [128, 1], F32)
    mscr = k.carve("m_scr", [128, 8], F32)
    sb16 = k.carve("m_sb16", [128, 512], BF16)

    k.dma("gpsimd", sb3(rw, 8), fm3(k.dram["l1_rw"]), r=("m_rw",), w=("m_rw",),
          tag="rt_ld")
    k.dma("sync", rb, k.dram["l1_rb"], w=("m_rb",), tag="rt_lb")
    for tc in range(4):
        tsl = slice(tc * 512, (tc + 1) * 512)
        for d in range(NDT):
            k.mm(bank[0][0:8, :], rw[:, d * 8:(d + 1) * 8], xs(xn, d, tsl),
                 d == 0, d == NDT - 1, r=("m_rw", "xn"), w=("bank0",), tag="rt_mm")
        k.op("vector", lambda e, tsl=tsl: e.tensor_single_scalar(
            k.t["scr0"][0:8, 0:512], bank[0][0:8, :], rb[0:8, :], ALU.add),
            r=("bank0", "m_rb"), w=("scr0_0",), tag="rt_b")
        for c in range(4):
            ch = tc * 4 + c
            k.op("tensor", lambda e, c=c: e.transpose(
                bank[5][:, 0:8],
                k.t["scr0"][0:8, c * 128:(c + 1) * 128], k.t["ident_f"][0:8, 0:8]),
                r=("scr0_0", "ident_f"), w=("bank5",), tag="rt_T")
            k.op("vector", lambda e, ch=ch: e.tensor_copy(
                lgt[:, ch * 8:(ch + 1) * 8], bank[5][:, 0:8]),
                r=("bank5",), w=("m_lgt",), tag="rt_Te")
    for ch in range(16):
        lg = lgt[:, ch * 8:(ch + 1) * 8]
        k.op("vector", lambda e, lg=lg: e.reduce_max(col0, lg, axis=AX.X),
             r=("m_lgt",), w=("col0",), tag="mx")
        k.op("vector", lambda e, lg=lg: e.tensor_single_scalar(
            mscr, lg, col0, ALU.subtract), r=("m_lgt", "col0"), w=("m_scr",),
            tag="msub")
        k.op("scalar", lambda e: e.activation(mscr, mscr, AF.Exp),
             r=("m_scr",), w=("m_scr",), tag="mexp")
        k.op("vector", lambda e: e.reduce_sum(col0, mscr, axis=AX.X),
             r=("m_scr",), w=("col0",), tag="msum")
        k.op("vector", lambda e: e.reciprocal(col0, col0),
             r=("col0",), w=("col0",), tag="mrec")
        k.op("vector", lambda e, lg=lg: e.tensor_single_scalar(
            mscr, mscr, 1.0, ALU.is_ge), r=("m_scr",), w=("m_scr",), tag="mge")
        k.op("vector", lambda e: e.tensor_single_scalar(
            mscr, mscr, col0, ALU.mult), r=("m_scr", "col0"), w=("m_scr",),
            tag="mmp")
        k.op("tensor", lambda e, ch=ch: e.transpose(
            bank[5][0:8, 0:128], mscr, k.t["ident_f"]),
            r=("m_scr", "ident_f"), w=("bank5",), tag="mT")
        k.op("vector", lambda e, ch=ch: e.tensor_copy(
            mrow[:, ch * 128:(ch + 1) * 128], bank[5][0:8, 0:128]),
            r=("bank5",), w=("m_mrow",), tag="mTe")

    for e_i in range(8):
        k.dma("gpsimd", mv1, mrow[e_i:e_i + 1, :], r=("m_mrow", "row"),
              w=("row",), tag="mvst")
        for c4 in range(4):
            tsl = slice(c4 * 512, (c4 + 1) * 512)
            k.mm(bank[6], k.orow(0), mv1[:, tsl], True, True,
                 r=("row",), w=("bank6",), tag="mvb")
            k.op("vector", lambda e, tsl=tsl: e.tensor_copy(bcm[:, tsl], bank[6]),
                 r=("bank6",), w=("m_bcm",), tag="mvbe")
        for jg in range(4):
            for j4 in range(4):
                j = jg * 4 + j4
                wb = w1b[j % 2]
                wbr = f"m_w1{j % 2}"
                k.dma("gpsimd", sb3(wb[:, 0:2048], 8),
                      fm3(w1d[e_i, :, j * 256:(j + 1) * 256]),
                      r=(wbr,), w=(wbr,), tag="w1la")
                k.dma("gpsimd", sb3(wb[:, 2048:4096], 8),
                      fm3(w1d[e_i, :, 4096 + j * 256: 4096 + (j + 1) * 256]),
                      r=(wbr,), w=(wbr,), tag="w1lg")
                for tc in range(4):
                    tsl = slice(tc * 512, (tc + 1) * 512)
                    for ms in range(2):
                        ba, bg = bank[tc % 2], bank[2 + tc % 2]
                        bar, bgr = f"bank{tc % 2}", f"bank{2 + tc % 2}"
                        for d in range(NDT):
                            k.mm(ba, wb[:, d * 256 + ms * 128: d * 256 + (ms + 1) * 128],
                                 xs(xn, d, tsl), d == 0, d == NDT - 1,
                                 r=(wbr, "xn"), w=(bar,), tag="amm")
                        for d in range(NDT):
                            k.mm(bg, wb[:, 2048 + d * 256 + ms * 128: 2048 + d * 256 + (ms + 1) * 128],
                                 xs(xn, d, tsl), d == 0, d == NDT - 1,
                                 r=(wbr, "xn"), w=(bgr,), tag="gmm")
                        k.op("scalar", lambda e, ba=ba: e.activation(
                            sb16, ba, AF.Silu), r=(bar,), w=("m_sb16",),
                            tag="asilu")
                        ht = j4 * 2 + ms
                        k.op("vector", lambda e, bg=bg, ht=ht, tc=tc: e.tensor_mul(
                            act[:, ht * T + tc * 512: ht * T + (tc + 1) * 512],
                            sb16, bg), r=("m_sb16", bgr), w=("m_act",),
                            tag="actw")
            for fo in range(8):
                wb = w2b[fo % 2]
                wbr = f"m_w2{fo % 2}"
                k.dma("gpsimd", sb3(wb, 8),
                      fm3(w2d[e_i, jg * 1024:(jg + 1) * 1024, fo * 128:(fo + 1) * 128]),
                      r=(wbr,), w=(wbr,), tag="w2ld")
                for tc in range(4):
                    ybk = bank[4 + tc % 2]
                    ybr = f"bank{4 + tc % 2}"
                    ysc = k.t["scr0"][:, (tc % 2) * 512:(tc % 2) * 512 + 512]
                    for ht in range(8):
                        k.mm(ybk,
                             wb[:, ht * 128:(ht + 1) * 128],
                             act[:, ht * T + tc * 512: ht * T + (tc + 1) * 512],
                             ht == 0, ht == 7, r=(wbr, "m_act"),
                             w=(ybr,), tag="ymm")
                    k.op("vector", lambda e, tc=tc, ybk=ybk, ysc=ysc: e.tensor_mul(
                        ysc, ybk, bcm[:, tc * 512:(tc + 1) * 512]),
                        r=(ybr, "m_bcm"), w=(f"scr0_{tc % 2}",), tag="ymask")
                    k.op("vector", lambda e, fo=fo, tc=tc, ysc=ysc: e.tensor_add(
                        xs(x, fo, slice(tc * 512, (tc + 1) * 512)),
                        xs(x, fo, slice(tc * 512, (tc + 1) * 512)), ysc),
                        r=(f"scr0_{tc % 2}", "x"), w=("x",), tag="yadd")
    k.pop("moe_done")


def emit_rms_bcast(k, src_tiles_fn, nd, eps, bc_dst, bcres, tag):
    bank = k.t["banks"]
    scr0 = k.t["scr0"]
    row = k.t["row"]
    oc = k.t["ones_col"]
    for nch in range(4):
        tsl = slice(nch * 512, (nch + 1) * 512)
        for d in range(nd):
            src, srcres = src_tiles_fn(d, tsl)
            sq = scr0[:, (d % 2) * 512:(d % 2) * 512 + 512]
            k.op("scalar", lambda e, src=src, sq=sq: e.activation(
                sq, src, AF.Square), r=(srcres,), w=(f"scr0_{d % 2}",),
                tag=f"{tag}sq")
            k.mm(bank[5][0:1, :], oc, sq, d == 0, d == nd - 1,
                 r=(f"scr0_{d % 2}",), w=("bank5",), tag=f"{tag}ssq")
        k.op("scalar", lambda e, tsl=tsl: e.activation(
            row[:, tsl], bank[5][0:1, :], AF.Copy, scale=1.0 / (nd * 128)),
            r=("bank5",), w=("row",), tag=f"{tag}ms")
    k.op("scalar", lambda e: e.activation(row, row, AF.Ln, bias=k.kcol(eps, 1)),
         r=("row",), w=("row",), tag=f"{tag}ln")
    k.op("scalar", lambda e: e.activation(row, row, AF.Exp, scale=-0.5),
         r=("row",), w=("row",), tag=f"{tag}ex")
    for nch in range(4):
        tsl = slice(nch * 512, (nch + 1) * 512)
        k.mm(bank[5], k.orow(0), row[:, tsl], True, True,
             r=("row",), w=("bank5",), tag=f"{tag}bm")
        k.op("vector", lambda e, tsl=tsl: e.tensor_copy(bc_dst[:, tsl], bank[5]),
             r=("bank5",), w=(bcres,), tag=f"{tag}be")


def emit_l2_ffn(k):
    x, xn = k.t["x"], k.t["xn"]
    bank = k.t["banks"]

    k.push()
    fwb = [k.carve("f2w0", [128, 4096], F32), k.carve("f2w1", [128, 4096], F32)]
    fqz = k.carve("f2q", [128, 4096], BF16)
    a1 = k.carve("f2a1", [128, 32], F32)
    a2 = k.carve("f2a2", [128, 8], F32)
    b1 = k.carve("f2b1", [128, 32], F32)
    b2 = k.carve("f2b2", [128, 8], F32)
    k.dma("sync", b1, k.dram["l2_fb1"], w=("f2b1",))
    k.dma("sync", b2, k.dram["l2_fb2"], w=("f2b2",))
    for nm, dst, dstres, ntile, ncol in (("l2_fw1n", a1, "f2a1", 32, 1024),
                                         ("l2_fw2n", a2, "f2a2", 8, 4096)):
        wd = k.dram[nm]
        for i in range(ntile):
            buf = fwb[i % 2]
            bnm = f"f2w{i % 2}"
            k.dma("sync", buf[:, 0:ncol], wd[i * 128:(i + 1) * 128, :],
                  r=(bnm,), w=(bnm,), tag="a_ld")
            k.op("vector", lambda e, buf=buf, dst=dst, i=i, ncol=ncol: e.reduce_sum(
                dst[:, i:i + 1], buf[:, 0:ncol], axis=AX.X,
                apply_absolute_value=True),
                r=(bnm,), w=(dstres,), tag="a_red")
        k.op("vector", lambda e, dst=dst, ntile=ntile, ncol=ncol: e.tensor_scalar_mul(
            dst[:, 0:ntile], dst[:, 0:ntile], 1.0 / ncol),
            r=(dstres,), w=(dstres,), tag="a_sc")
    for nm, qnm, ntile, ncol in (("l2_fw1t", "sgn_w1", 8, 4096),
                                 ("l2_fw2t", "sgn_w2", 32, 1024)):
        wd, qd = k.dram[nm], k.dram[qnm]
        for i in range(ntile):
            buf = fwb[i % 2]
            bnm = f"f2w{i % 2}"
            k.dma("sync", buf[:, 0:ncol], wd[i * 128:(i + 1) * 128, :],
                  r=(bnm,), w=(bnm,), tag="s_ld")
            k.op("scalar", lambda e, buf=buf, ncol=ncol: e.activation(
                fqz[:, 0:ncol], buf[:, 0:ncol], AF.Sign),
                r=(bnm,), w=("f2q",), tag="s_sgn")
            k.dma("sync", qd[i * 128:(i + 1) * 128, :], fqz[:, 0:ncol],
                  r=("f2q",), w=(qnm,), tag="s_st")
    k.pop("f2q_done")

    sg1 = k.dram["sgn_w1"]
    sg2 = k.dram["sgn_w2"]
    k.push()
    hbuf = k.carve("f2h", [128, 32 * 512], BF16)
    hw = [k.carve("f2hw0", [128, 1024], BF16), k.carve("f2hw1", [128, 1024], BF16)]
    for tq in range(4):
        tsl = slice(tq * 512, (tq + 1) * 512)
        for fc in range(32):
            wbuf = hw[fc % 2]
            wbr = f"f2hw{fc % 2}"
            k.dma("sync", sb3(wbuf, 8), fm3(sg1[:, fc * 128:(fc + 1) * 128]),
                  r=(wbr,), w=(wbr,), tag="f2hld")
            bk = bank[fc % 2]
            bkr = f"bank{fc % 2}"
            for d in range(NDT):
                k.mm(bk, wbuf[:, d * 128:(d + 1) * 128], xs(xn, d, tsl),
                     d == 0, d == NDT - 1, r=(wbr, "xn"), w=(bkr,), tag="f2hmm")
            k.op("scalar", lambda e, bk=bk, fc=fc: e.activation(
                hbuf[:, fc * 512:(fc + 1) * 512], bk, AF.Gelu,
                scale=a1[:, fc:fc + 1], bias=b1[:, fc:fc + 1]),
                r=(bkr, "f2a1", "f2b1"), w=("f2h",), tag="f2gelu")
        for fo in range(8):
            fbk = bank[2 + fo % 2]
            fbr = f"bank{2 + fo % 2}"
            fsc = k.t["scr0"][:, (fo % 2) * 512:(fo % 2) * 512 + 512]
            for hh in range(4):
                wbuf = hw[hh % 2]
                wbr = f"f2hw{hh % 2}"
                k.dma("sync", sb3(wbuf, 8),
                      fm3(sg2[hh * 1024:(hh + 1) * 1024, fo * 128:(fo + 1) * 128]),
                      r=(wbr,), w=(wbr,), tag="f2pld")
                for d8 in range(8):
                    hc = hh * 8 + d8
                    k.mm(fbk, wbuf[:, d8 * 128:(d8 + 1) * 128],
                         hbuf[:, hc * 512:(hc + 1) * 512],
                         hc == 0, hc == 31, r=(wbr, "f2h"), w=(fbr,),
                         tag="f2pmm")
            k.op("vector", lambda e, fo=fo, fbk=fbk, fsc=fsc: e.tensor_scalar(
                fsc, fbk, a2[:, fo:fo + 1], b2[:, fo:fo + 1],
                ALU.mult, ALU.add),
                r=(fbr, "f2a2", "f2b2"), w=(f"scr0_{fo % 2}",), tag="f2ev")
            k.op("vector", lambda e, fo=fo, tsl=tsl, fsc=fsc: e.tensor_add(
                xs(x, fo, tsl), xs(x, fo, tsl), fsc),
                r=(f"scr0_{fo % 2}", "x"), w=("x",), tag="f2add")
    k.pop("f2_done")


L2_STAGE = 99


def emit_layer2(k):
    emit_ln(k, "l2ln_")
    if L2_STAGE >= 1:
        emit_l2_ffn(k)
    if L2_STAGE >= 2:
        emit_l2_attn(k)


def emit_l2_attn(k):
    x, xn = k.t["x"], k.t["xn"]
    bank = k.t["banks"]
    scr0 = k.t["scr0"]
    identf = k.t["ident_f"]

    k.push()
    rwidx = k.carve("l2idx", [128, 16 * 4], F32)
    featsA = k.t["bc_mu"]
    featsC = k.t["bc_rs"]

    k.push()
    ug = k.carve("l2u", [128, NDT * T], BF16)

    k.push()
    bc2 = k.carve("l2bc2", [128, T], F32)
    gb = k.carve("l2gb", [128, 8], F32)
    gwb = [k.carve("l2gw0", [128, 1024], BF16), k.carve("l2gw1", [128, 1024], BF16)]
    emit_rms_bcast(k, lambda d, tsl: (xs(xn, d, tsl), "xn"), NDT, 1e-6,
                   bc2, "l2bc2", "l2rms_")
    k.dma("sync", gb, k.dram["l2_gate_b"], w=("l2gb",))
    gwt = k.dram["l2_gate_wt"]
    for fc in range(NDT):
        wbuf = gwb[fc % 2]
        wbr = f"l2gw{fc % 2}"
        k.dma("gpsimd", sb3(wbuf, 8), fm3(gwt[:, fc * 128:(fc + 1) * 128]),
              r=(wbr,), w=(wbr,), tag="l2gld")
        for tc in range(4):
            tsl = slice(tc * 512, (tc + 1) * 512)
            bk = bank[tc % 2]
            bkr = f"bank{tc % 2}"
            for d in range(NDT):
                k.mm(bk, wbuf[:, d * 128:(d + 1) * 128], xs(xn, d, tsl),
                     d == 0, d == NDT - 1, r=(wbr, "xn"), w=(bkr,), tag="l2gmm")
            k.op("vector", lambda e, bk=bk, tsl=tsl: e.tensor_mul(
                scr0[:, 0:512], bk, bc2[:, tsl]),
                r=(bkr, "l2bc2"), w=("scr0_0",), tag="l2gsc")
            k.op("scalar", lambda e, fc=fc, tsl=tsl: e.activation(
                xs(ug, fc, tsl), scr0[:, 0:512], AF.Sigmoid,
                bias=gb[:, fc:fc + 1]),
                r=("scr0_0", "l2gb"), w=("l2u",), tag="l2gsig")
    cw = k.carve("l2cw", [128, 32], F32)
    k.dma("sync", cw, k.dram["l2_conv"], w=("l2cw",))
    x2t = k.carve("l2x2", [128, T], F32)
    hcv = k.carve("l2hc", [128, T], F32)
    for d in range(NDT):
        k.op("vector", lambda e, d=d: e.tensor_mul(x2t, xs(xn, d), bc2),
             r=("xn", "l2bc2"), w=("l2x2",), tag="l2x2")
        k.op("vector", lambda e, d=d: e.tensor_single_scalar(
            hcv, x2t, cw[:, d * 4 + 3: d * 4 + 4], ALU.mult),
            r=("l2x2", "l2cw"), w=("l2hc",), tag="l2cv3")
        for jj in range(3):
            sft = 3 - jj
            k.op("vector", lambda e, d=d, jj=jj, sft=sft: e.scalar_tensor_tensor(
                hcv[:, sft:T], x2t[:, 0:T - sft],
                cw[:, d * 4 + jj: d * 4 + jj + 1],
                hcv[:, sft:T], ALU.mult, ALU.add),
                r=("l2x2", "l2cw", "l2hc"), w=("l2hc",), tag="l2cvj")
        k.op("scalar", lambda e: e.activation(x2t, hcv, AF.Silu),
             r=("l2hc",), w=("l2x2",), tag="l2silu")
        k.op("vector", lambda e, d=d: e.tensor_mul(xs(ug, d), x2t, xs(ug, d)),
             r=("l2x2", "l2u"), w=("l2u",), tag="l2umul")
    k.pop("l2front_done")
    if L2_STAGE < 3:
        return

    k.push()
    swb = k.carve("l2sw", [128, 8 * 320], BF16)
    k.dma("gpsimd", sb3(swb, 8), fm3(k.dram["l2_smallw"]), r=("l2sw",),
          w=("l2sw",), tag="l2swld")
    for tc in range(4):
        tsl = slice(tc * 512, (tc + 1) * 512)
        for d in range(NDT):
            k.mm(bank[0], swb[:, d * 320:d * 320 + 128], xs(ug, d, tsl),
                 d == 0, d == NDT - 1, r=("l2sw", "l2u"), w=("bank0",),
                 tag="l2fAm")
        k.op("vector", lambda e, tsl=tsl: e.tensor_copy(featsA[:, tsl], bank[0]),
             r=("bank0",), w=("bc_mu",), tag="l2fAe")
        for d in range(NDT):
            k.mm(bank[1][0:64, :], swb[:, d * 320 + 128:d * 320 + 192],
                 xs(ug, d, tsl), d == 0, d == NDT - 1,
                 r=("l2sw", "l2u"), w=("bank1",), tag="l2fBm")
        k.op("vector", lambda e: e.tensor_single_scalar(
            scr0[0:64, 0:512], bank[1][0:64, :], 0.0, ALU.is_gt),
            r=("bank1",), w=("scr0_0",), tag="l2bits")
        k.mm(bank[2][0:2, :], k.t["pow2"][0:18, 0:2], scr0[0:18, 0:512],
             True, True, r=("scr0_0", "pow2"), w=("bank2",), tag="l2ri")
        k.mm(bank[3][0:2, :], k.t["pow2"][32:50, 2:4], scr0[32:50, 0:512],
             True, True, r=("scr0_0", "pow2"), w=("bank3",), tag="l2wi")
        k.op("vector", lambda e: e.tensor_copy(scr0[0:2, 512:1024], bank[2][0:2, :]),
             r=("bank2",), w=("scr0_1",), tag="l2riE")
        k.op("vector", lambda e: e.tensor_copy(scr0[0:2, 1024:1536], bank[3][0:2, :]),
             r=("bank3",), w=("scr0_1",), tag="l2wiE")
        for c in range(4):
            ch = tc * 4 + c
            k.op("tensor", lambda e, c=c: e.transpose(
                bank[4][:, 0:2], scr0[0:2, 512 + c * 128:512 + (c + 1) * 128],
                identf[0:2, 0:2]),
                r=("scr0_1", "ident_f"), w=("bank4",), tag="l2riT")
            k.op("vector", lambda e, ch=ch: e.tensor_copy(
                rwidx[:, ch * 4:ch * 4 + 2], bank[4][:, 0:2]),
                r=("bank4",), w=("l2idx",), tag="l2riTe")
            k.op("tensor", lambda e, c=c: e.transpose(
                bank[4][:, 2:4], scr0[0:2, 1024 + c * 128:1024 + (c + 1) * 128],
                identf[0:2, 0:2]),
                r=("scr0_1", "ident_f"), w=("bank4",), tag="l2wiT")
            k.op("vector", lambda e, ch=ch: e.tensor_copy(
                rwidx[:, ch * 4 + 2:ch * 4 + 4], bank[4][:, 2:4]),
                r=("bank4",), w=("l2idx",), tag="l2wiTe")
        for d in range(NDT):
            k.mm(bank[5], swb[:, d * 320 + 192:d * 320 + 320], xs(ug, d, tsl),
                 d == 0, d == NDT - 1, r=("l2sw", "l2u"), w=("bank5",),
                 tag="l2fCm")
        k.op("vector", lambda e, tsl=tsl: e.tensor_copy(featsC[:, tsl], bank[5]),
             r=("bank5",), w=("bc_rs",), tag="l2fCe")
    fcb = k.carve("l2fcb", [128, 1], F32)
    k.dma("sync", fcb, k.dram["l2_fcb"], w=("l2fcb",))
    for base in (0, 32, 64):
        k.op("scalar", lambda e, base=base: e.activation(
            featsC[base:base + 1, :], featsC[base:base + 1, :], AF.Sigmoid,
            bias=fcb[base:base + 1, :]),
            r=("bc_rs", "l2fcb"), w=("bc_rs",), tag="l2fCsig")
    k.pop("l2feats_done")
    if L2_STAGE < 4:
        return

    k.push()
    glb = k.carve("l2glb", [128, T], BF16)
    for tc in range(4):
        tsl = slice(tc * 512, (tc + 1) * 512)
        k.mm(bank[6], k.orow(32), featsC[32:33, tsl], True, True,
             r=("bc_rs",), w=("bank6",), tag="l2glbm")
        k.op("vector", lambda e, tsl=tsl: e.tensor_copy(glb[:, tsl], bank[6]),
             r=("bank6",), w=("l2glb",), tag="l2glbe")
    sseq = k.carve("l2ss", [128, 4 * T], BF16)
    carry = k.carve("l2cy", [128, 32], F32)
    dky = scr0[:, 1536:2048]
    dcol = k.carve("l2dc", [128, 4], F32)
    sinb = [k.carve("l2si0", [128, 1024], BF16)]
    sob = [k.carve("l2so0", [128, 512], BF16)]
    k.op("gpsimd", lambda e: e.memset(carry, 0.0), w=("l2cy",), tag="l2cy0")
    dl = k.carve("l2dl", [128, 1], F32)
    k.dma("sync", dl, k.dram["l2_decay"], w=("l2dl",))
    k.op("scalar", lambda e: e.activation(dl[0:4, :], dl[0:4, :], AF.Sigmoid),
         r=("l2dl",), w=("l2dl",), tag="l2dsig")
    k.op("tensor", lambda e: e.transpose(bank[6][0:1, 0:4], dl[0:4, :],
                                         identf[0:4, 0:4]),
         r=("l2dl", "ident_f"), w=("bank6",), tag="l2dT")
    k.op("vector", lambda e: e.tensor_copy(dcol[0:1, :], bank[6][0:1, 0:4]),
         r=("bank6",), w=("l2dc",), tag="l2dTe")
    sin = k.dram["l2_sin_wt"]
    sout = k.dram["l2_sout_wt"]
    dkc = k.carve("l2dkc", [128, 1], F32)
    for cg in range(8):
        kk = cg // 2
        k.mm(bank[6][:, 0:1], k.orow(0), dcol[0:1, kk:kk + 1], True, True,
             r=("l2dc",), w=("bank6",), tag="l2dkb")
        k.op("vector", lambda e: e.tensor_copy(dkc, bank[6][:, 0:1]),
             r=("bank6",), w=("l2dkc",), tag="l2dkc")
        k.op("vector", lambda e: e.tensor_scalar(
            dky[:, 0:512], k.t["iota512"], 0.0, dkc, ALU.mult, ALU.add),
            r=("l2dkc", "iota512"), w=("scr0_3",), tag="l2dke")
        for c4 in range(4):
            ct = cg * 4 + c4
            wbuf = sinb[0]
            wbr = "l2si0"
            k.dma("gpsimd", sb3(wbuf, 8), fm3(sin[:, ct * 128:(ct + 1) * 128]),
                  r=(wbr,), w=(wbr,), tag="l2sild")
            for tc in range(4):
                tsl = slice(tc * 512, (tc + 1) * 512)
                bk = bank[tc % 2]
                bkr = f"bank{tc % 2}"
                for d in range(NDT):
                    k.mm(bk, wbuf[:, d * 128:(d + 1) * 128], xs(ug, d, tsl),
                         d == 0, d == NDT - 1, r=(wbr, "l2u"), w=(bkr,),
                         tag="l2simm")
                k.op("vector", lambda e, bk=bk, c4=c4, tc=tc, ct=ct: e.tensor_tensor_scan(
                    sseq[:, c4 * T + tc * 512: c4 * T + (tc + 1) * 512],
                    dky[:, 0:512], bk, carry[:, ct:ct + 1], ALU.mult, ALU.add),
                    r=(bkr, "scr0_3", "l2cy"), w=("l2ss",), tag="l2scan")
                k.op("vector", lambda e, c4=c4, tc=tc, ct=ct: e.tensor_copy(
                    carry[:, ct:ct + 1],
                    sseq[:, c4 * T + tc * 512 + 511: c4 * T + tc * 512 + 512]),
                    r=("l2ss",), w=("l2cy",), tag="l2carry")
        for fo in range(8):
            wbuf = sob[0]
            wbr = "l2so0"
            k.dma("gpsimd", sb3(wbuf, 4),
                  fm3(sout[cg * 512:(cg + 1) * 512, fo * 128:(fo + 1) * 128]),
                  r=(wbr,), w=(wbr,), tag="l2sold")
            for tc in range(4):
                tsl = slice(tc * 512, (tc + 1) * 512)
                bk = bank[4 + tc % 2]
                bkr = f"bank{4 + tc % 2}"
                for c4 in range(4):
                    k.mm(bk, wbuf[:, c4 * 128:(c4 + 1) * 128],
                         sseq[:, c4 * T + tc * 512: c4 * T + (tc + 1) * 512],
                         c4 == 0, c4 == 3, r=(wbr, "l2ss"), w=(bkr,),
                         tag="l2somm")
                k.op("vector", lambda e, bk=bk, tsl=tsl: e.tensor_mul(
                    scr0[:, 0:512], bk, glb[:, tsl]),
                    r=(bkr, "l2glb"), w=("scr0_0",), tag="l2lgl")
                k.op("vector", lambda e, fo=fo, tsl=tsl: e.tensor_add(
                    xs(xn, fo, tsl), xs(xn, fo, tsl), scr0[:, 0:512]),
                    r=("scr0_0", "xn"), w=("xn",), tag="l2ladd")
    k.pop("l2ssm_done")
    k.pop("l2u_done")
    k.dbg_ckpt("ssm")
    if L2_STAGE < 5:
        return

    emit_l2_memscan(k, featsA, featsC, rwidx)
    k.dbg_ckpt("mem")
    k.pop("l2attnA_done")
    if L2_STAGE < 6:
        return

    k.push()
    bc3 = k.carve("l2bc3", [128, T], F32)
    if L2_STAGE < 6.05:
        k.pop("dbg")
        return
    if L2_STAGE < 6.08:
        for nch in range(4):
            tsl = slice(nch * 512, (nch + 1) * 512)
            for d in range(NDT):
                sq = scr0[:, (d % 2) * 512:(d % 2) * 512 + 512]
                k.op("scalar", lambda e, d=d, tsl=tsl, sq=sq: e.activation(
                    sq, xs(xn, d, tsl), AF.Square), r=("xn",),
                    w=(f"scr0_{d % 2}",), tag="dbgsq")
        k.pop("dbg")
        return
    emit_rms_bcast(k, lambda d, tsl: (xs(xn, d, tsl), "xn"), NDT, 1e-6,
                   bc3, "l2bc3", "l2yrms_")
    if L2_STAGE < 6.2:
        k.pop("dbg")
        return
    ub = k.carve("l2ub", [128, 32], F32)
    db = k.carve("l2db", [128, 8], F32)
    k.dma("sync", ub, k.dram["l2_up_b"], w=("l2ub",))
    k.dma("sync", db, k.dram["l2_down_b"], w=("l2db",))
    hbuf = k.carve("l2mh", [128, 32 * 512], BF16)
    hw = [k.carve("l2mw0", [128, 1024], BF16), k.carve("l2mw1", [128, 1024], BF16)]
    upw = k.dram["l2_up_wt"]
    dnw = k.dram["l2_down_wt"]
    for tq in range(4):
        tsl = slice(tq * 512, (tq + 1) * 512)
        for fc in range(32):
            wbuf = hw[fc % 2]
            wbr = f"l2mw{fc % 2}"
            k.dma("gpsimd", sb3(wbuf, 8), fm3(upw[:, fc * 128:(fc + 1) * 128]),
                  r=(wbr,), w=(wbr,), tag="l2uld")
            bk = bank[fc % 2]
            bkr = f"bank{fc % 2}"
            for d in range(NDT):
                k.mm(bk, wbuf[:, d * 128:(d + 1) * 128], xs(xn, d, tsl),
                     d == 0, d == NDT - 1, r=(wbr, "xn"), w=(bkr,), tag="l2umm")
            k.op("vector", lambda e, bk=bk, tsl=tsl: e.tensor_mul(
                scr0[:, 0:512], bk, bc3[:, tsl]),
                r=(bkr, "l2bc3"), w=("scr0_0",), tag="l2urs")
            k.op("scalar", lambda e, fc=fc: e.activation(
                hbuf[:, fc * 512:(fc + 1) * 512], scr0[:, 0:512], AF.Silu,
                bias=ub[:, fc:fc + 1]),
                r=("scr0_0", "l2ub"), w=("l2mh",), tag="l2usilu")
        for fo in range(8):
            fbk = bank[2 + fo % 2]
            fbr = f"bank{2 + fo % 2}"
            for hh in range(4):
                wbuf = hw[hh % 2]
                wbr = f"l2mw{hh % 2}"
                k.dma("gpsimd", sb3(wbuf, 8),
                      fm3(dnw[hh * 1024:(hh + 1) * 1024, fo * 128:(fo + 1) * 128]),
                      r=(wbr,), w=(wbr,), tag="l2dld")
                for d8 in range(8):
                    hc = hh * 8 + d8
                    k.mm(fbk, wbuf[:, d8 * 128:(d8 + 1) * 128],
                         hbuf[:, hc * 512:(hc + 1) * 512],
                         hc == 0, hc == 31, r=(wbr, "l2mh"), w=(fbr,),
                         tag="l2dmm")
            k.op("vector", lambda e, fo=fo, tsl=tsl, fbk=fbk: e.scalar_tensor_tensor(
                xs(x, fo, tsl), fbk, db[:, fo:fo + 1], xs(x, fo, tsl),
                ALU.add, ALU.add),
                r=(fbr, "l2db", "x"), w=("x",), tag="l2dadd")
    if L2_STAGE >= 6.5:
        for fo in range(NDT):
            k.op("vector", lambda e, fo=fo: e.tensor_add(
                xs(x, fo), xs(x, fo), xs(xn, fo)),
                r=("x", "xn"), w=("x",), tag="l2yadd")
    k.pop("l2mlp_done")


def emit_l2_memscan(k, featsA, featsC, rwidx):
    xn = k.t["xn"]
    bank = k.t["banks"]
    bank_b = k.t["bank_b"]
    scr0 = k.t["scr0"]
    row = k.t["row"]
    oc, ocb = k.t["ones_col"], k.t["ones_col_b"]
    identb, identf = k.t["ident_b"], k.t["ident_f"]
    SLb, SUb = k.t["SLb"], k.t["SUb"]
    iota512, iotac = k.t["iota512"], k.t["iotacol"]
    parcol = k.t["parcol"]

    k.push()
    Km = k.carve("msK", [128, 512], BF16)
    Vm = k.carve("msV", [128, 512], BF16)
    lastm = k.carve("msL", [128, 8], BF16)
    readsT = k.carve("msRd", [32, T], BF16)
    OHT = k.carve("msOHT", [128, 512], BF16)
    OHRT = k.carve("msORT", [128, 512], BF16)
    OH = k.carve("msOH", [128, 512], BF16)
    OHR = k.carve("msOR", [128, 512], BF16)
    OHTw = k.carve("msOTW", [128, 512], BF16)
    W2W = k.carve("msW2W", [128, 128], BF16)
    W2WU = k.carve("msW2U", [128, 128], BF16)
    R2 = k.carve("msR2", [128, 128], BF16)
    R2L = k.carve("msR2L", [128, 128], BF16)
    OHN = k.carve("msOHN", [128, 128], BF16)
    EX = k.carve("msEX", [128, 128], BF16)
    CF = k.carve("msCF", [128, 128], BF16)
    ktok = k.carve("msk", [128, 32], BF16)
    vtok = k.carve("msv", [128, 32], BF16)
    qtok = k.carve("msq", [128, 32], F32)
    cols = k.carve("mscol", [128, 16], F32)
    colsb = k.carve("mscbb", [128, 8], BF16)
    rv = k.carve("msrv", [128, 128], F32)
    sc = k.carve("mssc", [128, 8], F32)
    rk = k.carve("msrk", [128, 32], F32)
    tt = k.carve("mstt", [128, 32], F32)
    tt4 = k.carve("mst4", [128, 8], F32)
    gmb = k.carve("msgmb", [128, T], BF16)
    mow = k.carve("msmw", [32, 1024], BF16)

    CG, CL, CLS, CC0, CPI, CSU = 0, 1, 2, 3, 4, 8
    CLMS, CGMS, CWC = 9, 10, 11
    BS1, BS0, BLMS, BIL = 0, 1, 2, 5

    k.op("gpsimd", lambda e: e.memset(Km, 0.0), w=("msK",), tag="msK0")
    k.op("gpsimd", lambda e: e.memset(Vm, 0.0), w=("msV",), tag="msV0")
    k.op("gpsimd", lambda e: e.memset(lastm, -1.0), w=("msL",), tag="msL0")
    k.dma("gpsimd", mow, k.dram["l2_mow"], w=("msmw",), tag="msmwl")
    for tc in range(4):
        tsl = slice(tc * 512, (tc + 1) * 512)
        k.mm(bank[6], k.orow(64), featsC[64:65, tsl], True, True,
             r=("bc_rs",), w=("bank6",), tag="msgmbm")
        k.op("vector", lambda e, tsl=tsl: e.tensor_copy(gmb[:, tsl], bank[6]),
             r=("bank6",), w=("msgmb",), tag="msgmbe")

    for c in range(16):
        csl = slice(c * 128, (c + 1) * 128)
        for (p0, dst, dstres, f32out) in ((32, ktok, "msk", False),
                                          (64, vtok, "msv", False),
                                          (0, qtok, "msq", True)):
            k.op("tensor", lambda e, p0=p0, csl=csl: e.transpose(
                bank[4][:, 0:32], featsA[p0:p0 + 32, csl],
                identf[p0:p0 + 32, p0:p0 + 32]),
                r=("bc_mu", "ident_f"), w=("bank4",), tag="mskvT")
            if f32out:
                k.op("scalar", lambda e, dst=dst: e.activation(
                    dst, bank[4][:, 0:32], AF.Copy, scale=1.0 / np.sqrt(32.0)),
                    r=("bank4",), w=(dstres,), tag="msqE")
            else:
                k.op("vector", lambda e, dst=dst: e.tensor_copy(dst, bank[4][:, 0:32]),
                     r=("bank4",), w=(dstres,), tag="mskvE")
        k.op("tensor", lambda e, csl=csl: e.transpose(
            bank[4][:, 0:1], featsC[0:1, csl], identf[0:1, 0:1]),
            r=("bc_rs", "ident_f"), w=("bank4",), tag="msgT")
        k.op("vector", lambda e: e.tensor_copy(cols[:, CG:CG + 1], bank[4][:, 0:1]),
             r=("bank4",), w=("mscol",), tag="msgE")
        k.op("vector", lambda e: e.tensor_scalar(
            cols[:, CL:CL + 1], cols[:, CG:CG + 1], -1.0, 1.0, ALU.mult, ALU.add),
            r=("mscol",), w=("mscol",), tag="msL1")
        k.op("vector", lambda e: e.tensor_scalar_max(
            cols[:, CL:CL + 1], cols[:, CL:CL + 1], 1e-30),
            r=("mscol",), w=("mscol",), tag="msL2")
        k.op("scalar", lambda e: e.activation(
            cols[:, CL:CL + 1], cols[:, CL:CL + 1], AF.Ln),
            r=("mscol",), w=("mscol",), tag="msL3")

        for h in range(2):
            wcol = rwidx[:, c * 4 + 2 + h: c * 4 + 3 + h]
            rcol = rwidx[:, c * 4 + h: c * 4 + 1 + h]
            k.op("vector", lambda e, wcol=wcol: e.tensor_single_scalar(
                OHT, iota512, wcol, ALU.is_equal),
                r=("l2idx", "iota512"), w=("msOHT",), tag="msOHT")
            k.op("vector", lambda e, rcol=rcol: e.tensor_single_scalar(
                OHRT, iota512, rcol, ALU.is_equal),
                r=("l2idx", "iota512"), w=("msORT",), tag="msORT")
            for bt in range(4):
                k.op("tensor", lambda e, bt=bt: e.transpose(
                    bank_b[:, 0:128], OHT[:, bt * 128:(bt + 1) * 128], identb),
                    r=("msOHT", "ident_b"), w=("bank_b",), tag="msOHTt")
                k.op("vector", lambda e, bt=bt: e.tensor_copy(
                    OH[:, bt * 128:(bt + 1) * 128], bank_b[:, 0:128]),
                    r=("bank_b",), w=("msOH",), tag="msOHe")
                k.op("tensor", lambda e, bt=bt: e.transpose(
                    bank_b[:, 128:256], OHRT[:, bt * 128:(bt + 1) * 128], identb),
                    r=("msORT", "ident_b"), w=("bank_b",), tag="msORTt")
                k.op("vector", lambda e, bt=bt: e.tensor_copy(
                    OHR[:, bt * 128:(bt + 1) * 128], bank_b[:, 128:256]),
                    r=("bank_b",), w=("msOR",), tag="msORe")
            for bt in range(4):
                k.mm(bank[0][:, 0:128], OH[:, bt * 128:(bt + 1) * 128],
                     OH[:, bt * 128:(bt + 1) * 128], bt == 0, bt == 3,
                     r=("msOH",), w=("bank0",), tag="msM")
            k.op("vector", lambda e: e.tensor_mul(W2W, bank[0][:, 0:128], SLb),
                 r=("bank0", "SLb"), w=("msW2W",), tag="msW2W")
            k.op("vector", lambda e: e.tensor_mul(W2WU, bank[0][:, 0:128], SUb),
                 r=("bank0", "SUb"), w=("msW2U",), tag="msW2U")
            for bt in range(4):
                k.mm(bank[1][:, 0:128], OH[:, bt * 128:(bt + 1) * 128],
                     OHR[:, bt * 128:(bt + 1) * 128], bt == 0, bt == 3,
                     r=("msOH", "msOR"), w=("bank1",), tag="msR2m")
            k.op("vector", lambda e: e.tensor_mul(R2, bank[1][:, 0:128], SLb),
                 r=("bank1", "SLb"), w=("msR2",), tag="msR2")
            k.mm(bank[2][0:1, 0:128], ocb, W2W, True, True,
                 r=("msW2W",), w=("bank2",), tag="msn")
            k.op("vector", lambda e: e.tensor_copy(row[:, 0:128], bank[2][0:1, 0:128]),
                 r=("bank2",), w=("row",), tag="msnE")
            k.mm(bank[3][:, 0:128], k.orow(0), row[:, 0:128], True, True,
                 r=("row",), w=("bank3",), tag="msnB")
            k.op("vector", lambda e: e.tensor_single_scalar(
                OHN, bank[3][:, 0:128], iotac, ALU.is_equal),
                r=("bank3", "iotacol"), w=("msOHN",), tag="msOHN")
            k.mm(bank[2][0:1, 0:128], parcol, OHN, True, True,
                 r=("msOHN", "parcol"), w=("bank2",), tag="mspn")
            for bt in range(4):
                k.mm(bank[4][0:1, 0:128], lastm[:, h * 4 + bt: h * 4 + bt + 1],
                     OH[:, bt * 128:(bt + 1) * 128], bt == 0, bt == 3,
                     r=("msL", "msOH"), w=("bank4",), tag="msbl")
            k.op("vector", lambda e: e.tensor_single_scalar(
                row[:, 128:256], bank[4][0:1, 0:128], 0.0, ALU.is_equal),
                r=("bank4",), w=("row",), tag="mspb")
            k.op("vector", lambda e: e.tensor_tensor(
                row[:, 256:384], row[:, 128:256], bank[2][0:1, 0:128], ALU.mult),
                r=("row", "bank2"), w=("row",), tag="mspp")
            k.op("vector", lambda e: e.tensor_tensor(
                row[:, 384:512], row[:, 128:256], bank[2][0:1, 0:128], ALU.add),
                r=("row", "bank2"), w=("row",), tag="msps")
            k.op("vector", lambda e: e.scalar_tensor_tensor(
                row[:, 0:128], row[:, 256:384], -2.0, row[:, 384:512],
                ALU.mult, ALU.add),
                r=("row",), w=("row",), tag="msslot")
            k.op("tensor", lambda e: e.transpose(
                bank[4][:, 0:1], row[:, 0:128], identf[0:1, 0:1]),
                r=("row", "ident_f"), w=("bank4",), tag="msslT")
            k.op("vector", lambda e: e.tensor_copy(cols[:, CLS:CLS + 1],
                                                   bank[4][:, 0:1]),
                 r=("bank4",), w=("mscol",), tag="msslE")
            k.op("vector", lambda e: e.tensor_copy(colsb[:, BS1:BS1 + 1],
                                                   cols[:, CLS:CLS + 1]),
                 r=("mscol",), w=("mscbb",), tag="mss1")
            k.op("vector", lambda e: e.tensor_scalar(
                colsb[:, BS0:BS0 + 1], cols[:, CLS:CLS + 1], -1.0, 1.0,
                ALU.mult, ALU.add),
                r=("mscol",), w=("mscbb",), tag="mss0")

            for s in range(2):
                smask = colsb[:, (BS1 if s == 1 else BS0):(BS1 if s == 1 else BS0) + 1]
                off = h * 256 + s * 128
                k.op("vector", lambda e, smask=smask: e.tensor_tensor(
                    cols[:, CLMS:CLMS + 1], cols[:, CL:CL + 1], smask, ALU.mult),
                    r=("mscol", "mscbb"), w=("mscol",), tag="mslms")
                k.op("vector", lambda e: e.tensor_copy(
                    colsb[:, BLMS:BLMS + 1], cols[:, CLMS:CLMS + 1]),
                    r=("mscol",), w=("mscbb",), tag="mslmsb")
                k.op("vector", lambda e, smask=smask: e.tensor_tensor(
                    cols[:, CGMS:CGMS + 1], cols[:, CG:CG + 1], smask, ALU.mult),
                    r=("mscol", "mscbb"), w=("mscol",), tag="msgms")
                k.op("vector", lambda e: e.tensor_single_scalar(
                    R2L, R2, cols[:, CLMS:CLMS + 1], ALU.mult),
                    r=("msR2", "mscol"), w=("msR2L",), tag="msR2L")
                k.mm(bank[5][:, 0:1], R2, colsb[:, BLMS:BLMS + 1], True, True,
                     r=("msR2", "mscbb"), w=("bank5",), tag="msc0")
                k.op("scalar", lambda e: e.activation(
                    cols[:, CC0:CC0 + 1], bank[5][:, 0:1], AF.Exp),
                    r=("bank5",), w=("mscol",), tag="msc0e")
                k.mm(bank[3][:, 0:128], SUb, R2L, True, True,
                     r=("msR2L", "SUb"), w=("bank3",), tag="msSsuf")
                k.op("scalar", lambda e: e.activation(EX, bank[3][:, 0:128], AF.Exp),
                     r=("bank3",), w=("msEX",), tag="msEXe")
                k.op("vector", lambda e: e.tensor_single_scalar(
                    CF, R2, cols[:, CGMS:CGMS + 1], ALU.mult),
                    r=("msR2", "mscol"), w=("msCF",), tag="msCF1")
                k.op("vector", lambda e: e.tensor_mul(CF, CF, EX),
                     r=("msCF", "msEX"), w=("msCF",), tag="msCF2")
                for (tokb, mem, memres, outdst, outres) in (
                        (ktok, Km, "msK", rk, "msrk"),
                        (vtok, Vm, "msV", None, None)):
                    k.mm(bank[2][:, 0:32], CF, tokb, True, True,
                         r=("msCF", "msk" if tokb is ktok else "msv"),
                         w=("bank2",), tag="msrx")
                    for bt in range(4):
                        k.mm(bank[4][:, 0:32], OHR[:, bt * 128:(bt + 1) * 128],
                             mem[:, off + bt * 32: off + (bt + 1) * 32],
                             bt == 0, bt == 3, r=("msOR", memres),
                             w=("bank4",), tag="msbx")
                    k.op("vector", lambda e: e.tensor_single_scalar(
                        tt, bank[4][:, 0:32], cols[:, CC0:CC0 + 1], ALU.mult),
                        r=("bank4", "mscol"), w=("mstt",), tag="msbc")
                    if outdst is not None:
                        k.op("vector", lambda e, outdst=outdst: e.tensor_tensor(
                            outdst, tt, bank[2][:, 0:32], ALU.add),
                            r=("mstt", "bank2"), w=(outres,), tag="msrkE")
                    else:
                        sl_rv = slice((h * 2 + s) * 32, (h * 2 + s + 1) * 32)
                        k.op("vector", lambda e, sl_rv=sl_rv: e.tensor_tensor(
                            rv[:, sl_rv], tt, bank[2][:, 0:32], ALU.add),
                            r=("mstt", "bank2"), w=("msrv",), tag="msrvE")
                k.op("vector", lambda e: e.tensor_mul(tt, rk, qtok),
                     r=("msrk", "msq"), w=("mstt",), tag="mssc1")
                k.op("vector", lambda e, h=h, s=s: e.reduce_sum(
                    sc[:, h * 2 + s: h * 2 + s + 1], tt, axis=AX.X),
                    r=("mstt",), w=("mssc",), tag="mssc2")
                for bt in range(4):
                    k.mm(bank[6][:, bt:bt + 1], OHT[:, bt * 128:(bt + 1) * 128],
                         colsb[:, BLMS:BLMS + 1], True, True,
                         r=("msOHT", "mscbb"), w=("bank6",), tag="msPi")
                k.op("scalar", lambda e: e.activation(
                    cols[:, CPI:CPI + 4], bank[6][:, 0:4], AF.Exp),
                    r=("bank6",), w=("mscol",), tag="msPie")
                k.mm(bank[5][:, 0:1], W2WU, colsb[:, BLMS:BLMS + 1], True, True,
                     r=("msW2U", "mscbb"), w=("bank5",), tag="msSa")
                k.op("scalar", lambda e: e.activation(
                    cols[:, CSU:CSU + 1], bank[5][:, 0:1], AF.Exp),
                    r=("bank5",), w=("mscol",), tag="msSae")
                k.op("vector", lambda e: e.tensor_tensor(
                    cols[:, CWC:CWC + 1], cols[:, CSU:CSU + 1],
                    cols[:, CGMS:CGMS + 1], ALU.mult),
                    r=("mscol",), w=("mscol",), tag="mswc")
                k.op("vector", lambda e: e.tensor_single_scalar(
                    OHTw, OHT, cols[:, CWC:CWC + 1], ALU.mult),
                    r=("msOHT", "mscol"), w=("msOTW",), tag="msOTW")
                for (tokb, tokres, mem, memres) in ((ktok, "msk", Km, "msK"),
                                                    (vtok, "msv", Vm, "msV")):
                    for bt in range(4):
                        k.mm(bank[2][:, 0:32], OHTw[:, bt * 128:(bt + 1) * 128],
                             tokb, True, True, r=("msOTW", tokres),
                             w=("bank2",), tag="msDx")
                        k.op("vector", lambda e, mem=mem, bt=bt, off=off: e.scalar_tensor_tensor(
                            mem[:, off + bt * 32: off + (bt + 1) * 32],
                            mem[:, off + bt * 32: off + (bt + 1) * 32],
                            cols[:, CPI + bt: CPI + bt + 1], bank[2][:, 0:32],
                            ALU.mult, ALU.add),
                            r=(memres, "mscol", "bank2"), w=(memres,), tag="msKu")
            k.mm(bank[5][:, 0:1], W2WU, ocb, True, True,
                 r=("msW2U",), w=("bank5",), tag="mssu")
            k.op("vector", lambda e: e.tensor_single_scalar(
                tt4[:, 0:1], bank[5][:, 0:1], 0.5, ALU.is_lt),
                r=("bank5",), w=("mst4",), tag="msil")
            k.op("vector", lambda e: e.tensor_tensor(
                colsb[:, BIL:BIL + 1], tt4[:, 0:1], cols[:, CLS:CLS + 1], ALU.mult),
                r=("mst4", "mscol"), w=("mscbb",), tag="msilb")
            for bt in range(4):
                k.mm(bank[6][:, bt:bt + 1], OHT[:, bt * 128:(bt + 1) * 128],
                     ocb, True, True, r=("msOHT",), w=("bank6",), tag="mscnt")
            k.op("vector", lambda e: e.tensor_scalar_min(
                tt4[:, 4:8], bank[6][:, 0:4], 1.0),
                r=("bank6",), w=("mst4",), tag="msanyw")
            for bt in range(4):
                k.mm(bank[3][:, bt:bt + 1], OHT[:, bt * 128:(bt + 1) * 128],
                     colsb[:, BIL:BIL + 1], True, True,
                     r=("msOHT", "mscbb"), w=("bank3",), tag="mslv")
            k.op("vector", lambda e, h=h: e.tensor_tensor(
                tt4[:, 0:4], bank[3][:, 0:4], lastm[:, h * 4:h * 4 + 4],
                ALU.subtract),
                r=("bank3", "msL"), w=("mst4",), tag="msld")
            k.op("vector", lambda e: e.tensor_mul(tt4[:, 0:4], tt4[:, 0:4],
                                                  tt4[:, 4:8]),
                 r=("mst4",), w=("mst4",), tag="msla")
            k.op("vector", lambda e, h=h: e.tensor_tensor(
                lastm[:, h * 4:h * 4 + 4], lastm[:, h * 4:h * 4 + 4],
                tt4[:, 0:4], ALU.add),
                r=("msL", "mst4"), w=("msL",), tag="mslu")
        k.op("vector", lambda e: e.reduce_max(tt4[:, 0:1], sc[:, 0:4], axis=AX.X),
             r=("mssc",), w=("mst4",), tag="msmx")
        k.op("vector", lambda e: e.tensor_single_scalar(
            sc[:, 0:4], sc[:, 0:4], tt4[:, 0:1], ALU.subtract),
            r=("mssc", "mst4"), w=("mssc",), tag="msms")
        k.op("scalar", lambda e: e.activation(sc[:, 0:4], sc[:, 0:4], AF.Exp),
             r=("mssc",), w=("mssc",), tag="msme")
        k.op("vector", lambda e: e.reduce_sum(tt4[:, 0:1], sc[:, 0:4], axis=AX.X),
             r=("mssc",), w=("mst4",), tag="msmsum")
        k.op("vector", lambda e: e.reciprocal(tt4[:, 0:1], tt4[:, 0:1]),
             r=("mst4",), w=("mst4",), tag="msmz")
        k.op("vector", lambda e: e.tensor_single_scalar(
            sc[:, 0:4], sc[:, 0:4], tt4[:, 0:1], ALU.mult),
            r=("mssc", "mst4"), w=("mssc",), tag="msmn")
        k.op("vector", lambda e: e.tensor_single_scalar(
            rk, rv[:, 0:32], sc[:, 0:1], ALU.mult),
            r=("msrv", "mssc"), w=("msrk",), tag="msrd0")
        for a in range(1, 4):
            k.op("vector", lambda e, a=a: e.scalar_tensor_tensor(
                rk, rv[:, a * 32:(a + 1) * 32], sc[:, a:a + 1], rk,
                ALU.mult, ALU.add),
                r=("msrv", "mssc", "msrk"), w=("msrk",), tag="msrdA")
        k.op("tensor", lambda e, csl=csl: e.transpose(
            bank[4][0:32, 0:128], rk, identf),
            r=("msrk", "ident_f"), w=("bank4",), tag="msrdT")
        k.op("vector", lambda e, csl=csl: e.tensor_copy(
            readsT[:, csl], bank[4][0:32, 0:128]),
            r=("bank4",), w=("msRd",), tag="msrdE")

    for fo in range(8):
        for tc in range(4):
            tsl = slice(tc * 512, (tc + 1) * 512)
            k.mm(bank[2], mow[:, fo * 128:(fo + 1) * 128], readsT[:, tsl],
                 True, True, r=("msmw", "msRd"), w=("bank2",), tag="msom")
            k.op("vector", lambda e, tsl=tsl: e.tensor_mul(
                scr0[:, 0:512], bank[2], gmb[:, tsl]),
                r=("bank2", "msgmb"), w=("scr0_0",), tag="msogm")
            k.op("vector", lambda e, fo=fo, tsl=tsl: e.tensor_add(
                xs(xn, fo, tsl), xs(xn, fo, tsl), scr0[:, 0:512]),
                r=("scr0_0", "xn"), w=("xn",), tag="msoadd")
    k.pop("l2ms_done")


L3_PARTS = "af"


def emit_layer3(k):
    emit_ln(k, "l3ln_")
    if "f" in L3_PARTS:
        emit_l3_ffn(k)
    if "a" in L3_PARTS:
        emit_l3_attn(k)


def emit_l3_ffn(k):
    x, xn = k.t["x"], k.t["xn"]
    bank = k.t["banks"]

    k.push()
    bd = k.carve("r_bd", [128, 32 * 512], BF16)
    sel = k.t["c_sel"]

    k.dma("gpsimd", k.dram["wc_bf"], k.dram["l3_wct"], w=("wc_bf",), tag="r_pre")

    k.push()
    rmts = k.carve("r_rm", [16, 1024], BF16)
    rupts = k.carve("r_ru", [16, 4096], BF16)
    stg = k.carve("r_st", [16, 512], BF16)
    for h in range(4):
        k.dma("gpsimd", rmts, k.dram["l3_rmt"][h], r=("r_rm",), w=("r_rm",),
              tag="r_rml")
        k.dma("gpsimd", rupts, k.dram["l3_rut"][h], r=("r_ru",), w=("r_ru",),
              tag="r_rul")
        for dt in range(8):
            for r8 in range(8):
                r = dt * 8 + r8
                k.mm(bank[0][0:16, r8 * 64:(r8 + 1) * 64],
                     rmts[:, r * 16:(r + 1) * 16],
                     rupts[:, r * 64:(r + 1) * 64], True, True,
                     r=("r_rm", "r_ru"), w=("bank0",), tag="r_cmb")
            k.op("vector", lambda e: e.tensor_copy(stg, bank[0][0:16, 0:512]),
                 r=("bank0",), w=("r_st",), tag="r_stge")
            for r8 in range(8):
                k.mm(bank[1][:, r8 * 64:(r8 + 1) * 64],
                     sel[:, r8 * 128:(r8 + 1) * 128],
                     stg[:, r8 * 64:(r8 + 1) * 64], True, True,
                     r=("r_st", "c_sel"), w=("bank1",), tag="r_sct")
            k.op("vector", lambda e, h=h, dt=dt: e.tensor_copy(
                bd[:, (h * 8 + dt) * 512:(h * 8 + dt + 1) * 512], bank[1]),
                r=("bank1",), w=("r_bd",), tag="r_bde")

    k.pop("r_bd_done")
    hz = k.carve("r_hz", [128, 32 * 512], BF16)
    wcb = [k.carve("r_w0", [128, 1024], BF16), k.carve("r_w1", [128, 1024], BF16)]
    wc = k.dram["wc_bf"]
    for tc in range(4):
        tsl = slice(tc * 512, (tc + 1) * 512)
        for h in range(4):
            for dt in range(8):
                for mc in range(4):
                    ktl = dt * 4 + mc
                    zbk = bank[mc % 2]
                    zbr = f"bank{mc % 2}"
                    k.mm(zbk,
                         bd[:, (h * 8 + dt) * 512 + mc * 128:
                            (h * 8 + dt) * 512 + (mc + 1) * 128],
                         xs(xn, dt, tsl), True, True,
                         r=("r_bd", "xn"), w=(zbr,), tag="r_zmm")
                    k.op("scalar", lambda e, ktl=ktl, zbk=zbk: e.activation(
                        hz[:, ktl * 512:(ktl + 1) * 512], zbk, AF.Gelu),
                        r=(zbr,), w=("r_hz",), tag="r_zg")
            for kt in range(32):
                wbuf = wcb[kt % 2]
                wbr = f"r_w{kt % 2}"
                gkt = h * 32 + kt
                k.dma("sync", wbuf, wc[gkt * 128:(gkt + 1) * 128, :],
                      r=(wbr,), w=(wbr,), tag="r_wld")
                for fo in range(8):
                    if fo < 7:
                        bko, bkor = bank[fo], f"bank{fo}"
                    else:
                        bko, bkor = k.t["bank_b"].bitcast(F32), "bank_b"
                    k.mm(bko[:, 0:512], wbuf[:, fo * 128:(fo + 1) * 128],
                         hz[:, kt * 512:(kt + 1) * 512],
                         kt == 0, kt == 31, r=(wbr, "r_hz"),
                         w=(bkor,), tag="r_cmm")
            for fo in range(8):
                if fo < 7:
                    bko, bkor = bank[fo], f"bank{fo}"
                else:
                    bko, bkor = k.t["bank_b"].bitcast(F32), "bank_b"
                k.op("vector", lambda e, fo=fo, tsl=tsl, bko=bko: e.tensor_add(
                    xs(x, fo, tsl), xs(x, fo, tsl), bko[:, 0:512]),
                    r=(bkor, "x"), w=("x",), tag="r_cadd")
    k.pop("r_done")


def emit_l3_attn(k):
    x, xn = k.t["x"], k.t["xn"]
    bank = k.t["banks"]
    bank_b = k.t["bank_b"]
    scr0 = k.t["scr0"]
    identb = k.t["ident_b"]
    caus01 = k.t["causal01"]
    oc = k.t["ones_col"]
    bc_mu = k.t["bc_mu"]
    row = k.t["row"]

    k.push()
    oa = k.carve("g_oa", [128, NDT * T], BF16)
    qh = k.carve("g_qh", [128, T], BF16)
    kh = k.carve("g_kh2", [128, T], BF16)
    vh = k.carve("g_vh", [128, 2 * T], BF16)
    gh = k.carve("g_gh", [128, T], BF16)
    gth = k.carve("g_gth", [128, 2 * T], BF16)
    t1b = k.carve("g_t1", [16, T], BF16)
    S = k.carve("g_S", [128, 256], F32)
    Sb = k.carve("g_Sb", [128, 256], BF16)
    qtb = k.carve("g_qt", [128, 64], BF16)
    ktb = k.carve("g_ktl", [128, 64], BF16)
    amb = k.carve("g_am", [64, 64], BF16)
    vtk = k.carve("g_vt", [64, 256], BF16)
    kht = k.carve("g_khx", [128, 64], BF16)
    khtk = k.carve("g_khT", [64, 128], BF16)
    eecol = k.carve("g_ee", [128, 1], F32)
    zs64 = k.carve("g_zs", [128, 64], F32)
    wvb = [k.carve("g_wv0", [128, 1024], BF16), k.carve("g_wv1", [128, 1024], BF16)]
    gk1 = k.carve("g_k1", [128, 128], BF16)
    gk2 = k.carve("g_k2", [16, 512], BF16)
    bg2 = k.carve("g_b2", [128, 4], F32)
    rmw = k.carve("g_rw", [128, 2], F32)

    k.dma("gpsimd", sb3(gk1, 8), fm3(k.dram["l3_wgk1"]), w=("g_k1",), tag="g_k1l")
    k.dma("gpsimd", gk2, k.dram["l3_wgk2"], w=("g_k2",), tag="g_k2l")
    k.dma("sync", bg2, k.dram["l3_bgk2"], w=("g_b2",), tag="g_b2l")
    k.dma("sync", rmw, k.dram["l3_rmsw"], w=("g_rw",), tag="g_rwl")
    k.op("gpsimd", lambda e: e.memset(zs64, 0.0), w=("g_zs",), tag="g_z0")

    for tc2 in range(4):
        tsl = slice(tc2 * 512, (tc2 + 1) * 512)
        for d in range(NDT):
            k.mm(bank[3][0:16, :], gk1[:, d * 16:(d + 1) * 16],
                 xs(xn, d, tsl), d == 0, d == NDT - 1,
                 r=("g_k1", "xn"), w=("bank3",), tag="g_t1m")
        k.op("vector", lambda e, tsl=tsl: e.tensor_copy(
            t1b[:, tsl], bank[3][0:16, :]),
            r=("bank3",), w=("g_t1",), tag="g_t1e")

    for h in range(4):
        projs = ((k.dram["l3_wq"][:, h * 128:(h + 1) * 128], qh, "g_qh", 1, None),
                 (k.dram["l3_wk"][:, h * 128:(h + 1) * 128], kh, "g_kh2", 1, None),
                 (k.dram["l3_wv"][:, h * 256:(h + 1) * 256], vh, "g_vh", 2, None),
                 (k.dram["l3_wg"][:, h * 256:(h + 1) * 256], gth, "g_gth", 2, AF.Silu))
        for (src, dst, dres, nfc, act) in projs:
            for fc in range(nfc):
                wbuf = wvb[fc % 2]
                wbr = f"g_wv{fc % 2}"
                k.dma("gpsimd", sb3(wbuf, 8),
                      fm3(src[:, fc * 128:(fc + 1) * 128]),
                      r=(wbr,), w=(wbr,), tag="g_wpl")
                for tc2 in range(4):
                    tsl = slice(tc2 * 512, (tc2 + 1) * 512)
                    bk = bank[tc2 % 2]
                    bkr = f"bank{tc2 % 2}"
                    for d in range(NDT):
                        k.mm(bk, wbuf[:, d * 128:(d + 1) * 128],
                             xs(xn, d, tsl), d == 0, d == NDT - 1,
                             r=(wbr, "xn"), w=(bkr,), tag="g_pm")
                    if act is None:
                        k.op("scalar", lambda e, bk=bk, dst=dst, fc=fc, tsl=tsl: e.copy(
                            dst[:, fc * T + tsl.start: fc * T + tsl.stop], bk),
                            r=(bkr,), w=(dres,), tag="g_pe")
                    else:
                        k.op("scalar", lambda e, bk=bk, dst=dst, fc=fc, tsl=tsl,
                             act=act: e.activation(
                            dst[:, fc * T + tsl.start: fc * T + tsl.stop], bk, act),
                            r=(bkr,), w=(dres,), tag="g_pea")
        for tc2 in range(4):
            tsl = slice(tc2 * 512, (tc2 + 1) * 512)
            k.mm(bank[4], gk2[:, h * 128:(h + 1) * 128], t1b[:, tsl],
                 True, True, r=("g_k2", "g_t1"), w=("bank4",), tag="g_t2m")
            sA = scr0[:, 0:512]
            sB = scr0[:, 512:1024]
            sC = scr0[:, 1024:1536]
            k.op("vector", lambda e, h=h: e.tensor_single_scalar(
                sA, bank[4], bg2[:, h:h + 1], ALU.add),
                r=("bank4", "g_b2"), w=("scr0_0",), tag="g_t2b")
            k.op("scalar", lambda e: e.activation(sB, sA, AF.Abs),
                 r=("scr0_0",), w=("scr0_1",), tag="g_abs")
            k.op("scalar", lambda e: e.activation(sB, sB, AF.Exp, scale=-1.0),
                 r=("scr0_1",), w=("scr0_1",), tag="g_expx")
            k.op("scalar", lambda e: e.activation(sB, sB, AF.Ln,
                                                  bias=k.kcol(1.0)),
                 r=("scr0_1",), w=("scr0_1",), tag="g_lnx")
            k.op("vector", lambda e: e.tensor_scalar_min(sC, sA, 0.0),
                 r=("scr0_0",), w=("scr0_2",), tag="g_minx")
            k.op("vector", lambda e: e.tensor_sub(sC, sC, sB),
                 r=("scr0_2", "scr0_1"), w=("scr0_2",), tag="g_subx")
            k.op("vector", lambda e, tsl=tsl: e.tensor_scalar(
                gh[:, tsl], sC, 1.0 / 16.0, -3.0, ALU.mult, ALU.max),
                r=("scr0_2",), w=("g_gh",), tag="g_gfin")
        k.op("gpsimd", lambda e: e.memset(S, 0.0), r=("g_S",), w=("g_S",),
             tag="g_S0")
        k.op("gpsimd", lambda e: e.memset(Sb, 0.0), r=("g_Sb",), w=("g_Sb",),
             tag="g_Sb0")
        for c in range(32):
            csl = slice(c * 64, (c + 1) * 64)
            gc = scr0[:, 0:64]
            egs = scr0[:, 64:128]
            ens = scr0[:, 128:192]
            k.op("vector", lambda e, csl=csl: e.tensor_tensor_scan(
                gc, gh[:, csl], zs64, 0.0, ALU.add, ALU.add),
                r=("g_gh", "g_zs"), w=("scr0_0",), tag="g_cum")
            k.op("scalar", lambda e: e.activation(egs, gc, AF.Exp),
                 r=("scr0_0",), w=("scr0_1",), tag="g_eg")
            k.op("vector", lambda e: e.tensor_scalar(
                ens, gc, -1.0, 60.0, ALU.mult, ALU.min),
                r=("scr0_0",), w=("scr0_2",), tag="g_en1")
            k.op("scalar", lambda e: e.activation(ens, ens, AF.Exp),
                 r=("scr0_2",), w=("scr0_2",), tag="g_en2")
            k.op("vector", lambda e, csl=csl: e.tensor_mul(qtb, qh[:, csl], egs),
                 r=("g_qh", "scr0_1"), w=("g_qt",), tag="g_qtm")
            k.op("vector", lambda e, csl=csl: e.tensor_mul(ktb, kh[:, csl], ens),
                 r=("g_kh2", "scr0_2"), w=("g_ktl",), tag="g_ktm")
            k.mm(bank[2][0:64, 0:64], ktb, qtb, True, True,
                 r=("g_qt", "g_ktl"), w=("bank2",), tag="g_amm")
            k.op("vector", lambda e: e.tensor_mul(
                amb, bank[2][0:64, 0:64], caus01[0:64, 0:64]),
                r=("bank2",), w=("g_am",), tag="g_amsk")
            for dvc in range(2):
                k.op("tensor", lambda e, dvc=dvc, csl=csl: e.transpose(
                    bank_b[0:64, 0:128],
                    vh[:, dvc * T + csl.start: dvc * T + csl.stop], identb),
                    r=("g_vh", "ident_b"), w=("bank_b",), tag="g_vT")
                k.op("vector", lambda e, dvc=dvc: e.tensor_copy(
                    vtk[0:64, dvc * 128:(dvc + 1) * 128], bank_b[0:64, 0:128]),
                    r=("bank_b",), w=("g_vt",), tag="g_vTe")
            for dvc in range(2):
                bko = bank[4 + dvc]
                bkor = f"bank{4 + dvc}"
                k.mm(bko[:, 0:64], Sb[:, dvc * 128:(dvc + 1) * 128], qtb,
                     True, False, r=("g_Sb", "g_qt"), w=(bkor,), tag="g_oin")
                k.mm(bko[:, 0:64], vtk[0:64, dvc * 128:(dvc + 1) * 128], amb,
                     False, True, r=("g_vt", "g_am"), w=(bkor,), tag="g_oia")
                k.op("scalar", lambda e, bko=bko, dvc=dvc, h=h, csl=csl: e.copy(
                    xs(oa, 2 * h + dvc, csl), bko[:, 0:64]),
                    r=(bkor,), w=("g_oa",), tag="g_oe")
            k.op("scalar", lambda e: e.activation(eecol, gc[:, 63:64], AF.Exp),
                 r=("scr0_0",), w=("g_ee",), tag="g_eec")
            k.op("vector", lambda e: e.tensor_single_scalar(
                kht, ktb, eecol, ALU.mult),
                r=("g_ktl", "g_ee"), w=("g_khx",), tag="g_khm")
            k.op("tensor", lambda e: e.transpose(bank_b[0:64, 128:256], kht,
                                                 identb),
                 r=("g_khx", "ident_b"), w=("bank_b",), tag="g_khT")
            k.op("vector", lambda e: e.tensor_copy(khtk, bank_b[0:64, 128:256]),
                 r=("bank_b",), w=("g_khT2",), tag="g_khTe")
            k.mm(bank[6][:, 0:256], khtk, vtk[0:64, :], True, True,
                 r=("g_khT2", "g_vt"), w=("bank6",), tag="g_dS")
            k.op("vector", lambda e: e.scalar_tensor_tensor(
                S, S, eecol, bank[6][:, 0:256], ALU.mult, ALU.add),
                r=("g_S", "g_ee", "bank6"), w=("g_S",), tag="g_Sup")
            k.op("vector", lambda e: e.tensor_copy(Sb, S),
                 r=("g_S",), w=("g_Sb",), tag="g_Sbc")
        for nch in range(4):
            tsl = slice(nch * 512, (nch + 1) * 512)
            for dvc in range(2):
                sq = scr0[:, (dvc % 2) * 512:(dvc % 2) * 512 + 512]
                k.op("scalar", lambda e, h=h, dvc=dvc, tsl=tsl, sq=sq: e.activation(
                    sq, xs(oa, 2 * h + dvc, tsl), AF.Square),
                    r=("g_oa",), w=(f"scr0_{dvc % 2}",), tag="g_rsq")
                k.mm(bank[5][0:1, :], oc, sq, dvc == 0, dvc == 1,
                     r=(f"scr0_{dvc % 2}",), w=("bank5",), tag="g_rssq")
            k.op("scalar", lambda e, tsl=tsl: e.activation(
                row[:, tsl], bank[5][0:1, :], AF.Copy, scale=1.0 / 256),
                r=("bank5",), w=("row",), tag="g_rmsc")
        k.op("scalar", lambda e: e.activation(row, row, AF.Ln,
                                              bias=k.kcol(1e-5, 1)),
             r=("row",), w=("row",), tag="g_rln")
        k.op("scalar", lambda e: e.activation(row, row, AF.Exp, scale=-0.5),
             r=("row",), w=("row",), tag="g_rex")
        for nch in range(4):
            tsl = slice(nch * 512, (nch + 1) * 512)
            k.mm(bank[5], k.orow(0), row[:, tsl], True, True,
                 r=("row",), w=("bank5",), tag="g_rbm")
            k.op("vector", lambda e, tsl=tsl: e.tensor_copy(bc_mu[:, tsl],
                                                            bank[5]),
                 r=("bank5",), w=("bc_mu",), tag="g_rbe")
        for dvc in range(2):
            dv = 2 * h + dvc
            k.op("vector", lambda e, dv=dv: e.tensor_mul(
                scr0[:, 0:T], xs(oa, dv), bc_mu),
                r=("g_oa", "bc_mu"),
                w=("scr0_0", "scr0_1", "scr0_2", "scr0_3"), tag="g_on1")
            k.op("vector", lambda e, dvc=dvc: e.tensor_single_scalar(
                scr0[:, 0:T], scr0[:, 0:T], rmw[:, dvc:dvc + 1], ALU.mult),
                r=("scr0_0", "scr0_1", "scr0_2", "scr0_3", "g_rw"),
                w=("scr0_0", "scr0_1", "scr0_2", "scr0_3"), tag="g_on2")
            k.op("vector", lambda e, dv=dv, dvc=dvc: e.tensor_mul(
                xs(oa, dv), scr0[:, 0:T], gth[:, dvc * T:(dvc + 1) * T]),
                r=("scr0_0", "scr0_1", "scr0_2", "scr0_3", "g_gth"),
                w=("g_oa",), tag="g_on3")

    for fo in range(8):
        wbuf = wvb[fo % 2]
        wbr = f"g_wv{fo % 2}"
        k.dma("gpsimd", sb3(wbuf, 8),
              fm3(k.dram["l3_wout"][:, fo * 128:(fo + 1) * 128]),
              r=(wbr,), w=(wbr,), tag="g_wol")
        for tc2 in range(4):
            tsl = slice(tc2 * 512, (tc2 + 1) * 512)
            bk = bank[2 + tc2 % 2]
            bkr = f"bank{2 + tc2 % 2}"
            for d in range(NDT):
                k.mm(bk, wbuf[:, d * 128:(d + 1) * 128], xs(oa, d, tsl),
                     d == 0, d == NDT - 1, r=(wbr, "g_oa"), w=(bkr,),
                     tag="g_wom")
            k.op("vector", lambda e, bk=bk, fo=fo, tsl=tsl: e.tensor_add(
                xs(x, fo, tsl), xs(x, fo, tsl), bk),
                r=(f"bank{2 + tc2 % 2}", "x"), w=("x",), tag="g_woa")
    k.pop("g_done")



LAST_K = [None]


def build_program(layers, load_x=True, store_x=True):
    nc = bass.Bass("TRN2", target_bir_lowering=False, debug=False)
    sched = Sched()
    k = K(nc, sched)
    LAST_K[0] = k

    xin = k.din("xin", [D, T], F32)
    xout = k.dout("xout", [D, T], F32)
    k.din("c_ones_mat", [128, 128], F32)
    k.din("c_ident_b", [128, 128], BF16)
    k.din("c_ident_f", [128, 128], F32)
    k.din("c_causal01", [128, 128], BF16)
    k.din("c_pow2", [128, 4], F32)
    k.din("c_SLb", [128, 128], BF16)
    k.din("c_SUb", [128, 128], BF16)
    k.din("c_iota512", [128, 512], F32)
    k.din("c_iotacol", [128, 1], F32)
    k.din("c_parcol", [128, 1], BF16)

    if 0 in layers:
        k.din("l0_wqkv_t", [D, 3 * D], F32)
        k.din("l0_wout_t", [D, D], F32)
        k.din("l0_wfc_t", [D, 4096], F32)
        k.din("l0_wproj_t", [4096, D], F32)
        k.din("l0_lnw", [128, 32], F32)
        k.din("l0_lnb", [128, 32], F32)
        k.dtmp("wq_fc", [D, 4096], BF16)
        k.dtmp("wq_proj", [4096, D], BF16)
    if 1 in layers:
        k.din("l1_wqkv", [D, 3 * D], F32)
        k.din("l1_wout", [D, D], F32)
        k.din("l1_bqkv", [128, 24], F32)
        k.din("l1_bout", [128, 8], F32)
        k.din("l1_rw", [D, 8], F32)
        k.din("l1_rb", [128, 1], F32)
        k.din("l1_w1", [8, D, 8192], F32)
        k.din("l1_w2", [8, 4096, D], F32)
    if 2 in layers:
        k.din("l2_gate_wt", [D, D], F32)
        k.din("l2_gate_b", [128, 8], F32)
        k.din("l2_up_wt", [D, 4096], F32)
        k.din("l2_up_b", [128, 32], F32)
        k.din("l2_down_wt", [4096, D], F32)
        k.din("l2_down_b", [128, 8], F32)
        k.din("l2_sin_wt", [D, 4096], F32)
        k.din("l2_sout_wt", [4096, D], F32)
        k.din("l2_decay", [128, 1], F32)
        k.din("l2_conv", [128, 32], F32)
        k.din("l2_smallw", [D, 320], F32)
        k.din("l2_fcb", [128, 1], F32)
        k.din("l2_mow", [32, D], F32)
        k.din("l2_fw1t", [D, 4096], F32)
        k.din("l2_fw1n", [4096, D], F32)
        k.din("l2_fw2t", [4096, D], F32)
        k.din("l2_fw2n", [D, 4096], F32)
        k.din("l2_fb1", [128, 32], F32)
        k.din("l2_fb2", [128, 8], F32)
        k.dtmp("sgn_w1", [D, 4096], BF16)
        k.dtmp("sgn_w2", [4096, D], BF16)
    if 3 in layers:
        k.din("l3_wq", [D, 512], F32)
        k.din("l3_wk", [D, 512], F32)
        k.din("l3_wv", [D, D], F32)
        k.din("l3_wg", [D, D], F32)
        k.din("l3_wgk1", [D, 16], F32)
        k.din("l3_wgk2", [16, 512], F32)
        k.din("l3_bgk2", [128, 4], F32)
        k.din("l3_wout", [D, D], F32)
        k.din("l3_rmsw", [128, 2], F32)
        k.din("l3_rmt", [4, 16, 1024], F32)
        k.din("l3_rut", [4, 16, 4096], F32)
        k.din("l3_wct", [16384, D], F32)
        k.din("c_sel", [16, 1024], BF16)
        k.dtmp("wc_bf", [16384, D], BF16)

    k.sb("x", [128, NDT * T], F32)
    k.sb("xn", [128, NDT * T], BF16)
    k.arena = k.sb("arena", [128, ARENA_F32], F32)
    k.sb("row", [1, T], F32)
    k.sb("scr0", [128, T], F32)
    k.sb("bc_mu", [128, T], F32)
    k.sb("bc_rs", [128, T], F32)
    k.sb("ones_mat", [128, 128], F32)
    k.t["ones_col"] = k.t["ones_mat"][:, 0:1]
    k.sb("ones_col_b", [128, 1], BF16)
    k.sb("ident_b", [128, 128], BF16)
    k.sb("ident_f", [128, 128], F32)
    k.sb("causal01", [128, 128], BF16)
    k.sb("pow2", [128, 4], F32)
    k.sb("SLb", [128, 128], BF16)
    k.sb("SUb", [128, 128], BF16)
    k.sb("iota512", [128, 512], F32)
    k.sb("iotacol", [128, 1], F32)
    k.sb("parcol", [128, 1], BF16)
    k.sb("col0", [128, 1], F32)
    for i in range(3):
        k.sb(f"one{i}", [1, 1], F32)

    k.t["banks"] = [k.ps(f"bank{i}", [128, 512], F32) for i in range(7)]
    k.t["bank_b"] = k.ps("bank_b", [128, 1024], BF16)

    for nm, dst in (("c_ones_mat", "ones_mat"), ("c_ident_f", "ident_f"),
                    ("c_ident_b", "ident_b"), ("c_causal01", "causal01"),
                    ("c_pow2", "pow2"), ("c_SLb", "SLb"), ("c_SUb", "SUb"),
                    ("c_iota512", "iota512"), ("c_iotacol", "iotacol"),
                    ("c_parcol", "parcol")):
        k.dma("sync", k.t[dst], k.dram[nm], w=(dst,), tag=f"ld{nm}")
    k.op("vector", lambda e: e.tensor_copy(k.t["ones_col_b"], k.t["ones_col"]),
         r=("ones_mat",), w=("ones_col_b",))

    k.kcol(1e-5)
    k.kcol(1e-6)
    k.kcol(1.0)
    if load_x:
        k.dma("sync", sb3(k.t["x"], NDT), fm3(xin), w=("x",), tag="ld_x")
    k.s.barrier("init")
    k.init_done = True

    if 3 in layers:
        k.sb("c_sel_s", [16, 1024], BF16)
        k.dma("sync", k.t["c_sel_s"], k.dram["c_sel"], w=("c_sel",), tag="selld")
        k.t["c_sel"] = k.t["c_sel_s"]
        k.s.barrier("init3")
    emitters = {0: emit_layer0, 1: emit_layer1, 2: emit_layer2, 3: emit_layer3}
    for li in layers:
        emitters[li](k)

    if store_x:
        k.dma("sync", fm3(xout), sb3(k.t["x"], NDT),
              r=("x",), w=("xout",), tag="st_x")

    with nc.Block() as block:
        sched.finalize(nc, block)
    return nc



def _f32c(a):
    return np.ascontiguousarray(a, dtype=np.float32)


def make_consts():
    c = {}
    c["c_ones_mat"] = np.ones((128, 128), np.float32)
    c["c_ident_b"] = np.eye(128, dtype=ml_dtypes.bfloat16)
    c["c_ident_f"] = np.eye(128, dtype=np.float32)
    kk = np.arange(128)[:, None]
    qq = np.arange(128)[None, :]
    c["c_causal01"] = (kk <= qq).astype(ml_dtypes.bfloat16)
    p2 = np.zeros((128, 4), np.float32)
    for h in range(2):
        for i in range(9):
            p2[9 * h + i, h] = float(2 ** i)
            p2[32 + 9 * h + i, 2 + h] = float(2 ** i)
    c["c_pow2"] = p2
    c["c_SLb"] = (kk < qq).astype(ml_dtypes.bfloat16)
    c["c_SUb"] = (kk > qq).astype(ml_dtypes.bfloat16)
    c["c_iota512"] = np.broadcast_to(np.arange(512, dtype=np.float32),
                                     (128, 512)).copy()
    c["c_iotacol"] = np.arange(128, dtype=np.float32).reshape(128, 1)
    c["c_parcol"] = (np.arange(128) % 2).astype(ml_dtypes.bfloat16).reshape(128, 1)
    return c


def make_weight_arrays(inputs, layers):
    w = {}
    if 0 in layers:
        w["l0_wqkv_t"] = _f32c(inputs["L0_attn_Wqkv"].T)
        w["l0_wout_t"] = _f32c(inputs["L0_attn_Wout"].T)
        w["l0_wfc_t"] = _f32c(inputs["L0_ffn_W_fc"].T)
        w["l0_wproj_t"] = _f32c(inputs["L0_ffn_W_proj"].T)
        w["l0_lnw"] = _f32c(np.asarray(inputs["L0_ffn_ln_w"]).reshape(32, 128).T)
        w["l0_lnb"] = _f32c(np.asarray(inputs["L0_ffn_ln_b"]).reshape(32, 128).T)
    if 1 in layers:
        w["l1_wqkv"] = _f32c(inputs["L1_attn_w_qkv"])
        w["l1_wout"] = _f32c(inputs["L1_attn_w_out"])
        w["l1_bqkv"] = _f32c(np.asarray(inputs["L1_attn_b_qkv"]).reshape(3, 8, 128)
                             .transpose(2, 0, 1).reshape(128, 24))
        w["l1_bout"] = _f32c(np.asarray(inputs["L1_attn_b_out"]).reshape(8, 128).T)
        w["l1_rw"] = _f32c(inputs["L1_ffn_router_w"])
        rb = np.zeros((128, 1), np.float32)
        rb[0:8, 0] = np.asarray(inputs["L1_ffn_router_b"], np.float32)
        w["l1_rb"] = rb
        w["l1_w1"] = _f32c(inputs["L1_ffn_w1"])
        w["l1_w2"] = _f32c(inputs["L1_ffn_w2"])
    if 2 in layers:
        w["l2_gate_wt"] = _f32c(np.asarray(inputs["L2_attn_gate_w"]).T)
        w["l2_gate_b"] = _f32c(np.asarray(inputs["L2_attn_gate_b"]).reshape(8, 128).T)
        w["l2_up_wt"] = _f32c(np.asarray(inputs["L2_attn_mlp_up_w"]).T)
        w["l2_up_b"] = _f32c(np.asarray(inputs["L2_attn_mlp_up_b"]).reshape(32, 128).T)
        w["l2_down_wt"] = _f32c(np.asarray(inputs["L2_attn_mlp_down_w"]).T)
        w["l2_down_b"] = _f32c(np.asarray(inputs["L2_attn_mlp_down_b"]).reshape(8, 128).T)
        w["l2_sin_wt"] = _f32c(np.asarray(inputs["L2_attn_state_in_w"]).T)
        w["l2_sout_wt"] = _f32c(np.asarray(inputs["L2_attn_state_out_w"]).T)
        dc = np.zeros((128, 1), np.float32)
        dc[0:4, 0] = np.asarray(inputs["L2_attn_state_decay_logit"], np.float32)
        w["l2_decay"] = dc
        w["l2_conv"] = _f32c(np.asarray(inputs["L2_attn_conv_w"])
                             .reshape(8, 128, 4).transpose(1, 0, 2).reshape(128, 32))
        sw = np.zeros((1024, 320), np.float32)
        sw[:, 0:32] = np.asarray(inputs["L2_attn_qkey_w"]).T
        sw[:, 32:64] = np.asarray(inputs["L2_attn_wkey_w"]).T
        sw[:, 64:96] = np.asarray(inputs["L2_attn_value_w"]).T
        sw[:, 128:146] = np.asarray(inputs["L2_attn_read_bits_w"]).T
        sw[:, 160:178] = np.asarray(inputs["L2_attn_write_bits_w"]).T
        sw[:, 192:193] = np.asarray(inputs["L2_attn_write_gate_w"]).T
        sw[:, 224:225] = np.asarray(inputs["L2_attn_gate_long_w"]).T
        sw[:, 256:257] = np.asarray(inputs["L2_attn_gate_mem_w"]).T
        w["l2_smallw"] = sw
        fcb = np.zeros((128, 1), np.float32)
        fcb[0, 0] = np.asarray(inputs["L2_attn_write_gate_b"]).reshape(-1)[0]
        fcb[32, 0] = np.asarray(inputs["L2_attn_gate_long_b"]).reshape(-1)[0]
        fcb[64, 0] = np.asarray(inputs["L2_attn_gate_mem_b"]).reshape(-1)[0]
        w["l2_fcb"] = fcb
        w["l2_mow"] = _f32c(np.asarray(inputs["L2_attn_mem_out_w"]).T)
        w["l2_fw1t"] = _f32c(np.asarray(inputs["L2_ffn_w1"]).T)
        w["l2_fw1n"] = _f32c(inputs["L2_ffn_w1"])
        w["l2_fw2t"] = _f32c(np.asarray(inputs["L2_ffn_w2"]).T)
        w["l2_fw2n"] = _f32c(inputs["L2_ffn_w2"])
        w["l2_fb1"] = _f32c(np.asarray(inputs["L2_ffn_b1"]).reshape(32, 128).T)
        w["l2_fb2"] = _f32c(np.asarray(inputs["L2_ffn_b2"]).reshape(8, 128).T)
    if 3 in layers:
        w["l3_wq"] = _f32c(inputs["L3_attn_Wq"])
        w["l3_wk"] = _f32c(inputs["L3_attn_Wk"])
        w["l3_wv"] = _f32c(inputs["L3_attn_Wv"])
        w["l3_wg"] = _f32c(inputs["L3_attn_Wg"])
        w["l3_wgk1"] = _f32c(inputs["L3_attn_Wgk1"])
        w["l3_wgk2"] = _f32c(inputs["L3_attn_Wgk2"])
        w["l3_bgk2"] = _f32c(np.asarray(inputs["L3_attn_bgk2"]).reshape(4, 128).T)
        w["l3_wout"] = _f32c(inputs["L3_attn_Wout"])
        w["l3_rmsw"] = _f32c(np.asarray(inputs["L3_attn_rms_w"]).reshape(2, 128).T)
        w["l3_rmt"] = _f32c(np.transpose(np.asarray(inputs["L3_ffn_rail_maker"]),
                                         (0, 3, 1, 2)).reshape(4, 16, 1024))
        w["l3_rut"] = _f32c(np.transpose(np.asarray(inputs["L3_ffn_rail_up_proj"]),
                                         (0, 3, 1, 2)).reshape(4, 16, 4096))
        w["l3_wct"] = _f32c(np.asarray(inputs["L3_ffn_w_cproj"]).T)
        sel = np.zeros((16, 1024), ml_dtypes.bfloat16)
        for r8 in range(8):
            for i in range(16):
                sel[i, r8 * 128 + r8 * 16 + i] = 1
        w["c_sel"] = sel
    return w


def run_layers(inputs, x_in, layers):
    nc = build_program(layers)
    consts = make_consts()
    warr = make_weight_arrays(inputs, layers)
    in_maps = []
    for b in range(NCORES):
        m = {"xin": _f32c(x_in[b])}
        m.update(consts)
        m.update(warr)
        in_maps.append(m)
    res = run_bass_kernel_spmd(nc, in_maps, core_ids=list(range(NCORES)))
    return np.stack([res.results[b]["xout"] for b in range(NCORES)], axis=0)


def kernel(**inputs):
    x = np.asarray(inputs["L0_attn_x"])
    x_fm = np.ascontiguousarray(np.transpose(x, (0, 2, 1)))
    out_fm = run_layers(inputs, x_fm, [0, 1, 2, 3])
    return np.ascontiguousarray(np.transpose(out_fm, (0, 2, 1)).astype(np.float32))
